# version 09
# baseline (speedup 1.0000x reference)
import math
import numpy as np
import ml_dtypes
import concourse.bass as bass
import concourse.mybir as mybir
from concourse.bass_utils import run_bass_kernel_spmd
from contextlib import ExitStack

F32 = mybir.dt.float32
BF16 = mybir.dt.bfloat16
I32 = mybir.dt.int32
AF = mybir.ActivationFunctionType
ALU = mybir.AluOpType
AX = mybir.AxisListType

NCORES = 8
S = 8192
T = S // NCORES
D = 2048
NBLK = 8
KCH = 16
H = 16
TQ = 256
HW = 512
SCALER = 1.0 / math.sqrt(192.0)
C_EXP = 126 << 23
EPS = 1e-5

ENGINES = ("sync", "gpsimd", "vector", "scalar", "tensor")


class Sched:

    def __init__(self, ctr, sems):
        self.q = {e: [] for e in ENGINES}
        self.ctr = ctr
        self.sems = sems

    def wait(self, eng, sname, val):
        if val > 0:
            self.q[eng].append(lambda e, s=self.sems[sname], v=val: e.wait_ge(s, v))

    def op(self, eng, fn, inc=None, n=1):
        if inc is None:
            self.q[eng].append(fn)
            return self.ctr.get("_", 0)
        self.ctr[inc] = self.ctr.get(inc, 0) + n
        cnt = self.ctr[inc]
        sem = self.sems[inc]

        def run(e, fn=fn, sem=sem, n=n):
            fn(e).then_inc(sem, n)

        self.q[eng].append(run)
        return cnt

    def val(self, name):
        return self.ctr.get(name, 0)

    def flush(self, blk):
        for ename in ENGINES:
            ops = self.q[ename]
            if not ops:
                continue

            def body(e, ops=ops):
                for f in ops:
                    f(e)

            getattr(blk, ename)(body)
        self.q = {e: [] for e in ENGINES}


def build_program():
    nc = bass.Bass("TRN2", target_bir_lowering=False)

    xT_d = nc.dram_tensor("xT", [D, T], F32, kind="ExternalInput")
    out_d = nc.dram_tensor("out", [T, D], F32, kind="ExternalOutput")
    aw = {}
    for name, shp in (("Wc", [D, 512]), ("Wcq", [512, D]), ("Wqr", [512, 1024]),
                      ("Wckv", [D, 512]), ("Wck", [512, D]), ("Wkr", [D, 1024]),
                      ("Wv", [512, D]), ("Wo", [D, D])):
        aw[name] = nc.dram_tensor(name, shp, F32, kind="ExternalInput")
    bias_d = nc.dram_tensor("biases", [128, 88], F32, kind="ExternalInput")
    rope_d = nc.dram_tensor("rope_cs", [128, 2 * T], BF16, kind="ExternalInput")
    rotm_d = nc.dram_tensor("rotm", [128, 128], F32, kind="ExternalInput")
    eyeb_d = nc.dram_tensor("eye_bf", [128, 128], BF16, kind="ExternalInput")
    eyef_d = nc.dram_tensor("eye_f32", [128, 128], F32, kind="ExternalInput")
    pk_d = {m: nc.dram_tensor(f"{m}_pk", [2048, 1024], I32, kind="ExternalInput")
            for m in ("g", "u", "d")}
    sc_d = {m: nc.dram_tensor(f"{m}_sc", [2048, 64], F32, kind="ExternalInput")
            for m in ("g", "u", "d")}
    w_in = {m: nc.dram_tensor(f"{m}_in", [16, 128, 16, 128], BF16) for m in ("g", "u", "d")}
    aw_bf = {n: nc.dram_tensor(n + "_bf", list(aw[n].shape), BF16) for n in aw}
    w_out = {m: nc.dram_tensor(f"{m}_out", [4, NBLK, 4, 128, 16, 128], BF16) for m in ("g", "u", "d")}

    ctr = {}

    with ExitStack() as top:
        def sb(name, shape, dtype):
            return top.enter_context(nc.sbuf_tensor(name, shape, dtype))

        xt = sb("xt", [128, KCH, T], F32)
        y = sb("y", [128, KCH, T], BF16)
        shI = sb("shI", [128, KCH, HW], BF16)
        biases = sb("biases_sb", [128, 88], F32)
        rotm = sb("rotm_sb", [128, 128], F32)
        eyeb = sb("eyeb_sb", [128, 128], BF16)
        ones_c = sb("ones_c", [128, 1], F32)
        ones_r = sb("ones_r", [1, 128], F32)
        rows = sb("rows", [1, 3, 512], F32)
        nm16 = sb("nm16", [128, 16], F32)
        rs16 = sb("rs16", [128, 16], F32)
        rs16b = sb("rs16b", [128, 16], F32)
        wr = {m: sb(f"ring_{m}", [128, 2, 16, 128], BF16) for m in ("g", "u", "d")}
        g1 = sb("g1", [128, 1, HW], F32)
        sq_sb = sb("sq_sb", [128, 1, HW], F32)
        lntmp = sb("lntmp", [128, HW], F32)

        P = [top.enter_context(nc.psum_tensor(f"P{i}", [128, 512], F32)) for i in range(6)]
        PT = [top.enter_context(nc.psum_tensor(f"PT{i}", [128, 1024], BF16)) for i in range(2)]
        PTf = [t.bitcast(F32) for t in PT]

        sems = {k: top.enter_context(nc.semaphore("s_" + k))
                for k in ("ld", "gd", "cc", "pe", "ve", "ac", "pre",
                          "pk0", "pk1", "st0", "st1",
                          "wg0", "wg1", "wu0", "wu1", "wd0", "wd1",
                          "wa0", "wa1", "os0", "os1")}

        bias_cols = {}
        off = 0
        for bn, w in (("bc", 4), ("bcq", 16), ("bqr", 8), ("bckv", 4),
                      ("bck", 16), ("bkr", 8), ("bv", 16), ("bo", 16)):
            bias_cols[bn] = off
            off += w

        def bias_ap(bn, chunk):
            c = bias_cols[bn] + chunk
            return biases[:, c:c + 1]

        sch = Sched(ctr, sems)

        pre_total = [0]
        cc_at = {("g", 0): 1, ("u", 0): 2}
        _n = 2
        for _q in range(1, 4):
            _n += 1; cc_at[("g", _q)] = _n
            _n += 1; cc_at[("u", _q)] = _n
        for _q in range(4):
            _n += 1; cc_at[("d", _q)] = _n

        ln_chain = [0, 0]

        def emit_ln(t0, tw, ydst_off, banks, stats_guards=(), bc_guards=(),
                    norm_guards=()):
            st0, st1, bc0, bc1 = banks
            q_pe = {}
            sch.wait("tensor", "pre", pre_total[0])
            sch.wait("scalar", "pre", pre_total[0])
            sch.wait("vector", "pre", pre_total[0])
            for (eng, sname, cnt) in stats_guards:
                sch.wait(eng, sname, cnt)
            if ln_chain[1]:
                sch.wait("scalar", "pe", ln_chain[1])
            for ch in range(KCH):
                if ch >= 1:
                    sch.wait("scalar", "pe", q_pe[ch - 1])
                sqv = sq_sb[:, 0, 0:tw]
                c_ac = sch.op("scalar",
                              lambda e, ch=ch, sqv=sqv: e.activation(
                                  sqv, xt[:, ch, t0:t0 + tw], AF.Square),
                              inc="ac")
                sch.wait("tensor", "ac", c_ac)
                sch.op("tensor",
                       lambda e, ch=ch, st0=st0: e.matmul(st0, ones_c[:], xt[:, ch, t0:t0 + tw],
                                                          start=(ch == 0), stop=(ch == KCH - 1)))
                q_pe[ch] = sch.op("tensor",
                                  lambda e, sqv=sqv, st1=st1: e.matmul(
                                      st1, ones_c[:], sqv,
                                      start=(ch == 0), stop=(ch == KCH - 1)),
                                  inc="pe")
            stats_done = q_pe[KCH - 1]
            sch.wait("vector", "pe", stats_done)
            sch.op("vector", lambda e, st0=st0: e.tensor_scalar(rows[:, 0, 0:tw], st0, 1.0 / D, None, ALU.mult))
            sch.op("vector", lambda e, st1=st1: e.tensor_scalar(rows[:, 1, 0:tw], st1, 1.0 / D, None, ALU.mult))
            sch.op("vector", lambda e: e.tensor_tensor(rows[:, 2, 0:tw], rows[:, 0, 0:tw], rows[:, 0, 0:tw], ALU.mult))
            sch.op("vector", lambda e: e.tensor_tensor(rows[:, 2, 0:tw], rows[:, 1, 0:tw], rows[:, 2, 0:tw], ALU.subtract))
            c_ve = sch.op("vector", lambda e: e.tensor_scalar(rows[:, 2, 0:tw], rows[:, 2, 0:tw], EPS, None, ALU.add),
                          inc="ve")
            sch.wait("scalar", "ve", c_ve)
            c_ac = sch.op("scalar", lambda e: e.activation(rows[:, 1, 0:tw], rows[:, 2, 0:tw], AF.Sqrt),
                          inc="ac")
            sch.wait("vector", "ac", c_ac)
            c_ve = sch.op("vector", lambda e: e.reciprocal(rows[:, 2, 0:tw], rows[:, 1, 0:tw]), inc="ve")
            sch.wait("tensor", "ve", c_ve)
            for (eng, sname, cnt) in bc_guards:
                sch.wait(eng, sname, cnt)
            sch.op("tensor", lambda e, bc0=bc0: e.matmul(bc0, ones_r[:], rows[:, 0, 0:tw], start=True, stop=True))
            bc_done = sch.op("tensor", lambda e, bc1=bc1: e.matmul(bc1, ones_r[:], rows[:, 2, 0:tw], start=True, stop=True),
                             inc="pe")
            sch.wait("vector", "pe", bc_done)
            for (eng, sname, cnt) in norm_guards:
                sch.wait(eng, sname, cnt)
            c_n = 0
            for ch in range(KCH):
                sch.op("vector", lambda e, ch=ch, bc0=bc0: e.tensor_tensor(
                    lntmp[:, 0:tw], xt[:, ch, t0:t0 + tw], bc0, ALU.subtract))
                c_n = sch.op("vector", lambda e, ch=ch, bc1=bc1: e.tensor_tensor(
                    y[:, ch, ydst_off:ydst_off + tw], lntmp[:, 0:tw], bc1, ALU.mult),
                    inc="ve" if ch == KCH - 1 else None)
            ln_chain[0] = c_n
            ln_chain[1] = max(stats_done, bc_done)
            return c_n

        app_marks = []

        def emit_ffn_app(t0, first=False, dq_hook=None, att_hook=None,
                         ln_guards=None, prev=None):
            lg = ln_guards or {}
            ln_done = emit_ln(
                t0, HW, t0,
                banks=(P[2][0:1, :], P[3][0:1, :], P[0][:, :], P[1][:, :]),
                stats_guards=lg.get("stats", ()),
                bc_guards=lg.get("bc", ()),
                norm_guards=lg.get("norm", ()))

            wload = {}
            pe_grp = {}
            gelu_at = {}
            inter_at = {}
            evac_at = {}
            down_pe_done = {}

            for nb in range(NBLK):
                for mc in range(16):
                    gidx = nb * 16 + mc
                    if dq_hook is not None:
                        dq_hook(gidx)
                    if att_hook is not None:
                        att_hook()
                    slot = gidx % 2
                    if first and nb == 0:
                        sch.wait("sync", "cc", cc_at[("g", mc // 4)])
                        sch.wait("scalar", "cc", cc_at[("u", mc // 4)])
                    for m, qm in (("g", "sync"), ("u", "scalar")):
                        if gidx >= 2:
                            sch.wait(qm, "pe", pe_grp[gidx - 2])
                        elif prev is not None:
                            sch.wait(qm, "pe", prev["gu_pe"])
                        wload[(m, gidx)] = sch.op(qm, lambda e, m=m, nb=nb, mc=mc, slot=slot: e.dma_start(
                            out=wr[m][:, slot], in_=w_out[m][mc // 4, nb, mc % 4]), inc="w%s%d" % (m, slot), n=16)
                    sch.wait("tensor", "wg%d" % slot, wload[("g", gidx)])
                    sch.wait("tensor", "wu%d" % slot, wload[("u", gidx)])
                    if gidx == 0:
                        sch.wait("tensor", "ve", ln_done)
                    if gidx >= 2:
                        sch.wait("tensor", "ac", gelu_at[gidx - 2])
                        sch.wait("tensor", "ve", inter_at[gidx - 2])
                    for k in range(KCH):
                        sch.op("tensor", lambda e, slot=slot, k=k: e.matmul(
                            P[slot][:, :], wr["g"][:, slot, k, :], y[:, k, t0:t0 + HW],
                            start=(k == 0), stop=(k == KCH - 1)))
                    grp_pe = 0
                    for k in range(KCH):
                        grp_pe = sch.op("tensor", lambda e, slot=slot, k=k: e.matmul(
                            P[2 + slot][:, :], wr["u"][:, slot, k, :], y[:, k, t0:t0 + HW],
                            start=(k == 0), stop=(k == KCH - 1)),
                            inc="pe" if k == KCH - 1 else None)
                    pe_grp[gidx] = grp_pe
                    sch.wait("scalar", "pe", grp_pe)
                    if gidx >= 1:
                        sch.wait("scalar", "ve", inter_at[gidx - 1])
                    elif prev is not None:
                        sch.wait("scalar", "ve", prev["inter_ve"])
                    gelu_at[gidx] = sch.op("scalar", lambda e, slot=slot: e.activation(
                        g1[:, 0, :], P[slot][:, :], AF.Gelu_apprx_tanh), inc="ac")
                    sch.wait("vector", "ac", gelu_at[gidx])
                    if nb >= 1 and mc == 0:
                        sch.wait("vector", "pe", down_pe_done[nb - 1])
                    elif nb == 0 and mc == 0 and prev is not None:
                        sch.wait("vector", "pe", prev["down_pe"])
                    inter_at[gidx] = sch.op("vector", lambda e, mc=mc, slot=slot: e.tensor_tensor(
                        shI[:, mc, :], g1[:, 0, :], P[2 + slot][:, :], ALU.mult),
                        inc="ve")
                for hc in range(16):
                    hgidx = nb * 16 + hc
                    slot = hgidx % 2
                    if att_hook is not None:
                        att_hook()
                    if first and nb == 0:
                        sch.wait("sync", "cc", cc_at[("d", hc // 4)])
                    if hgidx >= 2:
                        sch.wait("sync", "pe", pe_grp[("d", hgidx - 2)])
                    elif prev is not None:
                        sch.wait("sync", "pe", prev["down_pe"])
                    wload[("d", hgidx)] = sch.op("sync", lambda e, nb=nb, hc=hc, slot=slot: e.dma_start(
                        out=wr["d"][:, slot], in_=w_out["d"][hc // 4, nb, hc % 4]), inc="wd%d" % slot, n=16)
                    bank = 4 + (hc % 2)
                    sch.wait("tensor", "wd%d" % slot, wload[("d", hgidx)])
                    if hc == 0:
                        sch.wait("tensor", "ve", inter_at[nb * 16 + 15])
                    if hgidx >= 2:
                        sch.wait("tensor", "ve", evac_at[hgidx - 2])
                    grp_pe = 0
                    for k in range(KCH):
                        grp_pe = sch.op("tensor", lambda e, slot=slot, k=k, bank=bank: e.matmul(
                            P[bank][:, :], wr["d"][:, slot, k, :], shI[:, k, :],
                            start=(k == 0), stop=(k == KCH - 1)),
                            inc="pe" if k == KCH - 1 else None)
                    pe_grp[("d", hgidx)] = grp_pe
                    sch.wait("vector", "pe", grp_pe)
                    evac_at[hgidx] = sch.op("vector", lambda e, hc=hc, bank=bank, t0=t0: e.scalar_tensor_tensor(
                        xt[:, hc, t0:t0 + HW], P[bank][:, :], 0.5,
                        xt[:, hc, t0:t0 + HW], ALU.mult, ALU.add),
                        inc="ve")
                down_pe_done[nb] = pe_grp[("d", nb * 16 + 15)]

            app_marks.append(dict(
                gu_pe=pe_grp[NBLK * 16 - 1],
                down_ve=evac_at[NBLK * 16 - 1],
                gelu_ac=gelu_at[NBLK * 16 - 1],
                inter_ve=inter_at[NBLK * 16 - 1],
                down_pe=down_pe_done[NBLK - 1],
            ))

        sch.op("sync", lambda e: e.dma_start(out=xt[:], in_=xT_d.rearrange("(c p) t -> p c t", p=128)),
               inc="pre", n=16)
        sch.op("sync", lambda e: e.dma_start(out=biases[:], in_=bias_d[:]), inc="pre", n=16)
        sch.op("sync", lambda e: e.dma_start(out=rotm[:], in_=rotm_d[:]), inc="pre", n=16)
        sch.op("sync", lambda e: e.dma_start(out=eyeb[:], in_=eyeb_d[:]), inc="pre", n=16)
        PRE_ALL = sch.val("pre")
        pre_total[0] = PRE_ALL
        sch.op("vector", lambda e: e.memset(ones_c[:], 1.0))
        sch.op("vector", lambda e: e.memset(ones_r[:], 1.0), inc="ve")

        with ExitStack() as c0:
            pk_t = c0.enter_context(nc.sbuf_tensor("pk", [128, 2, 1024], I32))
            sc_t = c0.enter_context(nc.sbuf_tensor("sct", [128, 2, 64], F32))
            v0l = c0.enter_context(nc.sbuf_tensor("v0l", [128, 1024], I32))
            sl_t = c0.enter_context(nc.sbuf_tensor("sl", [128, 1024], I32))
            v8_t = c0.enter_context(nc.sbuf_tensor("v8", [128, 1024], F32))
            nat = c0.enter_context(nc.sbuf_tensor("nat", [128, 2, 2048], BF16))
            stage = c0.enter_context(nc.sbuf_tensor("stage", [128, 2, 16, 128], BF16))

            dq_done = {}
            tp_done = {}
            pt_evac = {}
            stage_dma = {}
            dq_hist = []

            def emit_dq_chunk(m, oc, gl):
                dq_hist.append(gl)
                slot = (len(dq_hist) - 1) % 2
                qeng = "gpsimd"
                if len(dq_hist) >= 3:
                    sch.wait(qeng, "ve", dq_done[dq_hist[-3]])
                pksem = "pk%d" % slot
                sch.op(qeng, lambda e, m=m, oc=oc, slot=slot: e.dma_start(
                    out=pk_t[:, slot, :],
                    in_=pk_d[m].rearrange("(c p) j -> c p j", p=128)[oc]), inc=pksem, n=16)
                c_ld = sch.op(qeng, lambda e, m=m, oc=oc, slot=slot: e.dma_start(
                    out=sc_t[:, slot, :],
                    in_=sc_d[m].rearrange("(c p) j -> c p j", p=128)[oc]), inc=pksem, n=16)
                sch.wait("vector", pksem, c_ld)
                if len(dq_hist) >= 3:
                    sch.wait("vector", "pe", tp_done[dq_hist[-3]])
                pkap = pk_t[:, slot, :]
                sc_b = sc_t.rearrange("p s (b one) -> p s b one", one=1)[:, slot].broadcast_to([128, 64, 16])
                c_dq = 0
                for par in (0, 1):
                    if par == 0:
                        sch.op("vector", lambda e, pkap=pkap: e.tensor_scalar(v0l[:], pkap, 7, 22, ALU.bitwise_and, ALU.logical_shift_left))
                        sch.op("vector", lambda e, pkap=pkap: e.tensor_scalar(sl_t[:], pkap, 8, 28, ALU.bitwise_and, ALU.logical_shift_left))
                    else:
                        sch.op("vector", lambda e, pkap=pkap: e.tensor_scalar(v0l[:], pkap, 4, None, ALU.logical_shift_right))
                        sch.op("vector", lambda e: e.tensor_scalar(v0l[:], v0l[:], 7, 22, ALU.bitwise_and, ALU.logical_shift_left))
                        sch.op("vector", lambda e, pkap=pkap: e.tensor_scalar(sl_t[:], pkap, 128, 24, ALU.bitwise_and, ALU.logical_shift_left))
                    sch.op("vector", lambda e: e.tensor_scalar(v0l[:], v0l[:], C_EXP, None, ALU.add))
                    v0f = v0l.bitcast(F32)
                    sch.op("vector", lambda e, v0f=v0f: e.tensor_scalar(v8_t[:], v0f[:], 1.0, 0.0, ALU.subtract, ALU.min))
                    sch.op("vector", lambda e, v0f=v0f: e.tensor_add(v8_t[:], v0f[:], v8_t[:]))
                    sch.op("vector", lambda e: e.tensor_tensor(v8_t.bitcast(I32)[:], v8_t.bitcast(I32)[:], sl_t[:], ALU.bitwise_xor))
                    natv = nat.rearrange("p s (b g two) -> p s b g two", two=2, g=16)[:, slot, :, :, par]
                    c_dq = sch.op("vector", lambda e, natv=natv, sc_b=sc_b: e.tensor_tensor(
                        natv, v8_t.rearrange("p (b g) -> p b g", g=16), sc_b, ALU.mult),
                        inc="ve" if par == 1 else None)
                dq_done[gl] = c_dq
                sch.wait("tensor", "ve", c_dq)
                if len(dq_hist) == 1:
                    sch.wait("tensor", "pre", PRE_ALL)
                for grp in range(2):
                    bank = grp
                    if len(dq_hist) >= 2:
                        sch.wait("tensor", "ac", pt_evac[(dq_hist[-2], grp)])
                    c_tp = 0
                    for kk in range(8):
                        k = grp * 8 + kk
                        c_tp = sch.op("tensor", lambda e, slot=slot, k=k, kk=kk, bank=bank: e.transpose(
                            PT[bank][:, kk * 128:(kk + 1) * 128],
                            nat[:, slot, k * 128:(k + 1) * 128], eyeb[:]),
                            inc="pe" if kk == 7 else None)
                    sch.wait("scalar", "pe", c_tp)
                    if len(dq_hist) >= 3:
                        sch.wait("scalar", "st%d" % slot, stage_dma[dq_hist[-3]])
                    pt_evac[(gl, grp)] = sch.op("scalar", lambda e, slot=slot, grp=grp, bank=bank: e.copy(
                        stage[:, slot, grp * 8:(grp + 1) * 8, :],
                        PT[bank].rearrange("p (g q) -> p g q", q=128)),
                        inc="ac")
                    if grp == 1:
                        tp_done[gl] = c_tp
                sch.wait("scalar", "ac", pt_evac[(gl, 1)])
                stage_dma[gl] = sch.op("scalar", lambda e, m=m, oc=oc, slot=slot: e.dma_start(
                    out=w_in[m][oc], in_=stage[:, slot]), inc="st%d" % slot, n=16)

            def emit_cc(m, q):
                sch.wait("gpsimd", "st0", sch.val("st0"))
                sch.wait("gpsimd", "st1", sch.val("st1"))
                rws = 4 * 128 * 16
                c = sch.op("gpsimd", lambda e, m=m, q=q, rws=rws: e.collective_compute(
                    "AllGather", ALU.bypass,
                    replica_groups=[list(range(NCORES))],
                    ins=[w_in[m].rearrange("a p k q -> (a p k) q")[q * rws:(q + 1) * rws, :]],
                    outs=[w_out[m][q].rearrange("n a p k q -> (n a p k) q")]),
                    inc="cc")
                assert c == cc_at[(m, q)], (m, q, c, cc_at[(m, q)])

            for m in ("g", "u"):
                for oc in range(4):
                    emit_dq_chunk(m, oc, (0 if m == "g" else 16) + oc)
                emit_cc(m, 0)

            rest = []
            for q in range(1, 4):
                rest += [("g", oc) for oc in range(4 * q, 4 * q + 4)]
                rest += [("u", oc) for oc in range(4 * q, 4 * q + 4)]
            rest += [("d", oc) for oc in range(16)]

            CPH = 3

            def dq_hook(gidx):
                for j in range(CPH * gidx, min(CPH * (gidx + 1), len(rest))):
                    m, oc = rest[j]
                    base = {"g": 0, "u": 16, "d": 32}[m]
                    emit_dq_chunk(m, oc, base + oc)
                    if oc % 4 == 3:
                        emit_cc(m, oc // 4)

            emit_ffn_app(0, first=True, dq_hook=dq_hook)
            DQ_PT_AC = ctr.get("ac", 0)
            DQ_ST0 = sch.val("st0")
            DQ_ST1 = sch.val("st1")
        for n in aw:
            sch.op("gpsimd", lambda e, n=n: e.dma_start(out=aw_bf[n][:], in_=aw[n][:]), inc="gd", n=16)
        AWBF_ALL = sch.val("gd")

        with ExitStack() as c2:
            attscr = c2.enter_context(nc.sbuf_tensor("attscr", [128, 6656], F32))
            cqb = c2.enter_context(nc.sbuf_tensor("cqb", [128, 4, TQ], BF16))
            ckvb = c2.enter_context(nc.sbuf_tensor("ckvb", [128, 4, TQ], BF16))
            qcb = c2.enter_context(nc.sbuf_tensor("qcb", [128, 16, TQ], BF16))
            kcb = c2.enter_context(nc.sbuf_tensor("kcb", [128, 16, TQ], BF16))
            qrb = c2.enter_context(nc.sbuf_tensor("qrb", [128, 8, TQ], BF16))
            krb = c2.enter_context(nc.sbuf_tensor("krb", [128, 8, TQ], BF16))
            wra = c2.enter_context(nc.sbuf_tensor("wra", [128, 1, 16, 128], BF16))

            attF = attscr
            attB = attscr.bitcast(BF16)
            prod = attB[:, 0:3072].rearrange("p (j d) -> p j d", d=192)
            prod2 = attB[:, 0:2048].rearrange("p (d j) -> p d j", j=16)
            q_t = attB[:, 3072:6144]
            k_t = attB[:, 6144:9216]
            v_t = attB[:, 9216:11264]
            o_t = attB[:, 11264:13312]
            es256 = attF[:, 5632:5888]
            rope = attB[:, 12032:12544]
            sc256 = attF[:, 6272:6528]
            at256 = attB[:, 13056:13312]
            rtm1 = lntmp[:, 0:TQ]
            rtm2 = lntmp[:, TQ:2 * TQ]
            q192 = q_t.rearrange("p (i d) -> p i d", d=192)
            k192 = k_t.rearrange("p (i d) -> p i d", d=192)

            def PB(h):
                return PTf[1][:, h * TQ:(h + 1) * TQ]

            ptst = {(b, h): [("ac", DQ_PT_AC)] for b in (0, 1) for h in (0, 1)}

            def pt_guard(eng, bank, halves):
                seen = set()
                for h in halves:
                    for mk in ptst[(bank, h)]:
                        if mk not in seen:
                            seen.add(mk)
                            sch.wait(eng, mk[0], mk[1])

            def pt_setcons(bank, halves, marks):
                for h in halves:
                    ptst[(bank, h)] = list(marks)

            proj_specs = [
                ("cq", "Wc", "bc", "y", 16, 4),
                ("ckv", "Wckv", "bckv", "y", 16, 4),
                ("qc", "Wcq", "bcq", "cq", 4, 16),
                ("qr", "Wqr", "bqr", "cq", 4, 8),
                ("kc", "Wck", "bck", "ckv", 4, 16),
                ("kr", "Wkr", "bkr", "y", 16, 8),
                ("v", "Wv", "bv", "ckv", 4, 16),
            ]

            wa_hist = []
            ast = dict(
                rope_ve=app_marks[0]["down_ve"], ot_pe=app_marks[0]["down_pe"],
                buf_pe={},
                yslot={},
                av_ve=0,
                pb_free={0: [], 1: []},
                wa_done=False,
                last=dict(ve=0, ac=0, pe=0),
                s3_ve={},
            )

            def wa_load(fn):
                slot = 0
                if not ast["wa_done"]:
                    sch.wait("gpsimd", "gd", AWBF_ALL)
                    ast["wa_done"] = True
                if len(wa_hist) >= 1:
                    sch.wait("gpsimd", "pe", wa_hist[-1][1])
                c_w = sch.op("gpsimd", lambda e, fn=fn, slot=slot: fn(e, slot), inc="wa0", n=16)
                ent = [c_w, None]
                wa_hist.append(ent)
                return slot, c_w, ent

            def att_half(half, xt_ready_ve, y_free_pe):
                abase = half * 512
                ao_ac = 0
                if half == 0:
                    sch.wait("vector", "st0", DQ_ST0)
                    sch.wait("vector", "st1", DQ_ST1)
                for qq in (0, 1):
                    q = 2 * half + qq
                    t0q = q * TQ
                    ys = qq
                    lnoff = abase + ys * TQ
                    voff = abase + (1 - ys) * TQ
                    sch.wait("gpsimd", "ve", ast["rope_ve"])
                    sch.wait("gpsimd", "pe", ast["ot_pe"])
                    c_rl = sch.op("gpsimd", lambda e, t0q=t0q: e.dma_start(
                        out=rope.rearrange("p (c t) -> p c t", c=2),
                        in_=rope_d.rearrange("p (c t) -> p c t", c=2)[:, :, t0q:t0q + TQ]), inc="pre", n=16)
                    yield
                    sg = [("scalar", "ve", xt_ready_ve), ("tensor", "ve", xt_ready_ve)]
                    for (s, c) in ptst[(0, 0)] + ptst[(0, 1)]:
                        sg.append(("tensor", s, c))
                    bg = [("tensor", s, c) for (s, c) in
                          ast["pb_free"][0] + ast["pb_free"][1]]
                    ng = [("vector",) + mk for mk in
                          ast["yslot"].get((half, ys), [("pe", y_free_pe)])]
                    ln_done = emit_ln(
                        t0q, TQ, lnoff,
                        banks=(PTf[0][0:1, 0:TQ], PTf[0][0:1, TQ:2 * TQ],
                               PB(0), PB(1)),
                        stats_guards=sg, bc_guards=bg, norm_guards=ng)
                    pt_setcons(0, (0, 1), [("ve", ln_done)])
                    ast["pb_free"] = {0: [("ve", ln_done)], 1: [("ve", ln_done)]}
                    yield
                    dst_ready = {}
                    pb_i = [0]

                    def rhs_of(src, k, lnoff=lnoff):
                        if src == "y":
                            return y[:, k, lnoff:lnoff + TQ]
                        if src == "cq":
                            return cqb[:, k, :]
                        return ckvb[:, k, :]

                    for (pname, wname, bname, ksrc, kch, mch) in proj_specs:
                        for m in range(mch):
                            slot, c_w, ent = wa_load(
                                lambda e, slot, wname=wname, kch=kch, m=m: e.dma_start(
                                    out=wra[:, slot, 0:kch, :],
                                    in_=aw_bf[wname].rearrange("(k p) n -> p k n", p=128)[:, :, m * 128:(m + 1) * 128]))
                            h = pb_i[0] % 2
                            pb_i[0] += 1
                            bank_ap = PB(h)
                            sch.wait("tensor", "wa%d" % slot, c_w)
                            if pname == "cq" and m == 0:
                                sch.wait("tensor", "ve", ln_done)
                            for mk in ast["pb_free"][h]:
                                sch.wait("tensor", mk[0], mk[1])
                            if pname in ("qc", "qr") and m == 0:
                                sch.wait("tensor", "ve", dst_ready[("cq", 3)])
                            if pname in ("kc", "v") and m == 0:
                                sch.wait("tensor", "ve", dst_ready[("ckv", 3)])
                            grp = 0
                            for k in range(kch):
                                grp = sch.op("tensor", lambda e, slot=slot, k=k, ksrc=ksrc, bank_ap=bank_ap, kch=kch: e.matmul(
                                    bank_ap, wra[:, slot, k, :], rhs_of(ksrc, k),
                                    start=(k == 0), stop=(k == kch - 1)),
                                    inc="pe" if k == kch - 1 else None)
                            ent[1] = grp
                            sch.wait("vector", "pe", grp)
                            if pname in ("qr", "kr"):
                                qr0 = es256
                                sch.wait("vector", "pe", ast["ot_pe"])
                                c1v = sch.op("vector", lambda e, bank_ap=bank_ap, bname=bname, m=m, qr0=qr0: e.tensor_scalar(
                                    qr0, bank_ap, bias_ap(bname, m), None, ALU.add), inc="ve")
                                ast["pb_free"][h] = [("ve", c1v)]
                                sch.wait("tensor", "ve", c1v)
                                pt_guard("tensor", 0, (0,))
                                c_rot = sch.op("tensor", lambda e, qr0=qr0: e.matmul(
                                    PTf[0][:, 0:TQ], rotm[:], qr0, start=True, stop=True), inc="pe")
                                sch.wait("vector", "pe", c_rot)
                                sch.wait("vector", "pre", c_rl)
                                cosap = rope[:, 0:TQ]
                                sinap = rope[:, TQ:2 * TQ]
                                sch.op("vector", lambda e, cosap=cosap, qr0=qr0: e.tensor_tensor(rtm1[:], qr0, cosap, ALU.mult))
                                sch.op("vector", lambda e, sinap=sinap: e.tensor_tensor(rtm2[:], PTf[0][:, 0:TQ], sinap, ALU.mult))
                                dstap = (qrb if pname == "qr" else krb)[:, m, :]
                                for mk in ast["buf_pe"].get(pname, []):
                                    sch.wait("vector", mk[0], mk[1])
                                c2v = sch.op("vector", lambda e, dstap=dstap: e.tensor_add(dstap, rtm1[:], rtm2[:]), inc="ve")
                                ast["rope_ve"] = c2v
                                pt_setcons(0, (0,), [("ve", c2v)])
                                dst_ready[(pname, m)] = c2v
                            else:
                                if pname == "v":
                                    dstap = y[:, m, voff:voff + TQ]
                                    for mk in ast["yslot"].get((half, 1 - ys), [("pe", y_free_pe)]):
                                        sch.wait("vector", mk[0], mk[1])
                                else:
                                    bufs = {"cq": cqb, "ckv": ckvb, "qc": qcb, "kc": kcb}
                                    dstap = bufs[pname][:, m, :]
                                    for mk in ast["buf_pe"].get(pname, []):
                                        sch.wait("vector", mk[0], mk[1])
                                c1v = sch.op("vector", lambda e, dstap=dstap, bank_ap=bank_ap, bname=bname, m=m: e.tensor_scalar(
                                    dstap, bank_ap, bias_ap(bname, m), None, ALU.add), inc="ve")
                                ast["pb_free"][h] = [("ve", c1v)]
                                dst_ready[(pname, m)] = c1v
                            if m % 4 == 3:
                                yield
                    proj_all_ve = sch.val("ve")
                    proj_all_pe = sch.val("pe")
                    ast["yslot"][(half, ys)] = [("pe", proj_all_pe)]
                    ast["buf_pe"]["cq"] = [("pe", proj_all_pe)]
                    ast["buf_pe"]["ckv"] = [("pe", proj_all_pe)]
                    yield
                    for sti in range(TQ // 128):
                        s0 = sti * 128
                        gi = [0]

                        def tgroup(srcs, dst_builder, split4=False):
                            bank = gi[0] % 2
                            gi[0] += 1
                            sch.wait("tensor", "ve", proj_all_ve)
                            pt_guard("tensor", bank, (0, 1))
                            c_tp = 0
                            for idx, (srcap,) in enumerate(srcs):
                                c_tp = sch.op("tensor", lambda e, srcap=srcap, idx=idx, bank=bank: e.transpose(
                                    PT[bank][:, idx * 128:(idx + 1) * 128], srcap, eyeb[:]),
                                    inc="pe" if idx == 7 else None)
                            sch.wait("scalar", "pe", c_tp)
                            if ast["av_ve"]:
                                sch.wait("scalar", "ve", ast["av_ve"])
                            if split4:
                                src_ap = PT[bank].rearrange("p (g two d) -> p g two d", two=2, d=64)
                            else:
                                src_ap = PT[bank].rearrange("p (g q) -> p g q", q=128)
                            c = sch.op("scalar", lambda e, b=dst_builder, s=src_ap: e.copy(b(), s), inc="ac")
                            pt_setcons(bank, (0, 1), [("ac", c)])
                            return c

                        marks = []
                        for hh in range(2):
                            marks.append(tgroup([(qcb[:, hh * 8 + j, s0:s0 + 128],) for j in range(8)],
                                                lambda hh=hh: q192[:, hh * 8:(hh + 1) * 8, 0:128]))
                            yield
                        marks.append(tgroup([(qrb[:, j, s0:s0 + 128],) for j in range(8)],
                                            lambda: q192[:, :, 128:192].rearrange("p (a two) d -> p a two d", two=2),
                                            split4=True))
                        yield
                        for hh in range(2):
                            marks.append(tgroup([(kcb[:, hh * 8 + j, s0:s0 + 128],) for j in range(8)],
                                                lambda hh=hh: k192[:, hh * 8:(hh + 1) * 8, 0:128]))
                            yield
                        marks.append(tgroup([(krb[:, j, s0:s0 + 128],) for j in range(8)],
                                            lambda: k192[:, :, 128:192].rearrange("p (a two) d -> p a two d", two=2),
                                            split4=True))
                        yield
                        for hh in range(2):
                            marks.append(tgroup([(y[:, hh * 8 + j, voff + s0:voff + s0 + 128],) for j in range(8)],
                                                lambda hh=hh: v_t.rearrange("p (j d) -> p j d", d=128)[:, hh * 8:(hh + 1) * 8, :]))
                            yield
                        tq_pe = sch.val("pe")
                        for c in marks:
                            sch.wait("vector", "ac", c)
                        sch.wait("vector", "pe", ast["ot_pe"])
                        c_sc = 0
                        for i in range(H):
                            qi = q192[:, i:i + 1, :].broadcast_to([128, H, 192])
                            sch.op("vector", lambda e, qi=qi: e.tensor_tensor(prod, qi, k192, ALU.mult))
                            c_sc = sch.op("vector", lambda e, i=i: e.tensor_reduce(
                                sc256[:, i * 16:(i + 1) * 16], prod, AX.X, ALU.add),
                                inc="ve" if i == H - 1 else None)
                        yield
                        sch.wait("vector", "ve", c_sc)
                        sc3 = sc256.rearrange("p (i j) -> p i j", j=16)
                        c_r1 = sch.op("vector", lambda e, sc3=sc3: e.tensor_reduce(nm16[:], sc3, AX.X, ALU.max, negate=True), inc="ve")
                        sch.wait("vector", "ve", c_r1)
                        c_sm = sch.op("vector", lambda e, sc3=sc3: e.tensor_tensor(
                            sc3, sc3,
                            nm16.rearrange("p (i one) -> p i one", one=1).broadcast_to([128, 16, 16]),
                            ALU.add), inc="ve")
                        sch.wait("scalar", "ve", c_sm)
                        sch.wait("scalar", "pe", ast["ot_pe"])
                        c_exp = sch.op("scalar", lambda e: e.activation(es256[:], sc256[:], AF.Exp, scale=SCALER), inc="ac")
                        sch.wait("vector", "ac", c_exp)
                        es3 = es256.rearrange("p (i j) -> p i j", j=16)
                        c_r2 = sch.op("vector", lambda e, es3=es3: e.tensor_reduce(rs16[:], es3, AX.X, ALU.add), inc="ve")
                        sch.wait("vector", "ve", c_r2)
                        c_r3 = sch.op("vector", lambda e: e.reciprocal(rs16b[:], rs16[:]), inc="ve")
                        sch.wait("vector", "ve", c_r3)
                        at3 = at256.rearrange("p (i j) -> p i j", j=16)
                        sch.op("vector", lambda e, es3=es3, at3=at3: e.tensor_tensor(
                            at3, es3,
                            rs16b.rearrange("p (i one) -> p i one", one=1).broadcast_to([128, 16, 16]),
                            ALU.mult))
                        yield
                        vv = v_t.rearrange("p (j d) -> p d j", d=128)

                        def av_reduce(e, i):
                            with nc.allow_low_precision(reason="attn@v: fp32 accum, bf16 out"):
                                return e.tensor_reduce(o_t[:, i * 128:(i + 1) * 128], prod2, AX.X, ALU.add)

                        c_av = 0
                        for i in range(H):
                            ai = at3[:, i:i + 1, :].broadcast_to([128, 128, 16])
                            sch.op("vector", lambda e, ai=ai, vv=vv: e.tensor_tensor(prod2, ai, vv, ALU.mult))
                            c_av = sch.op("vector", lambda e, i=i: av_reduce(e, i),
                                          inc="ve" if i == H - 1 else None)
                        ast["av_ve"] = c_av
                        ast["yslot"][(half, 1 - ys)] = [("ve", c_av), ("pe", tq_pe)]
                        yield
                        sch.wait("tensor", "ve", c_av)
                        last_tp = 0
                        for hh in range(2):
                            bank = hh % 2
                            pt_guard("tensor", bank, (0, 1))
                            c_tp = 0
                            for idx in range(8):
                                mch = hh * 8 + idx
                                c_tp = sch.op("tensor", lambda e, mch=mch, idx=idx, bank=bank: e.transpose(
                                    PT[bank][:, idx * 128:(idx + 1) * 128],
                                    o_t[:, mch * 128:(mch + 1) * 128], eyeb[:]),
                                    inc="pe" if idx == 7 else None)
                            last_tp = c_tp
                            sch.wait("scalar", "pe", c_tp)
                            sch.wait("scalar", "ve", c_av)
                            dstb = (lambda hh=hh, s0=s0, voff=voff: y[:, hh * 8:(hh + 1) * 8, voff + s0:voff + s0 + 128])
                            src_ap = PT[bank].rearrange("p (g q) -> p g q", q=128)
                            c = sch.op("scalar", lambda e, dstb=dstb, s=src_ap: e.copy(dstb(), s), inc="ac")
                            pt_setcons(bank, (0, 1), [("ac", c)])
                            yield
                        ast["ot_pe"] = last_tp
                        ao_ac = ctr.get("ac", 0)
                    cur_pe = sch.val("pe")
                    for nmq in ("qc", "kc", "qr", "kr"):
                        ast["buf_pe"][nmq] = [("pe", cur_pe)]
                    sch.wait("tensor", "ac", ao_ac)
                    s3_last_ve = 0
                    for m in range(16):
                        slot, c_w, ent = wa_load(
                            lambda e, slot, m=m: e.dma_start(
                                out=wra[:, slot, :, :],
                                in_=aw_bf["Wo"].rearrange("(k p) n -> p k n", p=128)[:, :, m * 128:(m + 1) * 128]))
                        h = m % 2
                        bank_ap = PB(h)
                        sch.wait("tensor", "wa%d" % slot, c_w)
                        for mk in ast["pb_free"][h]:
                            sch.wait("tensor", mk[0], mk[1])
                        grp = 0
                        for k in range(KCH):
                            grp = sch.op("tensor", lambda e, slot=slot, k=k, bank_ap=bank_ap, voff=voff: e.matmul(
                                bank_ap, wra[:, slot, k, :], y[:, k, voff:voff + TQ],
                                start=(k == 0), stop=(k == KCH - 1)),
                                inc="pe" if k == KCH - 1 else None)
                        ent[1] = grp
                        sch.wait("vector", "pe", grp)
                        s3_last_ve = sch.op("vector", lambda e, m=m, bank_ap=bank_ap, t0q=t0q: e.scalar_tensor_tensor(
                            xt[:, m, t0q:t0q + TQ], bank_ap, bias_ap("bo", m),
                            xt[:, m, t0q:t0q + TQ], ALU.add, ALU.add), inc="ve")
                        ast["pb_free"][h] = [("ve", s3_last_ve)]
                        if m % 2 == 1:
                            yield
                    ast["yslot"][(half, 1 - ys)] = [("pe", sch.val("pe"))]
                    ast["s3_ve"][q] = s3_last_ve
                ast["last"] = dict(ve=sch.val("ve"), ac=sch.val("ac"), pe=sch.val("pe"))

            RATE = 0.45

            def mk_hook(gen):
                state = [0.0, 0]

                def hook():
                    state[0] += RATE
                    while state[1] < int(state[0]):
                        state[1] += 1
                        try:
                            next(gen)
                        except StopIteration:
                            return
                return hook

            def drain(gen):
                for _ in gen:
                    pass

            genA = att_half(0, xt_ready_ve=app_marks[0]["down_ve"],
                            y_free_pe=app_marks[0]["gu_pe"])
            emit_ffn_app(512, att_hook=mk_hook(genA), prev=app_marks[0],
                         ln_guards=dict(
                             stats=[("tensor", "ve", app_marks[0]["inter_ve"]),
                                    ("tensor", "ac", app_marks[0]["gelu_ac"])]))
            drain(genA)
            ATa_done = dict(ast["last"])
            s3A_ve = ast["s3_ve"][1]

            genB = att_half(1, xt_ready_ve=app_marks[1]["down_ve"],
                            y_free_pe=app_marks[1]["gu_pe"])
            emit_ffn_app(0, att_hook=mk_hook(genB), prev=app_marks[1],
                         ln_guards=dict(
                             stats=[("scalar", "ve", s3A_ve),
                                    ("tensor", "ve", max(s3A_ve, app_marks[1]["inter_ve"])),
                                    ("tensor", "ac", app_marks[1]["gelu_ac"])],
                             norm=[("vector", "ve", ATa_done["ve"]),
                                   ("vector", "pe", ATa_done["pe"])]))
            drain(genB)
            ATb_done = dict(ast["last"])
            s3B_ve = ast["s3_ve"][3]

        with ExitStack() as c4:
            ost = c4.enter_context(nc.sbuf_tensor("ost", [128, 2, D], F32))
            eyef = c4.enter_context(nc.sbuf_tensor("eyef_sb", [128, 128], F32))
            sch.op("sync", lambda e: e.dma_start(out=eyef[:], in_=eyef_d[:]), inc="pre", n=16)

            ost_dma = {}
            pt_free = {}
            out_guarded = [False]

            def emit_out_st8(st8, xt_ve):
                slot = st8 % 2
                last_ev = 0
                for grp4 in range(4):
                    bank = grp4 % 2
                    if not out_guarded[0]:
                        sch.wait("tensor", "pre", sch.val("pre"))
                        sch.wait("tensor", "ve", ATb_done["ve"])
                        sch.wait("tensor", "ac", ATb_done["ac"])
                        sch.wait("scalar", "pe", ATb_done["pe"])
                        sch.wait("scalar", "ve", ATb_done["ve"])
                        out_guarded[0] = True
                    sch.wait("tensor", "ve", xt_ve)
                    if bank in pt_free:
                        sch.wait("tensor", "ac", pt_free[bank])
                    c_tp = 0
                    for idx in range(4):
                        mch = grp4 * 4 + idx
                        c_tp = sch.op("tensor", lambda e, mch=mch, idx=idx, st8=st8, bank=bank: e.transpose(
                            PTf[bank][:, idx * 128:(idx + 1) * 128],
                            xt[:, mch, st8 * 128:(st8 + 1) * 128], eyef[:]),
                            inc="pe" if idx == 3 else None)
                    sch.wait("scalar", "pe", c_tp)
                    if st8 >= 2:
                        sch.wait("scalar", "os%d" % slot, ost_dma[st8 - 2])
                    last_ev = sch.op("scalar", lambda e, grp4=grp4, slot=slot, bank=bank: e.copy(
                        ost[:, slot, grp4 * 512:(grp4 + 1) * 512], PTf[bank][:, :]), inc="ac")
                    pt_free[bank] = last_ev
                sch.wait("sync", "ac", last_ev)
                ost_dma[st8] = sch.op("sync", lambda e, st8=st8, slot=slot: e.dma_start(
                    out=out_d[st8 * 128:(st8 + 1) * 128, :], in_=ost[:, slot]), inc="os%d" % slot, n=16)

            outA = [0]

            def outA_hook():
                outA[0] += 1
                if outA[0] % 60 == 0 and outA[0] // 60 <= 4:
                    emit_out_st8(outA[0] // 60 - 1, app_marks[2]["down_ve"])

            emit_ffn_app(512, att_hook=outA_hook, prev=app_marks[2],
                         ln_guards=dict(
                             stats=[("scalar", "ve", s3B_ve),
                                    ("tensor", "ve", max(s3B_ve, app_marks[2]["inter_ve"])),
                                    ("tensor", "ac", app_marks[2]["gelu_ac"])],
                             norm=[("vector", "ve", ATb_done["ve"]),
                                   ("vector", "pe", ATb_done["pe"])]))
            for st8 in range(4, 8):
                emit_out_st8(st8, app_marks[3]["down_ve"])
            sch.wait("sync", "os0", sch.val("os0"))
            sch.wait("sync", "os1", sch.val("os1"))

            with nc.Block() as blk:
                sch.flush(blk)

    return nc


_CACHE = {}


def _host_constants():
    j = np.arange(32, dtype=np.float32)
    inv_freq = (1.0 / (10000.0 ** (j / np.float32(32.0)))).astype(np.float32)
    dh = np.arange(128) % 64
    fr = inv_freq[dh % 32]
    R = np.zeros((128, 128), np.float32)
    for h2 in range(2):
        b = h2 * 64
        for dd in range(32):
            R[b + dd + 32, b + dd] = -1.0
            R[b + dd, b + dd + 32] = 1.0
    return fr, R


def kernel(**inputs):
    inp = {k: np.asarray(v) for k, v in inputs.items()}
    h_t = np.asarray(inp["L0_attn_h_t"], np.float32).reshape(S, D)

    fr, R = _host_constants()
    eye_bf = np.eye(128, dtype=ml_dtypes.bfloat16)
    eye_f = np.eye(128, dtype=np.float32)

    def pack_bias(v, chunks):
        return np.ascontiguousarray(np.asarray(v, np.float32).reshape(chunks, 128).T)

    bias_host = np.ascontiguousarray(np.concatenate([
        pack_bias(inp["L0_attn_bc"], 4),
        pack_bias(inp["L0_attn_bcq"], 16),
        pack_bias(inp["L0_attn_bqr"], 8),
        pack_bias(inp["L0_attn_bckv"], 4),
        pack_bias(inp["L0_attn_bck"], 16),
        pack_bias(inp["L0_attn_bkr"], 8),
        pack_bias(inp["L0_attn_bv"], 16),
        pack_bias(inp["L0_attn_bo"], 16),
    ], axis=1))

    if "nc" not in _CACHE:
        _CACHE["nc"] = build_program()
    nc = _CACHE["nc"]

    in_maps = []
    for c in range(NCORES):
        t0 = c * T
        tpos = np.arange(t0, t0 + T, dtype=np.float32)
        ang = tpos[None, :] * fr[:, None]
        rope_cs = np.ascontiguousarray(
            np.concatenate([np.cos(ang), np.sin(ang)], axis=1).astype(ml_dtypes.bfloat16))
        m = {
            "xT": np.ascontiguousarray(h_t[t0:t0 + T, :].T),
            "biases": bias_host,
            "rope_cs": rope_cs,
            "rotm": R,
            "eye_bf": eye_bf,
            "eye_f32": eye_f,
            "Wc": np.asarray(inp["L0_attn_Wc"], np.float32),
            "Wcq": np.asarray(inp["L0_attn_Wcq"], np.float32),
            "Wqr": np.asarray(inp["L0_attn_Wqr"], np.float32),
            "Wckv": np.asarray(inp["L0_attn_Wckv"], np.float32),
            "Wck": np.asarray(inp["L0_attn_Wck"], np.float32),
            "Wkr": np.asarray(inp["L0_attn_Wkr"], np.float32),
            "Wv": np.asarray(inp["L0_attn_Wv"], np.float32),
            "Wo": np.asarray(inp["L0_attn_Wo"], np.float32),
            "g_pk": np.ascontiguousarray(inp["L0_ffn_gate_W_packed"][c * 2048:(c + 1) * 2048, :]).astype(np.int32),
            "g_sc": np.ascontiguousarray(inp["L0_ffn_gate_scales"][c * 2048:(c + 1) * 2048, :]).astype(np.float32),
            "u_pk": np.ascontiguousarray(inp["L0_ffn_up_W_packed"][c * 2048:(c + 1) * 2048, :]).astype(np.int32),
            "u_sc": np.ascontiguousarray(inp["L0_ffn_up_scales"][c * 2048:(c + 1) * 2048, :]).astype(np.float32),
            "d_pk": np.ascontiguousarray(inp["L0_ffn_down_W_packed"][:, c * 1024:(c + 1) * 1024]).astype(np.int32),
            "d_sc": np.ascontiguousarray(inp["L0_ffn_down_scales"][:, c * 64:(c + 1) * 64]).astype(np.float32),
        }
        in_maps.append(m)

    res = run_bass_kernel_spmd(nc, in_maps, core_ids=list(range(NCORES)))
    out = np.concatenate([np.asarray(res.results[c]["out"]) for c in range(NCORES)], axis=0)
    return out.reshape(1, S, D).astype(np.float32)


# version 10
# speedup vs baseline: 1.2126x; 1.2126x over previous
import math
import numpy as np
import ml_dtypes
import concourse.bass as bass
import concourse.mybir as mybir
from concourse.bass_utils import run_bass_kernel_spmd
from contextlib import ExitStack

F32 = mybir.dt.float32
BF16 = mybir.dt.bfloat16
I32 = mybir.dt.int32
AF = mybir.ActivationFunctionType
ALU = mybir.AluOpType
AX = mybir.AxisListType

NCORES = 8
S = 8192
T = S // NCORES
D = 2048
NBLK = 8
KCH = 16
H = 16
TQ = 256
HW = 512
SCALER = 1.0 / math.sqrt(192.0)
C_EXP = 126 << 23
EPS = 1e-5

ENGINES = ("sync", "gpsimd", "vector", "scalar", "tensor")


class Sched:

    def __init__(self, ctr, sems):
        self.q = {e: [] for e in ENGINES}
        self.ctr = ctr
        self.sems = sems

    def wait(self, eng, sname, val):
        if val > 0:
            self.q[eng].append(lambda e, s=self.sems[sname], v=val: e.wait_ge(s, v))

    def op(self, eng, fn, inc=None, n=1):
        if inc is None:
            self.q[eng].append(fn)
            return self.ctr.get("_", 0)
        self.ctr[inc] = self.ctr.get(inc, 0) + n
        cnt = self.ctr[inc]
        sem = self.sems[inc]

        def run(e, fn=fn, sem=sem, n=n):
            fn(e).then_inc(sem, n)

        self.q[eng].append(run)
        return cnt

    def val(self, name):
        return self.ctr.get(name, 0)

    def flush(self, blk):
        for ename in ENGINES:
            ops = self.q[ename]
            if not ops:
                continue

            def body(e, ops=ops):
                for f in ops:
                    f(e)

            getattr(blk, ename)(body)
        self.q = {e: [] for e in ENGINES}


def build_program():
    nc = bass.Bass("TRN2", target_bir_lowering=False)

    xT_d = nc.dram_tensor("xT", [D, T], F32, kind="ExternalInput")
    out_d = nc.dram_tensor("out", [T, D], F32, kind="ExternalOutput")
    aw = {}
    for name, shp in (("Wc", [D, 512]), ("Wcq", [512, D]), ("Wqr", [512, 1024]),
                      ("Wckv", [D, 512]), ("Wck", [512, D]), ("Wkr", [D, 1024]),
                      ("Wv", [512, D]), ("Wo", [D, D])):
        aw[name] = nc.dram_tensor(name, shp, F32, kind="ExternalInput")
    bias_d = nc.dram_tensor("biases", [128, 88], F32, kind="ExternalInput")
    rope_d = nc.dram_tensor("rope_cs", [128, 2 * T], BF16, kind="ExternalInput")
    rotm_d = nc.dram_tensor("rotm", [128, 128], F32, kind="ExternalInput")
    eyeb_d = nc.dram_tensor("eye_bf", [128, 128], BF16, kind="ExternalInput")
    eyef_d = nc.dram_tensor("eye_f32", [128, 128], F32, kind="ExternalInput")
    pk_d = {m: nc.dram_tensor(f"{m}_pk", [2048, 1024], I32, kind="ExternalInput")
            for m in ("g", "u", "d")}
    sc_d = {m: nc.dram_tensor(f"{m}_sc", [2048, 64], F32, kind="ExternalInput")
            for m in ("g", "u", "d")}
    w_in = {m: nc.dram_tensor(f"{m}_in", [16, 128, 16, 128], BF16) for m in ("g", "u", "d")}
    aw_bf = {n: nc.dram_tensor(n + "_bf", list(aw[n].shape), BF16) for n in aw}
    w_out = {m: nc.dram_tensor(f"{m}_out", [4, NBLK, 4, 128, 16, 128], BF16) for m in ("g", "u", "d")}

    ctr = {}

    with ExitStack() as top:
        def sb(name, shape, dtype):
            return top.enter_context(nc.sbuf_tensor(name, shape, dtype))

        xt = sb("xt", [128, KCH, T], F32)
        y = sb("y", [128, KCH, T], BF16)
        shI = sb("shI", [128, KCH, HW], BF16)
        biases = sb("biases_sb", [128, 88], F32)
        rotm = sb("rotm_sb", [128, 128], F32)
        eyeb = sb("eyeb_sb", [128, 128], BF16)
        ones_c = sb("ones_c", [128, 1], F32)
        ones_r = sb("ones_r", [1, 128], F32)
        rows = sb("rows", [1, 3, 512], F32)
        nm16 = sb("nm16", [128, 16], F32)
        rs16 = sb("rs16", [128, 16], F32)
        rs16b = sb("rs16b", [128, 16], F32)
        wr = {m: sb(f"ring_{m}", [128, 2, 16, 128], BF16) for m in ("g", "u", "d")}
        g1 = sb("g1", [128, 1, HW], F32)
        sq_sb = sb("sq_sb", [128, 1, HW], F32)
        lntmp = sb("lntmp", [128, HW], F32)

        P = [top.enter_context(nc.psum_tensor(f"P{i}", [128, 512], F32)) for i in range(6)]
        PT = [top.enter_context(nc.psum_tensor(f"PT{i}", [128, 1024], BF16)) for i in range(2)]
        PTf = [t.bitcast(F32) for t in PT]

        sems = {k: top.enter_context(nc.semaphore("s_" + k))
                for k in ("ld", "gd", "cc", "pe", "ve", "ac", "pre",
                          "pk0", "pk1", "st0", "st1",
                          "wg0", "wg1", "wu0", "wu1", "wd0", "wd1",
                          "wa0", "wa1", "os0", "os1")}

        bias_cols = {}
        off = 0
        for bn, w in (("bc", 4), ("bcq", 16), ("bqr", 8), ("bckv", 4),
                      ("bck", 16), ("bkr", 8), ("bv", 16), ("bo", 16)):
            bias_cols[bn] = off
            off += w

        def bias_ap(bn, chunk):
            c = bias_cols[bn] + chunk
            return biases[:, c:c + 1]

        sch = Sched(ctr, sems)

        pre_total = [0]
        cc_at = {("g", 0): 1, ("u", 0): 2}
        _n = 2
        for _q in range(1, 4):
            _n += 1; cc_at[("g", _q)] = _n
            _n += 1; cc_at[("u", _q)] = _n
        for _q in range(4):
            _n += 1; cc_at[("d", _q)] = _n

        ln_chain = [0, 0]

        def emit_ln(t0, tw, ydst_off, banks, stats_guards=(), bc_guards=(),
                    norm_guards=()):
            st0, st1, bc0, bc1 = banks
            q_pe = {}
            sch.wait("tensor", "pre", pre_total[0])
            sch.wait("scalar", "pre", pre_total[0])
            sch.wait("vector", "pre", pre_total[0])
            for (eng, sname, cnt) in stats_guards:
                sch.wait(eng, sname, cnt)
            if ln_chain[1]:
                sch.wait("scalar", "pe", ln_chain[1])
            for ch in range(KCH):
                if ch >= 1:
                    sch.wait("scalar", "pe", q_pe[ch - 1])
                sqv = sq_sb[:, 0, 0:tw]
                c_ac = sch.op("scalar",
                              lambda e, ch=ch, sqv=sqv: e.activation(
                                  sqv, xt[:, ch, t0:t0 + tw], AF.Square),
                              inc="ac")
                sch.wait("tensor", "ac", c_ac)
                sch.op("tensor",
                       lambda e, ch=ch, st0=st0: e.matmul(st0, ones_c[:], xt[:, ch, t0:t0 + tw],
                                                          start=(ch == 0), stop=(ch == KCH - 1)))
                q_pe[ch] = sch.op("tensor",
                                  lambda e, sqv=sqv, st1=st1: e.matmul(
                                      st1, ones_c[:], sqv,
                                      start=(ch == 0), stop=(ch == KCH - 1)),
                                  inc="pe")
            stats_done = q_pe[KCH - 1]
            sch.wait("vector", "pe", stats_done)
            sch.op("vector", lambda e, st0=st0: e.tensor_scalar(rows[:, 0, 0:tw], st0, 1.0 / D, None, ALU.mult))
            sch.op("vector", lambda e, st1=st1: e.tensor_scalar(rows[:, 1, 0:tw], st1, 1.0 / D, None, ALU.mult))
            sch.op("vector", lambda e: e.tensor_tensor(rows[:, 2, 0:tw], rows[:, 0, 0:tw], rows[:, 0, 0:tw], ALU.mult))
            sch.op("vector", lambda e: e.tensor_tensor(rows[:, 2, 0:tw], rows[:, 1, 0:tw], rows[:, 2, 0:tw], ALU.subtract))
            c_ve = sch.op("vector", lambda e: e.tensor_scalar(rows[:, 2, 0:tw], rows[:, 2, 0:tw], EPS, None, ALU.add),
                          inc="ve")
            sch.wait("scalar", "ve", c_ve)
            c_ac = sch.op("scalar", lambda e: e.activation(rows[:, 1, 0:tw], rows[:, 2, 0:tw], AF.Sqrt),
                          inc="ac")
            sch.wait("vector", "ac", c_ac)
            c_ve = sch.op("vector", lambda e: e.reciprocal(rows[:, 2, 0:tw], rows[:, 1, 0:tw]), inc="ve")
            sch.wait("tensor", "ve", c_ve)
            for (eng, sname, cnt) in bc_guards:
                sch.wait(eng, sname, cnt)
            sch.op("tensor", lambda e, bc0=bc0: e.matmul(bc0, ones_r[:], rows[:, 0, 0:tw], start=True, stop=True))
            bc_done = sch.op("tensor", lambda e, bc1=bc1: e.matmul(bc1, ones_r[:], rows[:, 2, 0:tw], start=True, stop=True),
                             inc="pe")
            sch.wait("vector", "pe", bc_done)
            for (eng, sname, cnt) in norm_guards:
                sch.wait(eng, sname, cnt)
            c_n = 0
            for ch in range(KCH):
                sch.op("vector", lambda e, ch=ch, bc0=bc0: e.tensor_tensor(
                    lntmp[:, 0:tw], xt[:, ch, t0:t0 + tw], bc0, ALU.subtract))
                c_n = sch.op("vector", lambda e, ch=ch, bc1=bc1: e.tensor_tensor(
                    y[:, ch, ydst_off:ydst_off + tw], lntmp[:, 0:tw], bc1, ALU.mult),
                    inc="ve" if ch == KCH - 1 else None)
            ln_chain[0] = c_n
            ln_chain[1] = max(stats_done, bc_done)
            return c_n

        app_marks = []

        def emit_ffn_app(t0, first=False, dq_hook=None, att_hook=None,
                         ln_guards=None, prev=None):
            lg = ln_guards or {}
            ln_done = emit_ln(
                t0, HW, t0,
                banks=(P[2][0:1, :], P[3][0:1, :], P[0][:, :], P[1][:, :]),
                stats_guards=lg.get("stats", ()),
                bc_guards=lg.get("bc", ()),
                norm_guards=lg.get("norm", ()))

            wload = {}
            pe_grp = {}
            gelu_at = {}
            inter_at = {}
            evac_at = {}
            down_pe_done = {}

            for nb in range(NBLK):
                for mc in range(16):
                    gidx = nb * 16 + mc
                    if dq_hook is not None:
                        dq_hook(gidx)
                    if att_hook is not None:
                        att_hook()
                    slot = gidx % 2
                    if first and nb == 0:
                        sch.wait("sync", "cc", cc_at[("u", mc // 4)])
                    for m, qm in (("g", "sync"), ("u", "sync")):
                        if gidx >= 2:
                            sch.wait(qm, "pe", pe_grp[gidx - 2])
                        elif prev is not None:
                            sch.wait(qm, "pe", prev["gu_pe"])
                        wload[(m, gidx)] = sch.op(qm, lambda e, m=m, nb=nb, mc=mc, slot=slot: e.dma_start(
                            out=wr[m][:, slot], in_=w_out[m][mc // 4, nb, mc % 4]), inc="w%s%d" % (m, slot), n=16)
                    sch.wait("tensor", "wg%d" % slot, wload[("g", gidx)])
                    sch.wait("tensor", "wu%d" % slot, wload[("u", gidx)])
                    if gidx == 0:
                        sch.wait("tensor", "ve", ln_done)
                    if gidx >= 2:
                        sch.wait("tensor", "ac", gelu_at[gidx - 2])
                        sch.wait("tensor", "ve", inter_at[gidx - 2])
                    for k in range(KCH):
                        sch.op("tensor", lambda e, slot=slot, k=k: e.matmul(
                            P[slot][:, :], wr["g"][:, slot, k, :], y[:, k, t0:t0 + HW],
                            start=(k == 0), stop=(k == KCH - 1)))
                    grp_pe = 0
                    for k in range(KCH):
                        grp_pe = sch.op("tensor", lambda e, slot=slot, k=k: e.matmul(
                            P[2 + slot][:, :], wr["u"][:, slot, k, :], y[:, k, t0:t0 + HW],
                            start=(k == 0), stop=(k == KCH - 1)),
                            inc="pe" if k == KCH - 1 else None)
                    pe_grp[gidx] = grp_pe
                    sch.wait("scalar", "pe", grp_pe)
                    if gidx >= 1:
                        sch.wait("scalar", "ve", inter_at[gidx - 1])
                    elif prev is not None:
                        sch.wait("scalar", "ve", prev["inter_ve"])
                    gelu_at[gidx] = sch.op("scalar", lambda e, slot=slot: e.activation(
                        g1[:, 0, :], P[slot][:, :], AF.Gelu_apprx_tanh), inc="ac")
                    sch.wait("vector", "ac", gelu_at[gidx])
                    if nb >= 1 and mc == 0:
                        sch.wait("vector", "pe", down_pe_done[nb - 1])
                    elif nb == 0 and mc == 0 and prev is not None:
                        sch.wait("vector", "pe", prev["down_pe"])
                    inter_at[gidx] = sch.op("vector", lambda e, mc=mc, slot=slot: e.tensor_tensor(
                        shI[:, mc, :], g1[:, 0, :], P[2 + slot][:, :], ALU.mult),
                        inc="ve")
                for hc in range(16):
                    hgidx = nb * 16 + hc
                    slot = hgidx % 2
                    if att_hook is not None:
                        att_hook()
                    if first and nb == 0:
                        sch.wait("sync", "cc", cc_at[("d", hc // 4)])
                    if hgidx >= 2:
                        sch.wait("sync", "pe", pe_grp[("d", hgidx - 2)])
                    elif prev is not None:
                        sch.wait("sync", "pe", prev["down_pe"])
                    wload[("d", hgidx)] = sch.op("sync", lambda e, nb=nb, hc=hc, slot=slot: e.dma_start(
                        out=wr["d"][:, slot], in_=w_out["d"][hc // 4, nb, hc % 4]), inc="wd%d" % slot, n=16)
                    bank = 4 + (hc % 2)
                    sch.wait("tensor", "wd%d" % slot, wload[("d", hgidx)])
                    if hc == 0:
                        sch.wait("tensor", "ve", inter_at[nb * 16 + 15])
                    if hgidx >= 2:
                        sch.wait("tensor", "ve", evac_at[hgidx - 2])
                    grp_pe = 0
                    for k in range(KCH):
                        grp_pe = sch.op("tensor", lambda e, slot=slot, k=k, bank=bank: e.matmul(
                            P[bank][:, :], wr["d"][:, slot, k, :], shI[:, k, :],
                            start=(k == 0), stop=(k == KCH - 1)),
                            inc="pe" if k == KCH - 1 else None)
                    pe_grp[("d", hgidx)] = grp_pe
                    sch.wait("vector", "pe", grp_pe)
                    evac_at[hgidx] = sch.op("vector", lambda e, hc=hc, bank=bank, t0=t0: e.scalar_tensor_tensor(
                        xt[:, hc, t0:t0 + HW], P[bank][:, :], 0.5,
                        xt[:, hc, t0:t0 + HW], ALU.mult, ALU.add),
                        inc="ve")
                down_pe_done[nb] = pe_grp[("d", nb * 16 + 15)]

            app_marks.append(dict(
                gu_pe=pe_grp[NBLK * 16 - 1],
                down_ve=evac_at[NBLK * 16 - 1],
                gelu_ac=gelu_at[NBLK * 16 - 1],
                inter_ve=inter_at[NBLK * 16 - 1],
                down_pe=down_pe_done[NBLK - 1],
            ))

        sch.op("sync", lambda e: e.dma_start(out=xt[:], in_=xT_d.rearrange("(c p) t -> p c t", p=128)),
               inc="pre", n=16)
        sch.op("sync", lambda e: e.dma_start(out=biases[:], in_=bias_d[:]), inc="pre", n=16)
        sch.op("sync", lambda e: e.dma_start(out=rotm[:], in_=rotm_d[:]), inc="pre", n=16)
        sch.op("sync", lambda e: e.dma_start(out=eyeb[:], in_=eyeb_d[:]), inc="pre", n=16)
        PRE_ALL = sch.val("pre")
        pre_total[0] = PRE_ALL
        sch.op("vector", lambda e: e.memset(ones_c[:], 1.0))
        sch.op("vector", lambda e: e.memset(ones_r[:], 1.0), inc="ve")

        with ExitStack() as c0:
            pk_t = c0.enter_context(nc.sbuf_tensor("pk", [128, 2, 1024], I32))
            sc_t = c0.enter_context(nc.sbuf_tensor("sct", [128, 2, 64], F32))
            v0l = c0.enter_context(nc.sbuf_tensor("v0l", [128, 1024], I32))
            sl_t = c0.enter_context(nc.sbuf_tensor("sl", [128, 1024], I32))
            v8_t = c0.enter_context(nc.sbuf_tensor("v8", [128, 1024], F32))
            nat = c0.enter_context(nc.sbuf_tensor("nat", [128, 2, 2048], BF16))
            stage = c0.enter_context(nc.sbuf_tensor("stage", [128, 2, 16, 128], BF16))

            dq_done = {}
            tp_done = {}
            pt_evac = {}
            stage_dma = {}
            dq_hist = []

            def emit_dq_chunk(m, oc, gl):
                dq_hist.append(gl)
                slot = (len(dq_hist) - 1) % 2
                qeng = "gpsimd"
                if len(dq_hist) >= 3:
                    sch.wait(qeng, "ve", dq_done[dq_hist[-3]])
                pksem = "pk%d" % slot
                sch.op(qeng, lambda e, m=m, oc=oc, slot=slot: e.dma_start(
                    out=pk_t[:, slot, :],
                    in_=pk_d[m].rearrange("(c p) j -> c p j", p=128)[oc]), inc=pksem, n=16)
                c_ld = sch.op(qeng, lambda e, m=m, oc=oc, slot=slot: e.dma_start(
                    out=sc_t[:, slot, :],
                    in_=sc_d[m].rearrange("(c p) j -> c p j", p=128)[oc]), inc=pksem, n=16)
                sch.wait("vector", pksem, c_ld)
                if len(dq_hist) >= 3:
                    sch.wait("vector", "pe", tp_done[dq_hist[-3]])
                pkap = pk_t[:, slot, :]
                sc_b = sc_t.rearrange("p s (b one) -> p s b one", one=1)[:, slot].broadcast_to([128, 64, 16])
                c_dq = 0
                for par in (0, 1):
                    if par == 0:
                        sch.op("vector", lambda e, pkap=pkap: e.tensor_scalar(v0l[:], pkap, 7, 22, ALU.bitwise_and, ALU.logical_shift_left))
                        sch.op("vector", lambda e, pkap=pkap: e.tensor_scalar(sl_t[:], pkap, 8, 28, ALU.bitwise_and, ALU.logical_shift_left))
                    else:
                        sch.op("vector", lambda e, pkap=pkap: e.tensor_scalar(v0l[:], pkap, 4, None, ALU.logical_shift_right))
                        sch.op("vector", lambda e: e.tensor_scalar(v0l[:], v0l[:], 7, 22, ALU.bitwise_and, ALU.logical_shift_left))
                        sch.op("vector", lambda e, pkap=pkap: e.tensor_scalar(sl_t[:], pkap, 128, 24, ALU.bitwise_and, ALU.logical_shift_left))
                    sch.op("vector", lambda e: e.tensor_scalar(v0l[:], v0l[:], C_EXP, None, ALU.add))
                    v0f = v0l.bitcast(F32)
                    sch.op("vector", lambda e, v0f=v0f: e.tensor_scalar(v8_t[:], v0f[:], 1.0, 0.0, ALU.subtract, ALU.min))
                    sch.op("vector", lambda e, v0f=v0f: e.tensor_add(v8_t[:], v0f[:], v8_t[:]))
                    sch.op("vector", lambda e: e.tensor_tensor(v8_t.bitcast(I32)[:], v8_t.bitcast(I32)[:], sl_t[:], ALU.bitwise_xor))
                    natv = nat.rearrange("p s (b g two) -> p s b g two", two=2, g=16)[:, slot, :, :, par]
                    c_dq = sch.op("vector", lambda e, natv=natv, sc_b=sc_b: e.tensor_tensor(
                        natv, v8_t.rearrange("p (b g) -> p b g", g=16), sc_b, ALU.mult),
                        inc="ve" if par == 1 else None)
                dq_done[gl] = c_dq
                sch.wait("tensor", "ve", c_dq)
                if len(dq_hist) == 1:
                    sch.wait("tensor", "pre", PRE_ALL)
                for grp in range(2):
                    bank = grp
                    if len(dq_hist) >= 2:
                        sch.wait("tensor", "ac", pt_evac[(dq_hist[-2], grp)])
                    c_tp = 0
                    for kk in range(8):
                        k = grp * 8 + kk
                        c_tp = sch.op("tensor", lambda e, slot=slot, k=k, kk=kk, bank=bank: e.transpose(
                            PT[bank][:, kk * 128:(kk + 1) * 128],
                            nat[:, slot, k * 128:(k + 1) * 128], eyeb[:]),
                            inc="pe" if kk == 7 else None)
                    sch.wait("scalar", "pe", c_tp)
                    if len(dq_hist) >= 3:
                        sch.wait("scalar", "st%d" % slot, stage_dma[dq_hist[-3]])
                    pt_evac[(gl, grp)] = sch.op("scalar", lambda e, slot=slot, grp=grp, bank=bank: e.copy(
                        stage[:, slot, grp * 8:(grp + 1) * 8, :],
                        PT[bank].rearrange("p (g q) -> p g q", q=128)),
                        inc="ac")
                    if grp == 1:
                        tp_done[gl] = c_tp
                sch.wait("scalar", "ac", pt_evac[(gl, 1)])
                stage_dma[gl] = sch.op("scalar", lambda e, m=m, oc=oc, slot=slot: e.dma_start(
                    out=w_in[m][oc], in_=stage[:, slot]), inc="st%d" % slot, n=16)

            def emit_cc(m, q):
                sch.wait("gpsimd", "st0", sch.val("st0"))
                sch.wait("gpsimd", "st1", sch.val("st1"))
                rws = 4 * 128 * 16
                c = sch.op("gpsimd", lambda e, m=m, q=q, rws=rws: e.collective_compute(
                    "AllGather", ALU.bypass,
                    replica_groups=[list(range(NCORES))],
                    ins=[w_in[m].rearrange("a p k q -> (a p k) q")[q * rws:(q + 1) * rws, :]],
                    outs=[w_out[m][q].rearrange("n a p k q -> (n a p k) q")]),
                    inc="cc")
                assert c == cc_at[(m, q)], (m, q, c, cc_at[(m, q)])

            for m in ("g", "u"):
                for oc in range(4):
                    emit_dq_chunk(m, oc, (0 if m == "g" else 16) + oc)
                emit_cc(m, 0)

            rest = []
            for q in range(1, 4):
                rest += [("g", oc) for oc in range(4 * q, 4 * q + 4)]
                rest += [("u", oc) for oc in range(4 * q, 4 * q + 4)]
            rest += [("d", oc) for oc in range(16)]

            CPH = 3

            def dq_hook(gidx):
                for j in range(CPH * gidx, min(CPH * (gidx + 1), len(rest))):
                    m, oc = rest[j]
                    base = {"g": 0, "u": 16, "d": 32}[m]
                    emit_dq_chunk(m, oc, base + oc)
                    if oc % 4 == 3:
                        emit_cc(m, oc // 4)

            emit_ffn_app(0, first=True, dq_hook=dq_hook)
            DQ_PT_AC = ctr.get("ac", 0)
            DQ_ST0 = sch.val("st0")
            DQ_ST1 = sch.val("st1")
        for n in aw:
            sch.op("gpsimd", lambda e, n=n: e.dma_start(out=aw_bf[n][:], in_=aw[n][:]), inc="gd", n=16)
        AWBF_ALL = sch.val("gd")

        with ExitStack() as c2:
            attscr = c2.enter_context(nc.sbuf_tensor("attscr", [128, 6656], F32))
            cqb = c2.enter_context(nc.sbuf_tensor("cqb", [128, 4, TQ], BF16))
            ckvb = c2.enter_context(nc.sbuf_tensor("ckvb", [128, 4, TQ], BF16))
            qcb = c2.enter_context(nc.sbuf_tensor("qcb", [128, 16, TQ], BF16))
            kcb = c2.enter_context(nc.sbuf_tensor("kcb", [128, 16, TQ], BF16))
            qrb = c2.enter_context(nc.sbuf_tensor("qrb", [128, 8, TQ], BF16))
            krb = c2.enter_context(nc.sbuf_tensor("krb", [128, 8, TQ], BF16))
            wra = c2.enter_context(nc.sbuf_tensor("wra", [128, 1, 16, 128], BF16))

            attF = attscr
            attB = attscr.bitcast(BF16)
            prod = attB[:, 0:3072].rearrange("p (j d) -> p j d", d=192)
            prod2 = attB[:, 0:2048].rearrange("p (d j) -> p d j", j=16)
            q_t = attB[:, 3072:6144]
            k_t = attB[:, 6144:9216]
            v_t = attB[:, 9216:11264]
            o_t = attB[:, 11264:13312]
            es256 = attF[:, 5632:5888]
            rope = attB[:, 12032:12544]
            sc256 = attF[:, 6272:6528]
            at256 = attB[:, 13056:13312]
            rtm1 = lntmp[:, 0:TQ]
            rtm2 = lntmp[:, TQ:2 * TQ]
            q192 = q_t.rearrange("p (i d) -> p i d", d=192)
            k192 = k_t.rearrange("p (i d) -> p i d", d=192)

            def PB(h):
                return PTf[1][:, h * TQ:(h + 1) * TQ]

            ptst = {(b, h): [("ac", DQ_PT_AC)] for b in (0, 1) for h in (0, 1)}

            def pt_guard(eng, bank, halves):
                seen = set()
                for h in halves:
                    for mk in ptst[(bank, h)]:
                        if mk not in seen:
                            seen.add(mk)
                            sch.wait(eng, mk[0], mk[1])

            def pt_setcons(bank, halves, marks):
                for h in halves:
                    ptst[(bank, h)] = list(marks)

            proj_specs = [
                ("cq", "Wc", "bc", "y", 16, 4),
                ("ckv", "Wckv", "bckv", "y", 16, 4),
                ("qc", "Wcq", "bcq", "cq", 4, 16),
                ("qr", "Wqr", "bqr", "cq", 4, 8),
                ("kc", "Wck", "bck", "ckv", 4, 16),
                ("kr", "Wkr", "bkr", "y", 16, 8),
                ("v", "Wv", "bv", "ckv", 4, 16),
            ]

            wa_hist = []
            ast = dict(
                rope_ve=app_marks[0]["down_ve"], ot_pe=app_marks[0]["down_pe"],
                buf_pe={},
                yslot={},
                av_ve=0,
                pb_free={0: [], 1: []},
                wa_done=False,
                last=dict(ve=0, ac=0, pe=0),
                s3_ve={},
            )

            def wa_load(fn):
                slot = 0
                if not ast["wa_done"]:
                    sch.wait("gpsimd", "gd", AWBF_ALL)
                    ast["wa_done"] = True
                if len(wa_hist) >= 1:
                    sch.wait("gpsimd", "pe", wa_hist[-1][1])
                c_w = sch.op("gpsimd", lambda e, fn=fn, slot=slot: fn(e, slot), inc="wa0", n=16)
                ent = [c_w, None]
                wa_hist.append(ent)
                return slot, c_w, ent

            def att_half(half, xt_ready_ve, y_free_pe):
                abase = half * 512
                ao_ac = 0
                if half == 0:
                    sch.wait("vector", "st0", DQ_ST0)
                    sch.wait("vector", "st1", DQ_ST1)
                for qq in (0, 1):
                    q = 2 * half + qq
                    t0q = q * TQ
                    ys = qq
                    lnoff = abase + ys * TQ
                    voff = abase + (1 - ys) * TQ
                    sch.wait("gpsimd", "ve", ast["rope_ve"])
                    sch.wait("gpsimd", "pe", ast["ot_pe"])
                    c_rl = sch.op("gpsimd", lambda e, t0q=t0q: e.dma_start(
                        out=rope.rearrange("p (c t) -> p c t", c=2),
                        in_=rope_d.rearrange("p (c t) -> p c t", c=2)[:, :, t0q:t0q + TQ]), inc="pre", n=16)
                    yield
                    sg = [("scalar", "ve", xt_ready_ve), ("tensor", "ve", xt_ready_ve)]
                    for (s, c) in ptst[(0, 0)] + ptst[(0, 1)]:
                        sg.append(("tensor", s, c))
                    bg = [("tensor", s, c) for (s, c) in
                          ast["pb_free"][0] + ast["pb_free"][1]]
                    ng = [("vector",) + mk for mk in
                          ast["yslot"].get((half, ys), [("pe", y_free_pe)])]
                    ln_done = emit_ln(
                        t0q, TQ, lnoff,
                        banks=(PTf[0][0:1, 0:TQ], PTf[0][0:1, TQ:2 * TQ],
                               PB(0), PB(1)),
                        stats_guards=sg, bc_guards=bg, norm_guards=ng)
                    pt_setcons(0, (0, 1), [("ve", ln_done)])
                    ast["pb_free"] = {0: [("ve", ln_done)], 1: [("ve", ln_done)]}
                    yield
                    dst_ready = {}
                    pb_i = [0]

                    def rhs_of(src, k, lnoff=lnoff):
                        if src == "y":
                            return y[:, k, lnoff:lnoff + TQ]
                        if src == "cq":
                            return cqb[:, k, :]
                        return ckvb[:, k, :]

                    for (pname, wname, bname, ksrc, kch, mch) in proj_specs:
                        for m in range(mch):
                            slot, c_w, ent = wa_load(
                                lambda e, slot, wname=wname, kch=kch, m=m: e.dma_start(
                                    out=wra[:, slot, 0:kch, :],
                                    in_=aw_bf[wname].rearrange("(k p) n -> p k n", p=128)[:, :, m * 128:(m + 1) * 128]))
                            h = pb_i[0] % 2
                            pb_i[0] += 1
                            bank_ap = PB(h)
                            sch.wait("tensor", "wa%d" % slot, c_w)
                            if pname == "cq" and m == 0:
                                sch.wait("tensor", "ve", ln_done)
                            for mk in ast["pb_free"][h]:
                                sch.wait("tensor", mk[0], mk[1])
                            if pname in ("qc", "qr") and m == 0:
                                sch.wait("tensor", "ve", dst_ready[("cq", 3)])
                            if pname in ("kc", "v") and m == 0:
                                sch.wait("tensor", "ve", dst_ready[("ckv", 3)])
                            grp = 0
                            for k in range(kch):
                                grp = sch.op("tensor", lambda e, slot=slot, k=k, ksrc=ksrc, bank_ap=bank_ap, kch=kch: e.matmul(
                                    bank_ap, wra[:, slot, k, :], rhs_of(ksrc, k),
                                    start=(k == 0), stop=(k == kch - 1)),
                                    inc="pe" if k == kch - 1 else None)
                            ent[1] = grp
                            sch.wait("vector", "pe", grp)
                            if pname in ("qr", "kr"):
                                qr0 = es256
                                sch.wait("vector", "pe", ast["ot_pe"])
                                c1v = sch.op("vector", lambda e, bank_ap=bank_ap, bname=bname, m=m, qr0=qr0: e.tensor_scalar(
                                    qr0, bank_ap, bias_ap(bname, m), None, ALU.add), inc="ve")
                                ast["pb_free"][h] = [("ve", c1v)]
                                sch.wait("tensor", "ve", c1v)
                                pt_guard("tensor", 0, (0,))
                                c_rot = sch.op("tensor", lambda e, qr0=qr0: e.matmul(
                                    PTf[0][:, 0:TQ], rotm[:], qr0, start=True, stop=True), inc="pe")
                                sch.wait("vector", "pe", c_rot)
                                sch.wait("vector", "pre", c_rl)
                                cosap = rope[:, 0:TQ]
                                sinap = rope[:, TQ:2 * TQ]
                                sch.op("vector", lambda e, cosap=cosap, qr0=qr0: e.tensor_tensor(rtm1[:], qr0, cosap, ALU.mult))
                                sch.op("vector", lambda e, sinap=sinap: e.tensor_tensor(rtm2[:], PTf[0][:, 0:TQ], sinap, ALU.mult))
                                dstap = (qrb if pname == "qr" else krb)[:, m, :]
                                for mk in ast["buf_pe"].get(pname, []):
                                    sch.wait("vector", mk[0], mk[1])
                                c2v = sch.op("vector", lambda e, dstap=dstap: e.tensor_add(dstap, rtm1[:], rtm2[:]), inc="ve")
                                ast["rope_ve"] = c2v
                                pt_setcons(0, (0,), [("ve", c2v)])
                                dst_ready[(pname, m)] = c2v
                            else:
                                if pname == "v":
                                    dstap = y[:, m, voff:voff + TQ]
                                    for mk in ast["yslot"].get((half, 1 - ys), [("pe", y_free_pe)]):
                                        sch.wait("vector", mk[0], mk[1])
                                else:
                                    bufs = {"cq": cqb, "ckv": ckvb, "qc": qcb, "kc": kcb}
                                    dstap = bufs[pname][:, m, :]
                                    for mk in ast["buf_pe"].get(pname, []):
                                        sch.wait("vector", mk[0], mk[1])
                                c1v = sch.op("vector", lambda e, dstap=dstap, bank_ap=bank_ap, bname=bname, m=m: e.tensor_scalar(
                                    dstap, bank_ap, bias_ap(bname, m), None, ALU.add), inc="ve")
                                ast["pb_free"][h] = [("ve", c1v)]
                                dst_ready[(pname, m)] = c1v
                            if m % 4 == 3:
                                yield
                    proj_all_ve = sch.val("ve")
                    proj_all_pe = sch.val("pe")
                    ast["yslot"][(half, ys)] = [("pe", proj_all_pe)]
                    ast["buf_pe"]["cq"] = [("pe", proj_all_pe)]
                    ast["buf_pe"]["ckv"] = [("pe", proj_all_pe)]
                    yield
                    for sti in range(TQ // 128):
                        s0 = sti * 128
                        gi = [0]

                        def tgroup(srcs, dst_builder, split4=False):
                            bank = gi[0] % 2
                            gi[0] += 1
                            sch.wait("tensor", "ve", proj_all_ve)
                            pt_guard("tensor", bank, (0, 1))
                            c_tp = 0
                            for idx, (srcap,) in enumerate(srcs):
                                c_tp = sch.op("tensor", lambda e, srcap=srcap, idx=idx, bank=bank: e.transpose(
                                    PT[bank][:, idx * 128:(idx + 1) * 128], srcap, eyeb[:]),
                                    inc="pe" if idx == 7 else None)
                            sch.wait("scalar", "pe", c_tp)
                            if ast["av_ve"]:
                                sch.wait("scalar", "ve", ast["av_ve"])
                            if split4:
                                src_ap = PT[bank].rearrange("p (g two d) -> p g two d", two=2, d=64)
                            else:
                                src_ap = PT[bank].rearrange("p (g q) -> p g q", q=128)
                            c = sch.op("scalar", lambda e, b=dst_builder, s=src_ap: e.copy(b(), s), inc="ac")
                            pt_setcons(bank, (0, 1), [("ac", c)])
                            return c

                        marks = []
                        for hh in range(2):
                            marks.append(tgroup([(qcb[:, hh * 8 + j, s0:s0 + 128],) for j in range(8)],
                                                lambda hh=hh: q192[:, hh * 8:(hh + 1) * 8, 0:128]))
                            yield
                        marks.append(tgroup([(qrb[:, j, s0:s0 + 128],) for j in range(8)],
                                            lambda: q192[:, :, 128:192].rearrange("p (a two) d -> p a two d", two=2),
                                            split4=True))
                        yield
                        for hh in range(2):
                            marks.append(tgroup([(kcb[:, hh * 8 + j, s0:s0 + 128],) for j in range(8)],
                                                lambda hh=hh: k192[:, hh * 8:(hh + 1) * 8, 0:128]))
                            yield
                        marks.append(tgroup([(krb[:, j, s0:s0 + 128],) for j in range(8)],
                                            lambda: k192[:, :, 128:192].rearrange("p (a two) d -> p a two d", two=2),
                                            split4=True))
                        yield
                        for hh in range(2):
                            marks.append(tgroup([(y[:, hh * 8 + j, voff + s0:voff + s0 + 128],) for j in range(8)],
                                                lambda hh=hh: v_t.rearrange("p (j d) -> p j d", d=128)[:, hh * 8:(hh + 1) * 8, :]))
                            yield
                        tq_pe = sch.val("pe")
                        for c in marks:
                            sch.wait("vector", "ac", c)
                        sch.wait("vector", "pe", ast["ot_pe"])
                        c_sc = 0
                        for i in range(H):
                            qi = q192[:, i:i + 1, :].broadcast_to([128, H, 192])
                            sch.op("vector", lambda e, qi=qi: e.tensor_tensor(prod, qi, k192, ALU.mult))
                            c_sc = sch.op("vector", lambda e, i=i: e.tensor_reduce(
                                sc256[:, i * 16:(i + 1) * 16], prod, AX.X, ALU.add),
                                inc="ve" if i == H - 1 else None)
                        yield
                        sch.wait("vector", "ve", c_sc)
                        sc3 = sc256.rearrange("p (i j) -> p i j", j=16)
                        c_r1 = sch.op("vector", lambda e, sc3=sc3: e.tensor_reduce(nm16[:], sc3, AX.X, ALU.max, negate=True), inc="ve")
                        sch.wait("vector", "ve", c_r1)
                        c_sm = sch.op("vector", lambda e, sc3=sc3: e.tensor_tensor(
                            sc3, sc3,
                            nm16.rearrange("p (i one) -> p i one", one=1).broadcast_to([128, 16, 16]),
                            ALU.add), inc="ve")
                        sch.wait("scalar", "ve", c_sm)
                        sch.wait("scalar", "pe", ast["ot_pe"])
                        c_exp = sch.op("scalar", lambda e: e.activation(es256[:], sc256[:], AF.Exp, scale=SCALER), inc="ac")
                        sch.wait("vector", "ac", c_exp)
                        es3 = es256.rearrange("p (i j) -> p i j", j=16)
                        c_r2 = sch.op("vector", lambda e, es3=es3: e.tensor_reduce(rs16[:], es3, AX.X, ALU.add), inc="ve")
                        sch.wait("vector", "ve", c_r2)
                        c_r3 = sch.op("vector", lambda e: e.reciprocal(rs16b[:], rs16[:]), inc="ve")
                        sch.wait("vector", "ve", c_r3)
                        at3 = at256.rearrange("p (i j) -> p i j", j=16)
                        sch.op("vector", lambda e, es3=es3, at3=at3: e.tensor_tensor(
                            at3, es3,
                            rs16b.rearrange("p (i one) -> p i one", one=1).broadcast_to([128, 16, 16]),
                            ALU.mult))
                        yield
                        vv = v_t.rearrange("p (j d) -> p d j", d=128)

                        def av_reduce(e, i):
                            with nc.allow_low_precision(reason="attn@v: fp32 accum, bf16 out"):
                                return e.tensor_reduce(o_t[:, i * 128:(i + 1) * 128], prod2, AX.X, ALU.add)

                        c_av = 0
                        for i in range(H):
                            ai = at3[:, i:i + 1, :].broadcast_to([128, 128, 16])
                            sch.op("vector", lambda e, ai=ai, vv=vv: e.tensor_tensor(prod2, ai, vv, ALU.mult))
                            c_av = sch.op("vector", lambda e, i=i: av_reduce(e, i),
                                          inc="ve" if i == H - 1 else None)
                        ast["av_ve"] = c_av
                        ast["yslot"][(half, 1 - ys)] = [("ve", c_av), ("pe", tq_pe)]
                        yield
                        sch.wait("tensor", "ve", c_av)
                        last_tp = 0
                        for hh in range(2):
                            bank = hh % 2
                            pt_guard("tensor", bank, (0, 1))
                            c_tp = 0
                            for idx in range(8):
                                mch = hh * 8 + idx
                                c_tp = sch.op("tensor", lambda e, mch=mch, idx=idx, bank=bank: e.transpose(
                                    PT[bank][:, idx * 128:(idx + 1) * 128],
                                    o_t[:, mch * 128:(mch + 1) * 128], eyeb[:]),
                                    inc="pe" if idx == 7 else None)
                            last_tp = c_tp
                            sch.wait("scalar", "pe", c_tp)
                            sch.wait("scalar", "ve", c_av)
                            dstb = (lambda hh=hh, s0=s0, voff=voff: y[:, hh * 8:(hh + 1) * 8, voff + s0:voff + s0 + 128])
                            src_ap = PT[bank].rearrange("p (g q) -> p g q", q=128)
                            c = sch.op("scalar", lambda e, dstb=dstb, s=src_ap: e.copy(dstb(), s), inc="ac")
                            pt_setcons(bank, (0, 1), [("ac", c)])
                            yield
                        ast["ot_pe"] = last_tp
                        ao_ac = ctr.get("ac", 0)
                    cur_pe = sch.val("pe")
                    for nmq in ("qc", "kc", "qr", "kr"):
                        ast["buf_pe"][nmq] = [("pe", cur_pe)]
                    sch.wait("tensor", "ac", ao_ac)
                    s3_last_ve = 0
                    for m in range(16):
                        slot, c_w, ent = wa_load(
                            lambda e, slot, m=m: e.dma_start(
                                out=wra[:, slot, :, :],
                                in_=aw_bf["Wo"].rearrange("(k p) n -> p k n", p=128)[:, :, m * 128:(m + 1) * 128]))
                        h = m % 2
                        bank_ap = PB(h)
                        sch.wait("tensor", "wa%d" % slot, c_w)
                        for mk in ast["pb_free"][h]:
                            sch.wait("tensor", mk[0], mk[1])
                        grp = 0
                        for k in range(KCH):
                            grp = sch.op("tensor", lambda e, slot=slot, k=k, bank_ap=bank_ap, voff=voff: e.matmul(
                                bank_ap, wra[:, slot, k, :], y[:, k, voff:voff + TQ],
                                start=(k == 0), stop=(k == KCH - 1)),
                                inc="pe" if k == KCH - 1 else None)
                        ent[1] = grp
                        sch.wait("vector", "pe", grp)
                        s3_last_ve = sch.op("vector", lambda e, m=m, bank_ap=bank_ap, t0q=t0q: e.scalar_tensor_tensor(
                            xt[:, m, t0q:t0q + TQ], bank_ap, bias_ap("bo", m),
                            xt[:, m, t0q:t0q + TQ], ALU.add, ALU.add), inc="ve")
                        ast["pb_free"][h] = [("ve", s3_last_ve)]
                        if m % 2 == 1:
                            yield
                    ast["yslot"][(half, 1 - ys)] = [("pe", sch.val("pe"))]
                    ast["s3_ve"][q] = s3_last_ve
                ast["last"] = dict(ve=sch.val("ve"), ac=sch.val("ac"), pe=sch.val("pe"))

            RATE = 0.45

            def mk_hook(gen):
                state = [0.0, 0]

                def hook():
                    state[0] += RATE
                    while state[1] < int(state[0]):
                        state[1] += 1
                        try:
                            next(gen)
                        except StopIteration:
                            return
                return hook

            def drain(gen):
                for _ in gen:
                    pass

            genA = att_half(0, xt_ready_ve=app_marks[0]["down_ve"],
                            y_free_pe=app_marks[0]["gu_pe"])
            emit_ffn_app(512, att_hook=mk_hook(genA), prev=app_marks[0],
                         ln_guards=dict(
                             stats=[("tensor", "ve", app_marks[0]["inter_ve"]),
                                    ("tensor", "ac", app_marks[0]["gelu_ac"])]))
            drain(genA)
            ATa_done = dict(ast["last"])
            s3A_ve = ast["s3_ve"][1]

            genB = att_half(1, xt_ready_ve=app_marks[1]["down_ve"],
                            y_free_pe=app_marks[1]["gu_pe"])
            emit_ffn_app(0, att_hook=mk_hook(genB), prev=app_marks[1],
                         ln_guards=dict(
                             stats=[("scalar", "ve", s3A_ve),
                                    ("tensor", "ve", max(s3A_ve, app_marks[1]["inter_ve"])),
                                    ("tensor", "ac", app_marks[1]["gelu_ac"])],
                             norm=[("vector", "ve", ATa_done["ve"]),
                                   ("vector", "pe", ATa_done["pe"])]))
            drain(genB)
            ATb_done = dict(ast["last"])
            s3B_ve = ast["s3_ve"][3]

        with ExitStack() as c4:
            ost = c4.enter_context(nc.sbuf_tensor("ost", [128, 2, D], F32))
            eyef = c4.enter_context(nc.sbuf_tensor("eyef_sb", [128, 128], F32))
            sch.op("sync", lambda e: e.dma_start(out=eyef[:], in_=eyef_d[:]), inc="pre", n=16)

            ost_dma = {}
            pt_free = {}
            out_guarded = [False]

            def emit_out_st8(st8, xt_ve):
                slot = st8 % 2
                last_ev = 0
                for grp4 in range(4):
                    bank = grp4 % 2
                    if not out_guarded[0]:
                        sch.wait("tensor", "pre", sch.val("pre"))
                        sch.wait("tensor", "ve", ATb_done["ve"])
                        sch.wait("tensor", "ac", ATb_done["ac"])
                        sch.wait("scalar", "pe", ATb_done["pe"])
                        sch.wait("scalar", "ve", ATb_done["ve"])
                        out_guarded[0] = True
                    sch.wait("tensor", "ve", xt_ve)
                    if bank in pt_free:
                        sch.wait("tensor", "ac", pt_free[bank])
                    c_tp = 0
                    for idx in range(4):
                        mch = grp4 * 4 + idx
                        c_tp = sch.op("tensor", lambda e, mch=mch, idx=idx, st8=st8, bank=bank: e.transpose(
                            PTf[bank][:, idx * 128:(idx + 1) * 128],
                            xt[:, mch, st8 * 128:(st8 + 1) * 128], eyef[:]),
                            inc="pe" if idx == 3 else None)
                    sch.wait("scalar", "pe", c_tp)
                    if st8 >= 2:
                        sch.wait("scalar", "os%d" % slot, ost_dma[st8 - 2])
                    last_ev = sch.op("scalar", lambda e, grp4=grp4, slot=slot, bank=bank: e.copy(
                        ost[:, slot, grp4 * 512:(grp4 + 1) * 512], PTf[bank][:, :]), inc="ac")
                    pt_free[bank] = last_ev
                sch.wait("sync", "ac", last_ev)
                ost_dma[st8] = sch.op("sync", lambda e, st8=st8, slot=slot: e.dma_start(
                    out=out_d[st8 * 128:(st8 + 1) * 128, :], in_=ost[:, slot]), inc="os%d" % slot, n=16)

            outA = [0]

            def outA_hook():
                outA[0] += 1
                if outA[0] % 60 == 0 and outA[0] // 60 <= 4:
                    emit_out_st8(outA[0] // 60 - 1, app_marks[2]["down_ve"])

            emit_ffn_app(512, att_hook=outA_hook, prev=app_marks[2],
                         ln_guards=dict(
                             stats=[("scalar", "ve", s3B_ve),
                                    ("tensor", "ve", max(s3B_ve, app_marks[2]["inter_ve"])),
                                    ("tensor", "ac", app_marks[2]["gelu_ac"])],
                             norm=[("vector", "ve", ATb_done["ve"]),
                                   ("vector", "pe", ATb_done["pe"])]))
            for st8 in range(4, 8):
                emit_out_st8(st8, app_marks[3]["down_ve"])
            sch.wait("sync", "os0", sch.val("os0"))
            sch.wait("sync", "os1", sch.val("os1"))

            with nc.Block() as blk:
                sch.flush(blk)

    return nc


_CACHE = {}


def _host_constants():
    j = np.arange(32, dtype=np.float32)
    inv_freq = (1.0 / (10000.0 ** (j / np.float32(32.0)))).astype(np.float32)
    dh = np.arange(128) % 64
    fr = inv_freq[dh % 32]
    R = np.zeros((128, 128), np.float32)
    for h2 in range(2):
        b = h2 * 64
        for dd in range(32):
            R[b + dd + 32, b + dd] = -1.0
            R[b + dd, b + dd + 32] = 1.0
    return fr, R


def kernel(**inputs):
    inp = {k: np.asarray(v) for k, v in inputs.items()}
    h_t = np.asarray(inp["L0_attn_h_t"], np.float32).reshape(S, D)

    fr, R = _host_constants()
    eye_bf = np.eye(128, dtype=ml_dtypes.bfloat16)
    eye_f = np.eye(128, dtype=np.float32)

    def pack_bias(v, chunks):
        return np.ascontiguousarray(np.asarray(v, np.float32).reshape(chunks, 128).T)

    bias_host = np.ascontiguousarray(np.concatenate([
        pack_bias(inp["L0_attn_bc"], 4),
        pack_bias(inp["L0_attn_bcq"], 16),
        pack_bias(inp["L0_attn_bqr"], 8),
        pack_bias(inp["L0_attn_bckv"], 4),
        pack_bias(inp["L0_attn_bck"], 16),
        pack_bias(inp["L0_attn_bkr"], 8),
        pack_bias(inp["L0_attn_bv"], 16),
        pack_bias(inp["L0_attn_bo"], 16),
    ], axis=1))

    if "nc" not in _CACHE:
        _CACHE["nc"] = build_program()
    nc = _CACHE["nc"]

    in_maps = []
    for c in range(NCORES):
        t0 = c * T
        tpos = np.arange(t0, t0 + T, dtype=np.float32)
        ang = tpos[None, :] * fr[:, None]
        rope_cs = np.ascontiguousarray(
            np.concatenate([np.cos(ang), np.sin(ang)], axis=1).astype(ml_dtypes.bfloat16))
        m = {
            "xT": np.ascontiguousarray(h_t[t0:t0 + T, :].T),
            "biases": bias_host,
            "rope_cs": rope_cs,
            "rotm": R,
            "eye_bf": eye_bf,
            "eye_f32": eye_f,
            "Wc": np.asarray(inp["L0_attn_Wc"], np.float32),
            "Wcq": np.asarray(inp["L0_attn_Wcq"], np.float32),
            "Wqr": np.asarray(inp["L0_attn_Wqr"], np.float32),
            "Wckv": np.asarray(inp["L0_attn_Wckv"], np.float32),
            "Wck": np.asarray(inp["L0_attn_Wck"], np.float32),
            "Wkr": np.asarray(inp["L0_attn_Wkr"], np.float32),
            "Wv": np.asarray(inp["L0_attn_Wv"], np.float32),
            "Wo": np.asarray(inp["L0_attn_Wo"], np.float32),
            "g_pk": np.ascontiguousarray(inp["L0_ffn_gate_W_packed"][c * 2048:(c + 1) * 2048, :]).astype(np.int32),
            "g_sc": np.ascontiguousarray(inp["L0_ffn_gate_scales"][c * 2048:(c + 1) * 2048, :]).astype(np.float32),
            "u_pk": np.ascontiguousarray(inp["L0_ffn_up_W_packed"][c * 2048:(c + 1) * 2048, :]).astype(np.int32),
            "u_sc": np.ascontiguousarray(inp["L0_ffn_up_scales"][c * 2048:(c + 1) * 2048, :]).astype(np.float32),
            "d_pk": np.ascontiguousarray(inp["L0_ffn_down_W_packed"][:, c * 1024:(c + 1) * 1024]).astype(np.int32),
            "d_sc": np.ascontiguousarray(inp["L0_ffn_down_scales"][:, c * 64:(c + 1) * 64]).astype(np.float32),
        }
        in_maps.append(m)

    res = run_bass_kernel_spmd(nc, in_maps, core_ids=list(range(NCORES)))
    out = np.concatenate([np.asarray(res.results[c]["out"]) for c in range(NCORES)], axis=0)
    return out.reshape(1, S, D).astype(np.float32)


# version 11
# speedup vs baseline: 1.3148x; 1.0843x over previous
import math
import numpy as np
import ml_dtypes
import concourse.bass as bass
import concourse.mybir as mybir
from concourse.bass_utils import run_bass_kernel_spmd
from contextlib import ExitStack

F32 = mybir.dt.float32
BF16 = mybir.dt.bfloat16
I32 = mybir.dt.int32
AF = mybir.ActivationFunctionType
ALU = mybir.AluOpType
AX = mybir.AxisListType

NCORES = 8
S = 8192
T = S // NCORES
D = 2048
NBLK = 8
KCH = 16
H = 16
TQ = 256
HW = 512
SCALER = 1.0 / math.sqrt(192.0)
C_EXP = 126 << 23
EPS = 1e-5

ENGINES = ("sync", "gpsimd", "vector", "scalar", "tensor")


class Sched:

    def __init__(self, ctr, sems):
        self.q = {e: [] for e in ENGINES}
        self.ctr = ctr
        self.sems = sems

    def wait(self, eng, sname, val):
        if val > 0:
            self.q[eng].append(lambda e, s=self.sems[sname], v=val: e.wait_ge(s, v))

    def op(self, eng, fn, inc=None, n=1):
        if inc is None:
            self.q[eng].append(fn)
            return self.ctr.get("_", 0)
        self.ctr[inc] = self.ctr.get(inc, 0) + n
        cnt = self.ctr[inc]
        sem = self.sems[inc]

        def run(e, fn=fn, sem=sem, n=n):
            fn(e).then_inc(sem, n)

        self.q[eng].append(run)
        return cnt

    def val(self, name):
        return self.ctr.get(name, 0)

    def flush(self, blk):
        for ename in ENGINES:
            ops = self.q[ename]
            if not ops:
                continue

            def body(e, ops=ops):
                for f in ops:
                    f(e)

            getattr(blk, ename)(body)
        self.q = {e: [] for e in ENGINES}


def build_program():
    nc = bass.Bass("TRN2", target_bir_lowering=False)

    xT_d = nc.dram_tensor("xT", [D, T], F32, kind="ExternalInput")
    out_d = nc.dram_tensor("out", [T, D], F32, kind="ExternalOutput")
    aw = {}
    for name, shp in (("Wc", [D, 512]), ("Wcq", [512, D]), ("Wqr", [512, 1024]),
                      ("Wckv", [D, 512]), ("Wck", [512, D]), ("Wkr", [D, 1024]),
                      ("Wv", [512, D]), ("Wo", [D, D])):
        aw[name] = nc.dram_tensor(name, shp, F32, kind="ExternalInput")
    bias_d = nc.dram_tensor("biases", [128, 88], F32, kind="ExternalInput")
    rope_d = nc.dram_tensor("rope_cs", [128, 2 * T], BF16, kind="ExternalInput")
    rotm_d = nc.dram_tensor("rotm", [128, 128], F32, kind="ExternalInput")
    eyeb_d = nc.dram_tensor("eye_bf", [128, 128], BF16, kind="ExternalInput")
    eyef_d = nc.dram_tensor("eye_f32", [128, 128], F32, kind="ExternalInput")
    pk_d = {m: nc.dram_tensor(f"{m}_pk", [2048, 1024], I32, kind="ExternalInput")
            for m in ("g", "u", "d")}
    sc_d = {m: nc.dram_tensor(f"{m}_sc", [2048, 64], F32, kind="ExternalInput")
            for m in ("g", "u", "d")}
    w_in = {m: nc.dram_tensor(f"{m}_in", [16, 128, 16, 128], BF16) for m in ("g", "u", "d")}
    aw_bf = {n: nc.dram_tensor(n + "_bf", list(aw[n].shape), BF16) for n in aw}
    w_out = {m: nc.dram_tensor(f"{m}_out", [4, NBLK, 4, 128, 16, 128], BF16) for m in ("g", "u", "d")}

    ctr = {}

    with ExitStack() as top:
        def sb(name, shape, dtype):
            return top.enter_context(nc.sbuf_tensor(name, shape, dtype))

        xt = sb("xt", [128, KCH, T], F32)
        y = sb("y", [128, KCH, T], BF16)
        shI = sb("shI", [128, KCH, HW], BF16)
        biases = sb("biases_sb", [128, 88], F32)
        rotm = sb("rotm_sb", [128, 128], F32)
        eyeb = sb("eyeb_sb", [128, 128], BF16)
        ones_c = sb("ones_c", [128, 1], F32)
        ones_r = sb("ones_r", [1, 128], F32)
        rows = sb("rows", [1, 3, 512], F32)
        nm16 = sb("nm16", [128, 16], F32)
        rs16 = sb("rs16", [128, 16], F32)
        rs16b = sb("rs16b", [128, 16], F32)
        wr = {m: sb(f"ring_{m}", [128, 2, 16, 128], BF16) for m in ("g", "u", "d")}
        g1 = sb("g1", [128, 1, HW], F32)
        sq_sb = sb("sq_sb", [128, 1, HW], F32)
        lntmp = sb("lntmp", [128, HW], F32)

        P = [top.enter_context(nc.psum_tensor(f"P{i}", [128, 512], F32)) for i in range(6)]
        PT = [top.enter_context(nc.psum_tensor(f"PT{i}", [128, 1024], BF16)) for i in range(2)]
        PTf = [t.bitcast(F32) for t in PT]

        sems = {k: top.enter_context(nc.semaphore("s_" + k))
                for k in ("ld", "gd", "cc", "pe", "ve", "ac", "pre",
                          "pk0", "pk1", "st0", "st1",
                          "wg0", "wg1", "wu0", "wu1", "wd0", "wd1",
                          "wa0", "wa1", "os0", "os1")}

        bias_cols = {}
        off = 0
        for bn, w in (("bc", 4), ("bcq", 16), ("bqr", 8), ("bckv", 4),
                      ("bck", 16), ("bkr", 8), ("bv", 16), ("bo", 16)):
            bias_cols[bn] = off
            off += w

        def bias_ap(bn, chunk):
            c = bias_cols[bn] + chunk
            return biases[:, c:c + 1]

        sch = Sched(ctr, sems)

        pre_total = [0]
        cc_at = {("g", 0): 1, ("u", 0): 2}
        _n = 2
        for _q in range(1, 4):
            _n += 1; cc_at[("g", _q)] = _n
            _n += 1; cc_at[("u", _q)] = _n
        for _q in range(4):
            _n += 1; cc_at[("d", _q)] = _n

        ln_chain = [0, 0]

        def g_ln(t0, tw, ydst_off, banks, res, stats_guards=(), bc_guards=(),
                 norm_guards=()):
            st0, st1, bc0, bc1 = banks
            q_pe = {}
            sch.wait("tensor", "pre", pre_total[0])
            sch.wait("scalar", "pre", pre_total[0])
            sch.wait("vector", "pre", pre_total[0])
            for (eng, sname, cnt) in stats_guards:
                sch.wait(eng, sname, cnt)
            if ln_chain[1]:
                sch.wait("scalar", "pe", ln_chain[1])
            for ch in range(KCH):
                if ch >= 1:
                    sch.wait("scalar", "pe", q_pe[ch - 1])
                sqv = sq_sb[:, 0, 0:tw]
                c_ac = sch.op("scalar",
                              lambda e, ch=ch, sqv=sqv: e.activation(
                                  sqv, xt[:, ch, t0:t0 + tw], AF.Square),
                              inc="ac")
                sch.wait("tensor", "ac", c_ac)
                sch.op("tensor",
                       lambda e, ch=ch, st0=st0: e.matmul(st0, ones_c[:], xt[:, ch, t0:t0 + tw],
                                                          start=(ch == 0), stop=(ch == KCH - 1)))
                q_pe[ch] = sch.op("tensor",
                                  lambda e, sqv=sqv, st1=st1: e.matmul(
                                      st1, ones_c[:], sqv,
                                      start=(ch == 0), stop=(ch == KCH - 1)),
                                  inc="pe")
                if ch % 2 == 1:
                    yield
            stats_done = q_pe[KCH - 1]
            sch.wait("vector", "pe", stats_done)
            sch.op("vector", lambda e, st0=st0: e.tensor_scalar(rows[:, 0, 0:tw], st0, 1.0 / D, None, ALU.mult))
            sch.op("vector", lambda e, st1=st1: e.tensor_scalar(rows[:, 1, 0:tw], st1, 1.0 / D, None, ALU.mult))
            sch.op("vector", lambda e: e.tensor_tensor(rows[:, 2, 0:tw], rows[:, 0, 0:tw], rows[:, 0, 0:tw], ALU.mult))
            sch.op("vector", lambda e: e.tensor_tensor(rows[:, 2, 0:tw], rows[:, 1, 0:tw], rows[:, 2, 0:tw], ALU.subtract))
            c_ve = sch.op("vector", lambda e: e.tensor_scalar(rows[:, 2, 0:tw], rows[:, 2, 0:tw], EPS, None, ALU.add),
                          inc="ve")
            sch.wait("scalar", "ve", c_ve)
            c_ac = sch.op("scalar", lambda e: e.activation(rows[:, 1, 0:tw], rows[:, 2, 0:tw], AF.Sqrt),
                          inc="ac")
            sch.wait("vector", "ac", c_ac)
            c_ve = sch.op("vector", lambda e: e.reciprocal(rows[:, 2, 0:tw], rows[:, 1, 0:tw]), inc="ve")
            yield
            sch.wait("tensor", "ve", c_ve)
            for (eng, sname, cnt) in bc_guards:
                sch.wait(eng, sname, cnt)
            sch.op("tensor", lambda e, bc0=bc0: e.matmul(bc0, ones_r[:], rows[:, 0, 0:tw], start=True, stop=True))
            bc_done = sch.op("tensor", lambda e, bc1=bc1: e.matmul(bc1, ones_r[:], rows[:, 2, 0:tw], start=True, stop=True),
                             inc="pe")
            sch.wait("vector", "pe", bc_done)
            for (eng, sname, cnt) in norm_guards:
                sch.wait(eng, sname, cnt)
            c_n = 0
            for ch in range(KCH):
                sch.op("vector", lambda e, ch=ch, bc0=bc0: e.tensor_tensor(
                    lntmp[:, 0:tw], xt[:, ch, t0:t0 + tw], bc0, ALU.subtract))
                c_n = sch.op("vector", lambda e, ch=ch, bc1=bc1: e.tensor_tensor(
                    y[:, ch, ydst_off:ydst_off + tw], lntmp[:, 0:tw], bc1, ALU.mult),
                    inc="ve" if ch == KCH - 1 else None)
                if ch % 4 == 3:
                    yield
            ln_chain[0] = c_n
            ln_chain[1] = max(stats_done, bc_done)
            res["ln_done"] = c_n

        def emit_ln(*a, **k):
            res = {}
            for _ in g_ln(*a, res=res, **k):
                pass
            return res["ln_done"]

        app_marks = []

        def emit_ffn_app(t0, first=False, dq_hook=None, att_hook=None,
                         ln_guards=None, prev=None, ln_done=None):
            if ln_done is None:
                lg = ln_guards or {}
                ln_done = emit_ln(
                    t0, HW, t0,
                    banks=(P[2][0:1, :], P[3][0:1, :], P[0][:, :], P[1][:, :]),
                    stats_guards=lg.get("stats", ()),
                    bc_guards=lg.get("bc", ()),
                    norm_guards=lg.get("norm", ()))

            wload = {}
            pe_grp = {}
            gelu_at = {}
            inter_at = {}
            evac_at = {}
            down_pe_done = {}

            for nb in range(NBLK):
                for mc in range(16):
                    gidx = nb * 16 + mc
                    if dq_hook is not None:
                        dq_hook(gidx)
                    if att_hook is not None:
                        att_hook()
                    slot = gidx % 2
                    if first and nb == 0:
                        sch.wait("sync", "cc", cc_at[("u", mc // 4)])
                    for m, qm in (("g", "sync"), ("u", "sync")):
                        if gidx >= 2:
                            sch.wait(qm, "pe", pe_grp[gidx - 2])
                        elif prev is not None:
                            sch.wait(qm, "pe", prev["gu_pe"])
                        wload[(m, gidx)] = sch.op(qm, lambda e, m=m, nb=nb, mc=mc, slot=slot: e.dma_start(
                            out=wr[m][:, slot], in_=w_out[m][mc // 4, nb, mc % 4]), inc="w%s%d" % (m, slot), n=16)
                    sch.wait("tensor", "wg%d" % slot, wload[("g", gidx)])
                    sch.wait("tensor", "wu%d" % slot, wload[("u", gidx)])
                    if gidx == 0:
                        sch.wait("tensor", "ve", ln_done)
                        if prev is not None:
                            sch.wait("tensor", "ac", prev["gelu_ac"])
                            sch.wait("tensor", "ve", prev["inter_ve"])
                    if gidx >= 2:
                        sch.wait("tensor", "ac", gelu_at[gidx - 2])
                        sch.wait("tensor", "ve", inter_at[gidx - 2])
                    for k in range(KCH):
                        sch.op("tensor", lambda e, slot=slot, k=k: e.matmul(
                            P[slot][:, :], wr["g"][:, slot, k, :], y[:, k, t0:t0 + HW],
                            start=(k == 0), stop=(k == KCH - 1)))
                    grp_pe = 0
                    for k in range(KCH):
                        grp_pe = sch.op("tensor", lambda e, slot=slot, k=k: e.matmul(
                            P[2 + slot][:, :], wr["u"][:, slot, k, :], y[:, k, t0:t0 + HW],
                            start=(k == 0), stop=(k == KCH - 1)),
                            inc="pe" if k == KCH - 1 else None)
                    pe_grp[gidx] = grp_pe
                    sch.wait("scalar", "pe", grp_pe)
                    if gidx >= 1:
                        sch.wait("scalar", "ve", inter_at[gidx - 1])
                    elif prev is not None:
                        sch.wait("scalar", "ve", prev["inter_ve"])
                    gelu_at[gidx] = sch.op("scalar", lambda e, slot=slot: e.activation(
                        g1[:, 0, :], P[slot][:, :], AF.Gelu_apprx_tanh), inc="ac")
                    sch.wait("vector", "ac", gelu_at[gidx])
                    if nb >= 1 and mc == 0:
                        sch.wait("vector", "pe", down_pe_done[nb - 1])
                    elif nb == 0 and mc == 0 and prev is not None:
                        sch.wait("vector", "pe", prev["down_pe"])
                    inter_at[gidx] = sch.op("vector", lambda e, mc=mc, slot=slot: e.tensor_tensor(
                        shI[:, mc, :], g1[:, 0, :], P[2 + slot][:, :], ALU.mult),
                        inc="ve")
                for hc in range(16):
                    hgidx = nb * 16 + hc
                    slot = hgidx % 2
                    if att_hook is not None:
                        att_hook()
                    if first and nb == 0:
                        sch.wait("sync", "cc", cc_at[("d", hc // 4)])
                    if hgidx >= 2:
                        sch.wait("sync", "pe", pe_grp[("d", hgidx - 2)])
                    elif prev is not None:
                        sch.wait("sync", "pe", prev["down_pe"])
                    wload[("d", hgidx)] = sch.op("sync", lambda e, nb=nb, hc=hc, slot=slot: e.dma_start(
                        out=wr["d"][:, slot], in_=w_out["d"][hc // 4, nb, hc % 4]), inc="wd%d" % slot, n=16)
                    bank = 4 + (hc % 2)
                    sch.wait("tensor", "wd%d" % slot, wload[("d", hgidx)])
                    if hc == 0:
                        sch.wait("tensor", "ve", inter_at[nb * 16 + 15])
                    if hgidx >= 2:
                        sch.wait("tensor", "ve", evac_at[hgidx - 2])
                    grp_pe = 0
                    for k in range(KCH):
                        grp_pe = sch.op("tensor", lambda e, slot=slot, k=k, bank=bank: e.matmul(
                            P[bank][:, :], wr["d"][:, slot, k, :], shI[:, k, :],
                            start=(k == 0), stop=(k == KCH - 1)),
                            inc="pe" if k == KCH - 1 else None)
                    pe_grp[("d", hgidx)] = grp_pe
                    sch.wait("vector", "pe", grp_pe)
                    evac_at[hgidx] = sch.op("vector", lambda e, hc=hc, bank=bank, t0=t0: e.scalar_tensor_tensor(
                        xt[:, hc, t0:t0 + HW], P[bank][:, :], 0.5,
                        xt[:, hc, t0:t0 + HW], ALU.mult, ALU.add),
                        inc="ve")
                down_pe_done[nb] = pe_grp[("d", nb * 16 + 15)]

            app_marks.append(dict(
                gu_pe=pe_grp[NBLK * 16 - 1],
                down_ve=evac_at[NBLK * 16 - 1],
                gelu_ac=gelu_at[NBLK * 16 - 1],
                inter_ve=inter_at[NBLK * 16 - 1],
                down_pe=down_pe_done[NBLK - 1],
            ))

        sch.op("sync", lambda e: e.dma_start(out=xt[:], in_=xT_d.rearrange("(c p) t -> p c t", p=128)),
               inc="pre", n=16)
        sch.op("sync", lambda e: e.dma_start(out=biases[:], in_=bias_d[:]), inc="pre", n=16)
        sch.op("sync", lambda e: e.dma_start(out=rotm[:], in_=rotm_d[:]), inc="pre", n=16)
        sch.op("sync", lambda e: e.dma_start(out=eyeb[:], in_=eyeb_d[:]), inc="pre", n=16)
        PRE_ALL = sch.val("pre")
        pre_total[0] = PRE_ALL
        sch.op("vector", lambda e: e.memset(ones_c[:], 1.0))
        sch.op("vector", lambda e: e.memset(ones_r[:], 1.0), inc="ve")

        with ExitStack() as c0:
            pk_t = c0.enter_context(nc.sbuf_tensor("pk", [128, 2, 1024], I32))
            sc_t = c0.enter_context(nc.sbuf_tensor("sct", [128, 2, 64], F32))
            v0l = c0.enter_context(nc.sbuf_tensor("v0l", [128, 1024], I32))
            sl_t = c0.enter_context(nc.sbuf_tensor("sl", [128, 1024], I32))
            v8_t = c0.enter_context(nc.sbuf_tensor("v8", [128, 1024], F32))
            nat = c0.enter_context(nc.sbuf_tensor("nat", [128, 2, 2048], BF16))
            stage = c0.enter_context(nc.sbuf_tensor("stage", [128, 2, 16, 128], BF16))

            dq_done = {}
            tp_done = {}
            pt_evac = {}
            stage_dma = {}
            dq_hist = []

            def emit_dq_chunk(m, oc, gl):
                dq_hist.append(gl)
                slot = (len(dq_hist) - 1) % 2
                qeng = "gpsimd"
                if len(dq_hist) >= 3:
                    sch.wait(qeng, "ve", dq_done[dq_hist[-3]])
                pksem = "pk%d" % slot
                sch.op(qeng, lambda e, m=m, oc=oc, slot=slot: e.dma_start(
                    out=pk_t[:, slot, :],
                    in_=pk_d[m].rearrange("(c p) j -> c p j", p=128)[oc]), inc=pksem, n=16)
                c_ld = sch.op(qeng, lambda e, m=m, oc=oc, slot=slot: e.dma_start(
                    out=sc_t[:, slot, :],
                    in_=sc_d[m].rearrange("(c p) j -> c p j", p=128)[oc]), inc=pksem, n=16)
                sch.wait("vector", pksem, c_ld)
                if len(dq_hist) >= 3:
                    sch.wait("vector", "pe", tp_done[dq_hist[-3]])
                pkap = pk_t[:, slot, :]
                sc_b = sc_t.rearrange("p s (b one) -> p s b one", one=1)[:, slot].broadcast_to([128, 64, 16])
                c_dq = 0
                for par in (0, 1):
                    if par == 0:
                        sch.op("vector", lambda e, pkap=pkap: e.tensor_scalar(v0l[:], pkap, 7, 22, ALU.bitwise_and, ALU.logical_shift_left))
                        sch.op("vector", lambda e, pkap=pkap: e.tensor_scalar(sl_t[:], pkap, 8, 28, ALU.bitwise_and, ALU.logical_shift_left))
                    else:
                        sch.op("vector", lambda e, pkap=pkap: e.tensor_scalar(v0l[:], pkap, 4, None, ALU.logical_shift_right))
                        sch.op("vector", lambda e: e.tensor_scalar(v0l[:], v0l[:], 7, 22, ALU.bitwise_and, ALU.logical_shift_left))
                        sch.op("vector", lambda e, pkap=pkap: e.tensor_scalar(sl_t[:], pkap, 128, 24, ALU.bitwise_and, ALU.logical_shift_left))
                    sch.op("vector", lambda e: e.tensor_scalar(v0l[:], v0l[:], C_EXP, None, ALU.add))
                    v0f = v0l.bitcast(F32)
                    sch.op("vector", lambda e, v0f=v0f: e.tensor_scalar(v8_t[:], v0f[:], 1.0, 0.0, ALU.subtract, ALU.min))
                    sch.op("vector", lambda e, v0f=v0f: e.tensor_add(v8_t[:], v0f[:], v8_t[:]))
                    sch.op("vector", lambda e: e.tensor_tensor(v8_t.bitcast(I32)[:], v8_t.bitcast(I32)[:], sl_t[:], ALU.bitwise_xor))
                    natv = nat.rearrange("p s (b g two) -> p s b g two", two=2, g=16)[:, slot, :, :, par]
                    c_dq = sch.op("vector", lambda e, natv=natv, sc_b=sc_b: e.tensor_tensor(
                        natv, v8_t.rearrange("p (b g) -> p b g", g=16), sc_b, ALU.mult),
                        inc="ve" if par == 1 else None)
                dq_done[gl] = c_dq
                sch.wait("tensor", "ve", c_dq)
                if len(dq_hist) == 1:
                    sch.wait("tensor", "pre", PRE_ALL)
                for grp in range(2):
                    bank = grp
                    if len(dq_hist) >= 2:
                        sch.wait("tensor", "ac", pt_evac[(dq_hist[-2], grp)])
                    c_tp = 0
                    for kk in range(8):
                        k = grp * 8 + kk
                        c_tp = sch.op("tensor", lambda e, slot=slot, k=k, kk=kk, bank=bank: e.transpose(
                            PT[bank][:, kk * 128:(kk + 1) * 128],
                            nat[:, slot, k * 128:(k + 1) * 128], eyeb[:]),
                            inc="pe" if kk == 7 else None)
                    sch.wait("scalar", "pe", c_tp)
                    if len(dq_hist) >= 3:
                        sch.wait("scalar", "st%d" % slot, stage_dma[dq_hist[-3]])
                    pt_evac[(gl, grp)] = sch.op("scalar", lambda e, slot=slot, grp=grp, bank=bank: e.copy(
                        stage[:, slot, grp * 8:(grp + 1) * 8, :],
                        PT[bank].rearrange("p (g q) -> p g q", q=128)),
                        inc="ac")
                    if grp == 1:
                        tp_done[gl] = c_tp
                sch.wait("scalar", "ac", pt_evac[(gl, 1)])
                stage_dma[gl] = sch.op("scalar", lambda e, m=m, oc=oc, slot=slot: e.dma_start(
                    out=w_in[m][oc], in_=stage[:, slot]), inc="st%d" % slot, n=16)

            def emit_cc(m, q):
                sch.wait("gpsimd", "st0", sch.val("st0"))
                sch.wait("gpsimd", "st1", sch.val("st1"))
                rws = 4 * 128 * 16
                c = sch.op("gpsimd", lambda e, m=m, q=q, rws=rws: e.collective_compute(
                    "AllGather", ALU.bypass,
                    replica_groups=[list(range(NCORES))],
                    ins=[w_in[m].rearrange("a p k q -> (a p k) q")[q * rws:(q + 1) * rws, :]],
                    outs=[w_out[m][q].rearrange("n a p k q -> (n a p k) q")]),
                    inc="cc")
                assert c == cc_at[(m, q)], (m, q, c, cc_at[(m, q)])

            for m in ("g", "u"):
                for oc in range(4):
                    emit_dq_chunk(m, oc, (0 if m == "g" else 16) + oc)
                emit_cc(m, 0)

            rest = []
            for q in range(1, 4):
                rest += [("g", oc) for oc in range(4 * q, 4 * q + 4)]
                rest += [("u", oc) for oc in range(4 * q, 4 * q + 4)]
            rest += [("d", oc) for oc in range(16)]

            CPH = 3

            def dq_hook(gidx):
                for j in range(CPH * gidx, min(CPH * (gidx + 1), len(rest))):
                    m, oc = rest[j]
                    base = {"g": 0, "u": 16, "d": 32}[m]
                    emit_dq_chunk(m, oc, base + oc)
                    if oc % 4 == 3:
                        emit_cc(m, oc // 4)

            f1b_ln = {}

            def f1b_ln_gen():
                sg = [("tensor", "ac", ctr.get("ac", 0))]
                yield from g_ln(
                    512, HW, 512,
                    banks=(PTf[0][0:1, :], PTf[1][0:1, :],
                           PTf[0][:, :], PTf[1][:, :]),
                    res=f1b_ln, stats_guards=sg)

            def hook_after(n, gen):
                cnt = [0]

                def hook():
                    cnt[0] += 1
                    if cnt[0] > n:
                        try:
                            next(gen)
                        except StopIteration:
                            pass
                return hook

            f1b_gen = f1b_ln_gen()
            emit_ffn_app(0, first=True, dq_hook=dq_hook,
                         att_hook=hook_after(200, f1b_gen))
            for _ in f1b_gen:
                pass
            DQ_ST0 = sch.val("st0")
            DQ_ST1 = sch.val("st1")
        for n in aw:
            sch.op("gpsimd", lambda e, n=n: e.dma_start(out=aw_bf[n][:], in_=aw[n][:]), inc="gd", n=16)
        AWBF_ALL = sch.val("gd")

        with ExitStack() as c2:
            attscr = c2.enter_context(nc.sbuf_tensor("attscr", [128, 6656], F32))
            cqb = c2.enter_context(nc.sbuf_tensor("cqb", [128, 4, TQ], BF16))
            ckvb = c2.enter_context(nc.sbuf_tensor("ckvb", [128, 4, TQ], BF16))
            qcb = c2.enter_context(nc.sbuf_tensor("qcb", [128, 16, TQ], BF16))
            kcb = c2.enter_context(nc.sbuf_tensor("kcb", [128, 16, TQ], BF16))
            qrb = c2.enter_context(nc.sbuf_tensor("qrb", [128, 8, TQ], BF16))
            krb = c2.enter_context(nc.sbuf_tensor("krb", [128, 8, TQ], BF16))
            wra = c2.enter_context(nc.sbuf_tensor("wra", [128, 1, 16, 128], BF16))

            attF = attscr
            attB = attscr.bitcast(BF16)
            prod = attB[:, 0:3072].rearrange("p (j d) -> p j d", d=192)
            prod2 = attB[:, 0:2048].rearrange("p (d j) -> p d j", j=16)
            q_t = attB[:, 3072:6144]
            k_t = attB[:, 6144:9216]
            v_t = attB[:, 9216:11264]
            o_t = attB[:, 11264:13312]
            es256 = attF[:, 5632:5888]
            rope = attB[:, 12032:12544]
            sc256 = attF[:, 6272:6528]
            at256 = attB[:, 13056:13312]
            rtm1 = lntmp[:, 0:TQ]
            rtm2 = lntmp[:, TQ:2 * TQ]
            q192 = q_t.rearrange("p (i d) -> p i d", d=192)
            k192 = k_t.rearrange("p (i d) -> p i d", d=192)

            def PB(h):
                return PTf[1][:, h * TQ:(h + 1) * TQ]

            ptst = {(b, h): [("ve", f1b_ln["ln_done"])] for b in (0, 1) for h in (0, 1)}

            def pt_guard(eng, bank, halves):
                seen = set()
                for h in halves:
                    for mk in ptst[(bank, h)]:
                        if mk not in seen:
                            seen.add(mk)
                            sch.wait(eng, mk[0], mk[1])

            def pt_setcons(bank, halves, marks):
                for h in halves:
                    ptst[(bank, h)] = list(marks)

            proj_specs = [
                ("cq", "Wc", "bc", "y", 16, 4),
                ("ckv", "Wckv", "bckv", "y", 16, 4),
                ("qc", "Wcq", "bcq", "cq", 4, 16),
                ("qr", "Wqr", "bqr", "cq", 4, 8),
                ("kc", "Wck", "bck", "ckv", 4, 16),
                ("kr", "Wkr", "bkr", "y", 16, 8),
                ("v", "Wv", "bv", "ckv", 4, 16),
            ]

            wa_hist = []
            ast = dict(
                rope_ve=app_marks[0]["down_ve"], ot_pe=app_marks[0]["down_pe"],
                buf_pe={},
                yslot={},
                av_ve=0,
                pb_free={0: [("ve", f1b_ln["ln_done"])],
                         1: [("ve", f1b_ln["ln_done"])]},
                wa_done=False,
                last=dict(ve=0, ac=0, pe=0),
                s3_ve={},
            )

            def wa_load(fn):
                slot = 0
                if not ast["wa_done"]:
                    sch.wait("gpsimd", "gd", AWBF_ALL)
                    ast["wa_done"] = True
                if len(wa_hist) >= 1:
                    sch.wait("gpsimd", "pe", wa_hist[-1][1])
                c_w = sch.op("gpsimd", lambda e, fn=fn, slot=slot: fn(e, slot), inc="wa0", n=16)
                ent = [c_w, None]
                wa_hist.append(ent)
                return slot, c_w, ent

            def att_half(half, xt_ready_ve, y_free_pe):
                abase = half * 512
                ao_ac = 0
                if half == 0:
                    sch.wait("vector", "st0", DQ_ST0)
                    sch.wait("vector", "st1", DQ_ST1)
                for qq in (0, 1):
                    q = 2 * half + qq
                    t0q = q * TQ
                    ys = qq
                    lnoff = abase + ys * TQ
                    voff = abase + (1 - ys) * TQ
                    sch.wait("gpsimd", "ve", ast["rope_ve"])
                    sch.wait("gpsimd", "pe", ast["ot_pe"])
                    c_rl = sch.op("gpsimd", lambda e, t0q=t0q: e.dma_start(
                        out=rope.rearrange("p (c t) -> p c t", c=2),
                        in_=rope_d.rearrange("p (c t) -> p c t", c=2)[:, :, t0q:t0q + TQ]), inc="pre", n=16)
                    yield
                    sg = [("scalar", "ve", xt_ready_ve), ("tensor", "ve", xt_ready_ve)]
                    for (s, c) in ptst[(0, 0)] + ptst[(0, 1)]:
                        sg.append(("tensor", s, c))
                    bg = [("tensor", s, c) for (s, c) in
                          ast["pb_free"][0] + ast["pb_free"][1]]
                    ng = [("vector",) + mk for mk in
                          ast["yslot"].get((half, ys), [("pe", y_free_pe)])]
                    ln_res = {}
                    yield from g_ln(
                        t0q, TQ, lnoff,
                        banks=(PTf[0][0:1, 0:TQ], PTf[0][0:1, TQ:2 * TQ],
                               PB(0), PB(1)),
                        res=ln_res,
                        stats_guards=sg, bc_guards=bg, norm_guards=ng)
                    ln_done = ln_res["ln_done"]
                    pt_setcons(0, (0, 1), [("ve", ln_done)])
                    ast["pb_free"] = {0: [("ve", ln_done)], 1: [("ve", ln_done)]}
                    yield
                    dst_ready = {}
                    pb_i = [0]

                    def rhs_of(src, k, lnoff=lnoff):
                        if src == "y":
                            return y[:, k, lnoff:lnoff + TQ]
                        if src == "cq":
                            return cqb[:, k, :]
                        return ckvb[:, k, :]

                    for (pname, wname, bname, ksrc, kch, mch) in proj_specs:
                        for m in range(mch):
                            slot, c_w, ent = wa_load(
                                lambda e, slot, wname=wname, kch=kch, m=m: e.dma_start(
                                    out=wra[:, slot, 0:kch, :],
                                    in_=aw_bf[wname].rearrange("(k p) n -> p k n", p=128)[:, :, m * 128:(m + 1) * 128]))
                            h = pb_i[0] % 2
                            pb_i[0] += 1
                            bank_ap = PB(h)
                            sch.wait("tensor", "wa%d" % slot, c_w)
                            if pname == "cq" and m == 0:
                                sch.wait("tensor", "ve", ln_done)
                            for mk in ast["pb_free"][h]:
                                sch.wait("tensor", mk[0], mk[1])
                            if pname in ("qc", "qr") and m == 0:
                                sch.wait("tensor", "ve", dst_ready[("cq", 3)])
                            if pname in ("kc", "v") and m == 0:
                                sch.wait("tensor", "ve", dst_ready[("ckv", 3)])
                            grp = 0
                            for k in range(kch):
                                grp = sch.op("tensor", lambda e, slot=slot, k=k, ksrc=ksrc, bank_ap=bank_ap, kch=kch: e.matmul(
                                    bank_ap, wra[:, slot, k, :], rhs_of(ksrc, k),
                                    start=(k == 0), stop=(k == kch - 1)),
                                    inc="pe" if k == kch - 1 else None)
                            ent[1] = grp
                            sch.wait("vector", "pe", grp)
                            if pname in ("qr", "kr"):
                                qr0 = es256
                                sch.wait("vector", "pe", ast["ot_pe"])
                                c1v = sch.op("vector", lambda e, bank_ap=bank_ap, bname=bname, m=m, qr0=qr0: e.tensor_scalar(
                                    qr0, bank_ap, bias_ap(bname, m), None, ALU.add), inc="ve")
                                ast["pb_free"][h] = [("ve", c1v)]
                                sch.wait("tensor", "ve", c1v)
                                pt_guard("tensor", 0, (0,))
                                c_rot = sch.op("tensor", lambda e, qr0=qr0: e.matmul(
                                    PTf[0][:, 0:TQ], rotm[:], qr0, start=True, stop=True), inc="pe")
                                sch.wait("vector", "pe", c_rot)
                                sch.wait("vector", "pre", c_rl)
                                cosap = rope[:, 0:TQ]
                                sinap = rope[:, TQ:2 * TQ]
                                sch.op("vector", lambda e, cosap=cosap, qr0=qr0: e.tensor_tensor(rtm1[:], qr0, cosap, ALU.mult))
                                sch.op("vector", lambda e, sinap=sinap: e.tensor_tensor(rtm2[:], PTf[0][:, 0:TQ], sinap, ALU.mult))
                                dstap = (qrb if pname == "qr" else krb)[:, m, :]
                                for mk in ast["buf_pe"].get(pname, []):
                                    sch.wait("vector", mk[0], mk[1])
                                c2v = sch.op("vector", lambda e, dstap=dstap: e.tensor_add(dstap, rtm1[:], rtm2[:]), inc="ve")
                                ast["rope_ve"] = c2v
                                pt_setcons(0, (0,), [("ve", c2v)])
                                dst_ready[(pname, m)] = c2v
                            else:
                                if pname == "v":
                                    dstap = y[:, m, voff:voff + TQ]
                                    for mk in ast["yslot"].get((half, 1 - ys), [("pe", y_free_pe)]):
                                        sch.wait("vector", mk[0], mk[1])
                                else:
                                    bufs = {"cq": cqb, "ckv": ckvb, "qc": qcb, "kc": kcb}
                                    dstap = bufs[pname][:, m, :]
                                    for mk in ast["buf_pe"].get(pname, []):
                                        sch.wait("vector", mk[0], mk[1])
                                c1v = sch.op("vector", lambda e, dstap=dstap, bank_ap=bank_ap, bname=bname, m=m: e.tensor_scalar(
                                    dstap, bank_ap, bias_ap(bname, m), None, ALU.add), inc="ve")
                                ast["pb_free"][h] = [("ve", c1v)]
                                dst_ready[(pname, m)] = c1v
                            yield
                    proj_all_ve = sch.val("ve")
                    proj_all_pe = sch.val("pe")
                    ast["yslot"][(half, ys)] = [("pe", proj_all_pe)]
                    ast["buf_pe"]["cq"] = [("pe", proj_all_pe)]
                    ast["buf_pe"]["ckv"] = [("pe", proj_all_pe)]
                    yield
                    for sti in range(TQ // 128):
                        s0 = sti * 128
                        gi = [0]

                        def tgroup(srcs, dst_builder, split4=False):
                            bank = gi[0] % 2
                            gi[0] += 1
                            sch.wait("tensor", "ve", proj_all_ve)
                            pt_guard("tensor", bank, (0, 1))
                            c_tp = 0
                            for idx, (srcap,) in enumerate(srcs):
                                c_tp = sch.op("tensor", lambda e, srcap=srcap, idx=idx, bank=bank: e.transpose(
                                    PT[bank][:, idx * 128:(idx + 1) * 128], srcap, eyeb[:]),
                                    inc="pe" if idx == 7 else None)
                            sch.wait("scalar", "pe", c_tp)
                            if ast["av_ve"]:
                                sch.wait("scalar", "ve", ast["av_ve"])
                            if split4:
                                src_ap = PT[bank].rearrange("p (g two d) -> p g two d", two=2, d=64)
                            else:
                                src_ap = PT[bank].rearrange("p (g q) -> p g q", q=128)
                            c = sch.op("scalar", lambda e, b=dst_builder, s=src_ap: e.copy(b(), s), inc="ac")
                            pt_setcons(bank, (0, 1), [("ac", c)])
                            return c

                        marks = []
                        for hh in range(2):
                            marks.append(tgroup([(qcb[:, hh * 8 + j, s0:s0 + 128],) for j in range(8)],
                                                lambda hh=hh: q192[:, hh * 8:(hh + 1) * 8, 0:128]))
                            yield
                        marks.append(tgroup([(qrb[:, j, s0:s0 + 128],) for j in range(8)],
                                            lambda: q192[:, :, 128:192].rearrange("p (a two) d -> p a two d", two=2),
                                            split4=True))
                        yield
                        for hh in range(2):
                            marks.append(tgroup([(kcb[:, hh * 8 + j, s0:s0 + 128],) for j in range(8)],
                                                lambda hh=hh: k192[:, hh * 8:(hh + 1) * 8, 0:128]))
                            yield
                        marks.append(tgroup([(krb[:, j, s0:s0 + 128],) for j in range(8)],
                                            lambda: k192[:, :, 128:192].rearrange("p (a two) d -> p a two d", two=2),
                                            split4=True))
                        yield
                        for hh in range(2):
                            marks.append(tgroup([(y[:, hh * 8 + j, voff + s0:voff + s0 + 128],) for j in range(8)],
                                                lambda hh=hh: v_t.rearrange("p (j d) -> p j d", d=128)[:, hh * 8:(hh + 1) * 8, :]))
                            yield
                        tq_pe = sch.val("pe")
                        for c in marks:
                            sch.wait("vector", "ac", c)
                        sch.wait("vector", "pe", ast["ot_pe"])
                        c_sc = 0
                        for i in range(H):
                            qi = q192[:, i:i + 1, :].broadcast_to([128, H, 192])
                            sch.op("vector", lambda e, qi=qi: e.tensor_tensor(prod, qi, k192, ALU.mult))
                            c_sc = sch.op("vector", lambda e, i=i: e.tensor_reduce(
                                sc256[:, i * 16:(i + 1) * 16], prod, AX.X, ALU.add),
                                inc="ve" if i == H - 1 else None)
                        yield
                        sch.wait("vector", "ve", c_sc)
                        sc3 = sc256.rearrange("p (i j) -> p i j", j=16)
                        c_r1 = sch.op("vector", lambda e, sc3=sc3: e.tensor_reduce(nm16[:], sc3, AX.X, ALU.max, negate=True), inc="ve")
                        sch.wait("vector", "ve", c_r1)
                        c_sm = sch.op("vector", lambda e, sc3=sc3: e.tensor_tensor(
                            sc3, sc3,
                            nm16.rearrange("p (i one) -> p i one", one=1).broadcast_to([128, 16, 16]),
                            ALU.add), inc="ve")
                        sch.wait("scalar", "ve", c_sm)
                        sch.wait("scalar", "pe", ast["ot_pe"])
                        c_exp = sch.op("scalar", lambda e: e.activation(es256[:], sc256[:], AF.Exp, scale=SCALER), inc="ac")
                        sch.wait("vector", "ac", c_exp)
                        es3 = es256.rearrange("p (i j) -> p i j", j=16)
                        c_r2 = sch.op("vector", lambda e, es3=es3: e.tensor_reduce(rs16[:], es3, AX.X, ALU.add), inc="ve")
                        sch.wait("vector", "ve", c_r2)
                        c_r3 = sch.op("vector", lambda e: e.reciprocal(rs16b[:], rs16[:]), inc="ve")
                        sch.wait("vector", "ve", c_r3)
                        at3 = at256.rearrange("p (i j) -> p i j", j=16)
                        sch.op("vector", lambda e, es3=es3, at3=at3: e.tensor_tensor(
                            at3, es3,
                            rs16b.rearrange("p (i one) -> p i one", one=1).broadcast_to([128, 16, 16]),
                            ALU.mult))
                        yield
                        vv = v_t.rearrange("p (j d) -> p d j", d=128)

                        def av_reduce(e, i):
                            with nc.allow_low_precision(reason="attn@v: fp32 accum, bf16 out"):
                                return e.tensor_reduce(o_t[:, i * 128:(i + 1) * 128], prod2, AX.X, ALU.add)

                        c_av = 0
                        for i in range(H):
                            ai = at3[:, i:i + 1, :].broadcast_to([128, 128, 16])
                            sch.op("vector", lambda e, ai=ai, vv=vv: e.tensor_tensor(prod2, ai, vv, ALU.mult))
                            c_av = sch.op("vector", lambda e, i=i: av_reduce(e, i),
                                          inc="ve" if i == H - 1 else None)
                        ast["av_ve"] = c_av
                        ast["yslot"][(half, 1 - ys)] = [("ve", c_av), ("pe", tq_pe)]
                        yield
                        sch.wait("tensor", "ve", c_av)
                        last_tp = 0
                        for hh in range(2):
                            bank = hh % 2
                            pt_guard("tensor", bank, (0, 1))
                            c_tp = 0
                            for idx in range(8):
                                mch = hh * 8 + idx
                                c_tp = sch.op("tensor", lambda e, mch=mch, idx=idx, bank=bank: e.transpose(
                                    PT[bank][:, idx * 128:(idx + 1) * 128],
                                    o_t[:, mch * 128:(mch + 1) * 128], eyeb[:]),
                                    inc="pe" if idx == 7 else None)
                            last_tp = c_tp
                            sch.wait("scalar", "pe", c_tp)
                            sch.wait("scalar", "ve", c_av)
                            dstb = (lambda hh=hh, s0=s0, voff=voff: y[:, hh * 8:(hh + 1) * 8, voff + s0:voff + s0 + 128])
                            src_ap = PT[bank].rearrange("p (g q) -> p g q", q=128)
                            c = sch.op("scalar", lambda e, dstb=dstb, s=src_ap: e.copy(dstb(), s), inc="ac")
                            pt_setcons(bank, (0, 1), [("ac", c)])
                            yield
                        ast["ot_pe"] = last_tp
                        ao_ac = ctr.get("ac", 0)
                    cur_pe = sch.val("pe")
                    for nmq in ("qc", "kc", "qr", "kr"):
                        ast["buf_pe"][nmq] = [("pe", cur_pe)]
                    sch.wait("tensor", "ac", ao_ac)
                    s3_last_ve = 0
                    for m in range(16):
                        slot, c_w, ent = wa_load(
                            lambda e, slot, m=m: e.dma_start(
                                out=wra[:, slot, :, :],
                                in_=aw_bf["Wo"].rearrange("(k p) n -> p k n", p=128)[:, :, m * 128:(m + 1) * 128]))
                        h = m % 2
                        bank_ap = PB(h)
                        sch.wait("tensor", "wa%d" % slot, c_w)
                        for mk in ast["pb_free"][h]:
                            sch.wait("tensor", mk[0], mk[1])
                        grp = 0
                        for k in range(KCH):
                            grp = sch.op("tensor", lambda e, slot=slot, k=k, bank_ap=bank_ap, voff=voff: e.matmul(
                                bank_ap, wra[:, slot, k, :], y[:, k, voff:voff + TQ],
                                start=(k == 0), stop=(k == KCH - 1)),
                                inc="pe" if k == KCH - 1 else None)
                        ent[1] = grp
                        sch.wait("vector", "pe", grp)
                        s3_last_ve = sch.op("vector", lambda e, m=m, bank_ap=bank_ap, t0q=t0q: e.scalar_tensor_tensor(
                            xt[:, m, t0q:t0q + TQ], bank_ap, bias_ap("bo", m),
                            xt[:, m, t0q:t0q + TQ], ALU.add, ALU.add), inc="ve")
                        ast["pb_free"][h] = [("ve", s3_last_ve)]
                        yield
                    ast["yslot"][(half, 1 - ys)] = [("pe", sch.val("pe"))]
                    ast["s3_ve"][q] = s3_last_ve
                ast["last"] = dict(ve=sch.val("ve"), ac=sch.val("ac"), pe=sch.val("pe"))

            RATE = 1.0

            def mk_hook(gen):
                state = [0.0, 0]

                def hook():
                    state[0] += RATE
                    while state[1] < int(state[0]):
                        state[1] += 1
                        try:
                            next(gen)
                        except StopIteration:
                            return
                return hook

            def drain(gen):
                for _ in gen:
                    pass

            f2a_ln = {}
            f2b_ln = {}

            def next_ln_gen(t0n, res):
                q_last = 1 if t0n == 0 else 3
                s3v = ast["s3_ve"][q_last]
                sg = [("scalar", "ve", s3v), ("tensor", "ve", s3v)]
                for (b, h) in ptst:
                    for mk in ptst[(b, h)]:
                        sg.append(("tensor", mk[0], mk[1]))
                for h in (0, 1):
                    for mk in ast["pb_free"][h]:
                        sg.append(("tensor", mk[0], mk[1]))
                ng = [("vector", "pe", ast["last"]["pe"]),
                      ("vector", "ve", ast["last"]["ve"])]
                yield from g_ln(
                    t0n, HW, t0n,
                    banks=(PTf[0][0:1, :], PTf[1][0:1, :],
                           PTf[0][:, :], PTf[1][:, :]),
                    res=res, stats_guards=sg, norm_guards=ng)
                pt_setcons(0, (0, 1), [("ve", res["ln_done"])])
                pt_setcons(1, (0, 1), [("ve", res["ln_done"])])
                ast["pb_free"] = {0: [("ve", res["ln_done"])],
                                  1: [("ve", res["ln_done"])]}

            import itertools

            genA = itertools.chain(
                att_half(0, xt_ready_ve=app_marks[0]["down_ve"],
                         y_free_pe=app_marks[0]["gu_pe"]),
                next_ln_gen(0, f2a_ln))
            emit_ffn_app(512, att_hook=mk_hook(genA), prev=app_marks[0],
                         ln_done=f1b_ln["ln_done"])
            drain(genA)
            ATa_done = dict(ast["last"])

            genB = itertools.chain(
                att_half(1, xt_ready_ve=app_marks[1]["down_ve"],
                         y_free_pe=app_marks[1]["gu_pe"]),
                next_ln_gen(512, f2b_ln))
            emit_ffn_app(0, att_hook=mk_hook(genB), prev=app_marks[1],
                         ln_done=f2a_ln["ln_done"])
            drain(genB)
            ATb_done = dict(ast["last"])

        with ExitStack() as c4:
            ost = c4.enter_context(nc.sbuf_tensor("ost", [128, 2, D], F32))
            eyef = c4.enter_context(nc.sbuf_tensor("eyef_sb", [128, 128], F32))
            sch.op("sync", lambda e: e.dma_start(out=eyef[:], in_=eyef_d[:]), inc="pre", n=16)

            ost_dma = {}
            pt_free = {}
            out_guarded = [False]

            def emit_out_st8(st8, xt_ve):
                slot = st8 % 2
                last_ev = 0
                for grp4 in range(4):
                    bank = grp4 % 2
                    if not out_guarded[0]:
                        sch.wait("tensor", "pre", sch.val("pre"))
                        sch.wait("tensor", "ve", f2b_ln["ln_done"])
                        sch.wait("tensor", "ac", ATb_done["ac"])
                        sch.wait("scalar", "pe", ATb_done["pe"])
                        sch.wait("scalar", "ve", f2b_ln["ln_done"])
                        out_guarded[0] = True
                    sch.wait("tensor", "ve", xt_ve)
                    if bank in pt_free:
                        sch.wait("tensor", "ac", pt_free[bank])
                    c_tp = 0
                    for idx in range(4):
                        mch = grp4 * 4 + idx
                        c_tp = sch.op("tensor", lambda e, mch=mch, idx=idx, st8=st8, bank=bank: e.transpose(
                            PTf[bank][:, idx * 128:(idx + 1) * 128],
                            xt[:, mch, st8 * 128:(st8 + 1) * 128], eyef[:]),
                            inc="pe" if idx == 3 else None)
                    sch.wait("scalar", "pe", c_tp)
                    if st8 >= 2:
                        sch.wait("scalar", "os%d" % slot, ost_dma[st8 - 2])
                    last_ev = sch.op("scalar", lambda e, grp4=grp4, slot=slot, bank=bank: e.copy(
                        ost[:, slot, grp4 * 512:(grp4 + 1) * 512], PTf[bank][:, :]), inc="ac")
                    pt_free[bank] = last_ev
                sch.wait("sync", "ac", last_ev)
                ost_dma[st8] = sch.op("sync", lambda e, st8=st8, slot=slot: e.dma_start(
                    out=out_d[st8 * 128:(st8 + 1) * 128, :], in_=ost[:, slot]), inc="os%d" % slot, n=16)

            outA = [0]

            def outA_hook():
                outA[0] += 1
                if outA[0] % 60 == 0 and outA[0] // 60 <= 4:
                    emit_out_st8(outA[0] // 60 - 1, app_marks[2]["down_ve"])

            emit_ffn_app(512, att_hook=outA_hook, prev=app_marks[2],
                         ln_done=f2b_ln["ln_done"])
            for st8 in range(4, 8):
                emit_out_st8(st8, app_marks[3]["down_ve"])
            sch.wait("sync", "os0", sch.val("os0"))
            sch.wait("sync", "os1", sch.val("os1"))

            with nc.Block() as blk:
                sch.flush(blk)

    return nc


_CACHE = {}


def _host_constants():
    j = np.arange(32, dtype=np.float32)
    inv_freq = (1.0 / (10000.0 ** (j / np.float32(32.0)))).astype(np.float32)
    dh = np.arange(128) % 64
    fr = inv_freq[dh % 32]
    R = np.zeros((128, 128), np.float32)
    for h2 in range(2):
        b = h2 * 64
        for dd in range(32):
            R[b + dd + 32, b + dd] = -1.0
            R[b + dd, b + dd + 32] = 1.0
    return fr, R


def kernel(**inputs):
    inp = {k: np.asarray(v) for k, v in inputs.items()}
    h_t = np.asarray(inp["L0_attn_h_t"], np.float32).reshape(S, D)

    fr, R = _host_constants()
    eye_bf = np.eye(128, dtype=ml_dtypes.bfloat16)
    eye_f = np.eye(128, dtype=np.float32)

    def pack_bias(v, chunks):
        return np.ascontiguousarray(np.asarray(v, np.float32).reshape(chunks, 128).T)

    bias_host = np.ascontiguousarray(np.concatenate([
        pack_bias(inp["L0_attn_bc"], 4),
        pack_bias(inp["L0_attn_bcq"], 16),
        pack_bias(inp["L0_attn_bqr"], 8),
        pack_bias(inp["L0_attn_bckv"], 4),
        pack_bias(inp["L0_attn_bck"], 16),
        pack_bias(inp["L0_attn_bkr"], 8),
        pack_bias(inp["L0_attn_bv"], 16),
        pack_bias(inp["L0_attn_bo"], 16),
    ], axis=1))

    if "nc" not in _CACHE:
        _CACHE["nc"] = build_program()
    nc = _CACHE["nc"]

    in_maps = []
    for c in range(NCORES):
        t0 = c * T
        tpos = np.arange(t0, t0 + T, dtype=np.float32)
        ang = tpos[None, :] * fr[:, None]
        rope_cs = np.ascontiguousarray(
            np.concatenate([np.cos(ang), np.sin(ang)], axis=1).astype(ml_dtypes.bfloat16))
        m = {
            "xT": np.ascontiguousarray(h_t[t0:t0 + T, :].T),
            "biases": bias_host,
            "rope_cs": rope_cs,
            "rotm": R,
            "eye_bf": eye_bf,
            "eye_f32": eye_f,
            "Wc": np.asarray(inp["L0_attn_Wc"], np.float32),
            "Wcq": np.asarray(inp["L0_attn_Wcq"], np.float32),
            "Wqr": np.asarray(inp["L0_attn_Wqr"], np.float32),
            "Wckv": np.asarray(inp["L0_attn_Wckv"], np.float32),
            "Wck": np.asarray(inp["L0_attn_Wck"], np.float32),
            "Wkr": np.asarray(inp["L0_attn_Wkr"], np.float32),
            "Wv": np.asarray(inp["L0_attn_Wv"], np.float32),
            "Wo": np.asarray(inp["L0_attn_Wo"], np.float32),
            "g_pk": np.ascontiguousarray(inp["L0_ffn_gate_W_packed"][c * 2048:(c + 1) * 2048, :]).astype(np.int32),
            "g_sc": np.ascontiguousarray(inp["L0_ffn_gate_scales"][c * 2048:(c + 1) * 2048, :]).astype(np.float32),
            "u_pk": np.ascontiguousarray(inp["L0_ffn_up_W_packed"][c * 2048:(c + 1) * 2048, :]).astype(np.int32),
            "u_sc": np.ascontiguousarray(inp["L0_ffn_up_scales"][c * 2048:(c + 1) * 2048, :]).astype(np.float32),
            "d_pk": np.ascontiguousarray(inp["L0_ffn_down_W_packed"][:, c * 1024:(c + 1) * 1024]).astype(np.int32),
            "d_sc": np.ascontiguousarray(inp["L0_ffn_down_scales"][:, c * 64:(c + 1) * 64]).astype(np.float32),
        }
        in_maps.append(m)

    res = run_bass_kernel_spmd(nc, in_maps, core_ids=list(range(NCORES)))
    out = np.concatenate([np.asarray(res.results[c]["out"]) for c in range(NCORES)], axis=0)
    return out.reshape(1, S, D).astype(np.float32)


# version 12
# speedup vs baseline: 1.3384x; 1.0179x over previous
import math
import numpy as np
import ml_dtypes
import concourse.bass as bass
import concourse.mybir as mybir
from concourse.bass_utils import run_bass_kernel_spmd
from contextlib import ExitStack

F32 = mybir.dt.float32
BF16 = mybir.dt.bfloat16
I32 = mybir.dt.int32
AF = mybir.ActivationFunctionType
ALU = mybir.AluOpType
AX = mybir.AxisListType

NCORES = 8
S = 8192
T = S // NCORES
D = 2048
NBLK = 8
KCH = 16
H = 16
TQ = 256
HW = 512
SCALER = 1.0 / math.sqrt(192.0)
C_EXP = 126 << 23
EPS = 1e-5

ENGINES = ("sync", "gpsimd", "vector", "scalar", "tensor")


class Sched:

    def __init__(self, ctr, sems):
        self.q = {e: [] for e in ENGINES}
        self.ctr = ctr
        self.sems = sems

    def wait(self, eng, sname, val):
        if val > 0:
            self.q[eng].append(lambda e, s=self.sems[sname], v=val: e.wait_ge(s, v))

    def op(self, eng, fn, inc=None, n=1):
        if inc is None:
            self.q[eng].append(fn)
            return self.ctr.get("_", 0)
        self.ctr[inc] = self.ctr.get(inc, 0) + n
        cnt = self.ctr[inc]
        sem = self.sems[inc]

        def run(e, fn=fn, sem=sem, n=n):
            fn(e).then_inc(sem, n)

        self.q[eng].append(run)
        return cnt

    def val(self, name):
        return self.ctr.get(name, 0)

    def flush(self, blk):
        for ename in ENGINES:
            ops = self.q[ename]
            if not ops:
                continue

            def body(e, ops=ops):
                for f in ops:
                    f(e)

            getattr(blk, ename)(body)
        self.q = {e: [] for e in ENGINES}


def build_program():
    nc = bass.Bass("TRN2", target_bir_lowering=False)

    xT_d = nc.dram_tensor("xT", [D, T], F32, kind="ExternalInput")
    out_d = nc.dram_tensor("out", [T, D], F32, kind="ExternalOutput")
    aw = {}
    for name, shp in (("Wc", [D, 512]), ("Wcq", [512, D]), ("Wqr", [512, 1024]),
                      ("Wckv", [D, 512]), ("Wck", [512, D]), ("Wkr", [D, 1024]),
                      ("Wv", [512, D]), ("Wo", [D, D])):
        aw[name] = nc.dram_tensor(name, shp, F32, kind="ExternalInput")
    bias_d = nc.dram_tensor("biases", [128, 88], F32, kind="ExternalInput")
    rope_d = nc.dram_tensor("rope_cs", [128, 2 * T], BF16, kind="ExternalInput")
    rotm_d = nc.dram_tensor("rotm", [128, 128], F32, kind="ExternalInput")
    eyeb_d = nc.dram_tensor("eye_bf", [128, 128], BF16, kind="ExternalInput")
    eyef_d = nc.dram_tensor("eye_f32", [128, 128], F32, kind="ExternalInput")
    pk_d = {m: nc.dram_tensor(f"{m}_pk", [2048, 1024], I32, kind="ExternalInput")
            for m in ("g", "u", "d")}
    sc_d = {m: nc.dram_tensor(f"{m}_sc", [2048, 64], F32, kind="ExternalInput")
            for m in ("g", "u", "d")}
    w_in = {m: nc.dram_tensor(f"{m}_in", [16, 128, 16, 128], BF16) for m in ("g", "u", "d")}
    aw_bf = {n: nc.dram_tensor(n + "_bf", list(aw[n].shape), BF16) for n in aw}
    w_out = {m: nc.dram_tensor(f"{m}_out", [4, NBLK, 4, 128, 16, 128], BF16) for m in ("g", "u", "d")}

    ctr = {}

    with ExitStack() as top:
        def sb(name, shape, dtype):
            return top.enter_context(nc.sbuf_tensor(name, shape, dtype))

        xt = sb("xt", [128, KCH, T], F32)
        y = sb("y", [128, KCH, T], BF16)
        shI = sb("shI", [128, KCH, HW], BF16)
        biases = sb("biases_sb", [128, 88], F32)
        rotm = sb("rotm_sb", [128, 128], F32)
        eyeb = sb("eyeb_sb", [128, 128], BF16)
        ones_c = sb("ones_c", [128, 1], F32)
        ones_r = sb("ones_r", [1, 128], F32)
        rows = sb("rows", [1, 3, 512], F32)
        nm16 = sb("nm16", [128, 16], F32)
        rs16 = sb("rs16", [128, 16], F32)
        rs16b = sb("rs16b", [128, 16], F32)
        wr = {m: sb(f"ring_{m}", [128, 2, 16, 128], BF16) for m in ("g", "u", "d")}
        g1 = sb("g1", [128, 1, HW], F32)
        sq_sb = sb("sq_sb", [128, 1, HW], F32)
        lntmp = sb("lntmp", [128, HW], F32)

        P = [top.enter_context(nc.psum_tensor(f"P{i}", [128, 512], F32)) for i in range(6)]
        PT = [top.enter_context(nc.psum_tensor(f"PT{i}", [128, 1024], BF16)) for i in range(2)]
        PTf = [t.bitcast(F32) for t in PT]

        sems = {k: top.enter_context(nc.semaphore("s_" + k))
                for k in ("ld", "gd", "cc", "pe", "ve", "ac", "pre",
                          "pk0", "pk1", "st0", "st1",
                          "wg0", "wg1", "wu0", "wu1", "wd0", "wd1",
                          "wa0", "wa1", "os0", "os1")}

        bias_cols = {}
        off = 0
        for bn, w in (("bc", 4), ("bcq", 16), ("bqr", 8), ("bckv", 4),
                      ("bck", 16), ("bkr", 8), ("bv", 16), ("bo", 16)):
            bias_cols[bn] = off
            off += w

        def bias_ap(bn, chunk):
            c = bias_cols[bn] + chunk
            return biases[:, c:c + 1]

        sch = Sched(ctr, sems)

        pre_total = [0]
        cc_at = {("g", 0): 1, ("u", 0): 2}
        _n = 2
        for _q in range(1, 4):
            _n += 1; cc_at[("g", _q)] = _n
            _n += 1; cc_at[("u", _q)] = _n
        for _q in range(4):
            _n += 1; cc_at[("d", _q)] = _n

        ln_chain = [0, 0]

        def g_ln(t0, tw, ydst_off, banks, res, stats_guards=(), bc_guards=(),
                 norm_guards=()):
            st0, st1, bc0, bc1 = banks
            q_pe = {}
            sch.wait("tensor", "pre", pre_total[0])
            sch.wait("scalar", "pre", pre_total[0])
            sch.wait("vector", "pre", pre_total[0])
            for (eng, sname, cnt) in stats_guards:
                sch.wait(eng, sname, cnt)
            if ln_chain[1]:
                sch.wait("scalar", "pe", ln_chain[1])
            for ch in range(KCH):
                if ch >= 1:
                    sch.wait("scalar", "pe", q_pe[ch - 1])
                sqv = sq_sb[:, 0, 0:tw]
                c_ac = sch.op("scalar",
                              lambda e, ch=ch, sqv=sqv: e.activation(
                                  sqv, xt[:, ch, t0:t0 + tw], AF.Square),
                              inc="ac")
                sch.wait("tensor", "ac", c_ac)
                sch.op("tensor",
                       lambda e, ch=ch, st0=st0: e.matmul(st0, ones_c[:], xt[:, ch, t0:t0 + tw],
                                                          start=(ch == 0), stop=(ch == KCH - 1)))
                q_pe[ch] = sch.op("tensor",
                                  lambda e, sqv=sqv, st1=st1: e.matmul(
                                      st1, ones_c[:], sqv,
                                      start=(ch == 0), stop=(ch == KCH - 1)),
                                  inc="pe")
                if ch % 2 == 1:
                    yield
            stats_done = q_pe[KCH - 1]
            sch.wait("vector", "pe", stats_done)
            sch.op("vector", lambda e, st0=st0: e.tensor_scalar(rows[:, 0, 0:tw], st0, 1.0 / D, None, ALU.mult))
            sch.op("vector", lambda e, st1=st1: e.tensor_scalar(rows[:, 1, 0:tw], st1, 1.0 / D, None, ALU.mult))
            sch.op("vector", lambda e: e.tensor_tensor(rows[:, 2, 0:tw], rows[:, 0, 0:tw], rows[:, 0, 0:tw], ALU.mult))
            sch.op("vector", lambda e: e.tensor_tensor(rows[:, 2, 0:tw], rows[:, 1, 0:tw], rows[:, 2, 0:tw], ALU.subtract))
            c_ve = sch.op("vector", lambda e: e.tensor_scalar(rows[:, 2, 0:tw], rows[:, 2, 0:tw], EPS, None, ALU.add),
                          inc="ve")
            sch.wait("scalar", "ve", c_ve)
            c_ac = sch.op("scalar", lambda e: e.activation(rows[:, 1, 0:tw], rows[:, 2, 0:tw], AF.Sqrt),
                          inc="ac")
            sch.wait("vector", "ac", c_ac)
            c_ve = sch.op("vector", lambda e: e.reciprocal(rows[:, 2, 0:tw], rows[:, 1, 0:tw]), inc="ve")
            yield
            sch.wait("tensor", "ve", c_ve)
            for (eng, sname, cnt) in bc_guards:
                sch.wait(eng, sname, cnt)
            sch.op("tensor", lambda e, bc0=bc0: e.matmul(bc0, ones_r[:], rows[:, 0, 0:tw], start=True, stop=True))
            bc_done = sch.op("tensor", lambda e, bc1=bc1: e.matmul(bc1, ones_r[:], rows[:, 2, 0:tw], start=True, stop=True),
                             inc="pe")
            sch.wait("vector", "pe", bc_done)
            for (eng, sname, cnt) in norm_guards:
                sch.wait(eng, sname, cnt)
            c_n = 0
            for ch in range(KCH):
                sch.op("vector", lambda e, ch=ch, bc0=bc0: e.tensor_tensor(
                    lntmp[:, 0:tw], xt[:, ch, t0:t0 + tw], bc0, ALU.subtract))
                c_n = sch.op("vector", lambda e, ch=ch, bc1=bc1: e.tensor_tensor(
                    y[:, ch, ydst_off:ydst_off + tw], lntmp[:, 0:tw], bc1, ALU.mult),
                    inc="ve" if ch == KCH - 1 else None)
                if ch % 4 == 3:
                    yield
            ln_chain[0] = c_n
            ln_chain[1] = max(stats_done, bc_done)
            res["ln_done"] = c_n

        def emit_ln(*a, **k):
            res = {}
            for _ in g_ln(*a, res=res, **k):
                pass
            return res["ln_done"]

        app_marks = []

        def emit_ffn_app(t0, first=False, dq_hook=None, att_hook=None,
                         ln_guards=None, prev=None, ln_done=None):
            if ln_done is None:
                lg = ln_guards or {}
                ln_done = emit_ln(
                    t0, HW, t0,
                    banks=(P[2][0:1, :], P[3][0:1, :], P[0][:, :], P[1][:, :]),
                    stats_guards=lg.get("stats", ()),
                    bc_guards=lg.get("bc", ()),
                    norm_guards=lg.get("norm", ()))

            wload = {}
            pe_grp = {}
            gelu_at = {}
            inter_at = {}
            evac_at = {}
            down_pe_done = {}

            for nb in range(NBLK):
                for mc in range(16):
                    gidx = nb * 16 + mc
                    if dq_hook is not None:
                        dq_hook(gidx)
                    if att_hook is not None:
                        att_hook()
                    slot = gidx % 2
                    if first and nb == 0:
                        sch.wait("sync", "cc", cc_at[("u", mc // 4)])
                    for m, qm in (("g", "sync"), ("u", "sync")):
                        if gidx >= 2:
                            sch.wait(qm, "pe", pe_grp[gidx - 2])
                        elif prev is not None:
                            sch.wait(qm, "pe", prev["gu_pe"])
                        wload[(m, gidx)] = sch.op(qm, lambda e, m=m, nb=nb, mc=mc, slot=slot: e.dma_start(
                            out=wr[m][:, slot], in_=w_out[m][mc // 4, nb, mc % 4]), inc="w%s%d" % (m, slot), n=16)
                    sch.wait("tensor", "wg%d" % slot, wload[("g", gidx)])
                    sch.wait("tensor", "wu%d" % slot, wload[("u", gidx)])
                    if gidx == 0:
                        sch.wait("tensor", "ve", ln_done)
                        if prev is not None:
                            sch.wait("tensor", "ac", prev["gelu_ac"])
                            sch.wait("tensor", "ve", prev["inter_ve"])
                    if gidx >= 2:
                        sch.wait("tensor", "ac", gelu_at[gidx - 2])
                        sch.wait("tensor", "ve", inter_at[gidx - 2])
                    for k in range(KCH):
                        sch.op("tensor", lambda e, slot=slot, k=k: e.matmul(
                            P[slot][:, :], wr["g"][:, slot, k, :], y[:, k, t0:t0 + HW],
                            start=(k == 0), stop=(k == KCH - 1)))
                    grp_pe = 0
                    for k in range(KCH):
                        grp_pe = sch.op("tensor", lambda e, slot=slot, k=k: e.matmul(
                            P[2 + slot][:, :], wr["u"][:, slot, k, :], y[:, k, t0:t0 + HW],
                            start=(k == 0), stop=(k == KCH - 1)),
                            inc="pe" if k == KCH - 1 else None)
                    pe_grp[gidx] = grp_pe
                    sch.wait("scalar", "pe", grp_pe)
                    if gidx >= 1:
                        sch.wait("scalar", "ve", inter_at[gidx - 1])
                    elif prev is not None:
                        sch.wait("scalar", "ve", prev["inter_ve"])
                    gelu_at[gidx] = sch.op("scalar", lambda e, slot=slot: e.activation(
                        g1[:, 0, :], P[slot][:, :], AF.Gelu_apprx_tanh), inc="ac")
                    sch.wait("vector", "ac", gelu_at[gidx])
                    if nb >= 1 and mc == 0:
                        sch.wait("vector", "pe", down_pe_done[nb - 1])
                    elif nb == 0 and mc == 0 and prev is not None:
                        sch.wait("vector", "pe", prev["down_pe"])
                    inter_at[gidx] = sch.op("vector", lambda e, mc=mc, slot=slot: e.tensor_tensor(
                        shI[:, mc, :], g1[:, 0, :], P[2 + slot][:, :], ALU.mult),
                        inc="ve")
                for hc in range(16):
                    hgidx = nb * 16 + hc
                    slot = hgidx % 2
                    if att_hook is not None:
                        att_hook()
                    if first and nb == 0:
                        sch.wait("sync", "cc", cc_at[("d", hc // 4)])
                    if hgidx >= 2:
                        sch.wait("sync", "pe", pe_grp[("d", hgidx - 2)])
                    elif prev is not None:
                        sch.wait("sync", "pe", prev["down_pe"])
                    wload[("d", hgidx)] = sch.op("sync", lambda e, nb=nb, hc=hc, slot=slot: e.dma_start(
                        out=wr["d"][:, slot], in_=w_out["d"][hc // 4, nb, hc % 4]), inc="wd%d" % slot, n=16)
                    bank = 4 + (hc % 2)
                    sch.wait("tensor", "wd%d" % slot, wload[("d", hgidx)])
                    if hc == 0:
                        sch.wait("tensor", "ve", inter_at[nb * 16 + 15])
                    if hgidx >= 2:
                        sch.wait("tensor", "ve", evac_at[hgidx - 2])
                    grp_pe = 0
                    for k in range(KCH):
                        grp_pe = sch.op("tensor", lambda e, slot=slot, k=k, bank=bank: e.matmul(
                            P[bank][:, :], wr["d"][:, slot, k, :], shI[:, k, :],
                            start=(k == 0), stop=(k == KCH - 1)),
                            inc="pe" if k == KCH - 1 else None)
                    pe_grp[("d", hgidx)] = grp_pe
                    sch.wait("vector", "pe", grp_pe)
                    evac_at[hgidx] = sch.op("vector", lambda e, hc=hc, bank=bank, t0=t0: e.scalar_tensor_tensor(
                        xt[:, hc, t0:t0 + HW], P[bank][:, :], 0.5,
                        xt[:, hc, t0:t0 + HW], ALU.mult, ALU.add),
                        inc="ve")
                down_pe_done[nb] = pe_grp[("d", nb * 16 + 15)]

            app_marks.append(dict(
                gu_pe=pe_grp[NBLK * 16 - 1],
                down_ve=evac_at[NBLK * 16 - 1],
                gelu_ac=gelu_at[NBLK * 16 - 1],
                inter_ve=inter_at[NBLK * 16 - 1],
                down_pe=down_pe_done[NBLK - 1],
            ))

        sch.op("sync", lambda e: e.dma_start(out=xt[:], in_=xT_d.rearrange("(c p) t -> p c t", p=128)),
               inc="pre", n=16)
        sch.op("sync", lambda e: e.dma_start(out=biases[:], in_=bias_d[:]), inc="pre", n=16)
        sch.op("sync", lambda e: e.dma_start(out=rotm[:], in_=rotm_d[:]), inc="pre", n=16)
        sch.op("sync", lambda e: e.dma_start(out=eyeb[:], in_=eyeb_d[:]), inc="pre", n=16)
        PRE_ALL = sch.val("pre")
        pre_total[0] = PRE_ALL
        sch.op("vector", lambda e: e.memset(ones_c[:], 1.0))
        sch.op("vector", lambda e: e.memset(ones_r[:], 1.0), inc="ve")

        with ExitStack() as c0:
            pk_t = c0.enter_context(nc.sbuf_tensor("pk", [128, 2, 1024], I32))
            sc_t = c0.enter_context(nc.sbuf_tensor("sct", [128, 2, 64], F32))
            v0l = c0.enter_context(nc.sbuf_tensor("v0l", [128, 1024], I32))
            sl_t = c0.enter_context(nc.sbuf_tensor("sl", [128, 1024], I32))
            v8_t = c0.enter_context(nc.sbuf_tensor("v8", [128, 1024], F32))
            nat = c0.enter_context(nc.sbuf_tensor("nat", [128, 2, 2048], BF16))
            stage = c0.enter_context(nc.sbuf_tensor("stage", [128, 2, 16, 128], BF16))

            dq_done = {}
            tp_done = {}
            pt_evac = {}
            stage_dma = {}
            dq_hist = []

            def emit_dq_chunk(m, oc, gl):
                dq_hist.append(gl)
                slot = (len(dq_hist) - 1) % 2
                qeng = "gpsimd"
                if len(dq_hist) >= 3:
                    sch.wait(qeng, "ve", dq_done[dq_hist[-3]])
                pksem = "pk%d" % slot
                sch.op(qeng, lambda e, m=m, oc=oc, slot=slot: e.dma_start(
                    out=pk_t[:, slot, :],
                    in_=pk_d[m].rearrange("(c p) j -> c p j", p=128)[oc]), inc=pksem, n=16)
                c_ld = sch.op(qeng, lambda e, m=m, oc=oc, slot=slot: e.dma_start(
                    out=sc_t[:, slot, :],
                    in_=sc_d[m].rearrange("(c p) j -> c p j", p=128)[oc]), inc=pksem, n=16)
                sch.wait("vector", pksem, c_ld)
                if len(dq_hist) >= 3:
                    sch.wait("vector", "pe", tp_done[dq_hist[-3]])
                pkap = pk_t[:, slot, :]
                sc_b = sc_t.rearrange("p s (b one) -> p s b one", one=1)[:, slot].broadcast_to([128, 64, 16])
                c_dq = 0
                for par in (0, 1):
                    if par == 0:
                        sch.op("vector", lambda e, pkap=pkap: e.tensor_scalar(v0l[:], pkap, 7, 22, ALU.bitwise_and, ALU.logical_shift_left))
                        sch.op("vector", lambda e, pkap=pkap: e.tensor_scalar(sl_t[:], pkap, 8, 28, ALU.bitwise_and, ALU.logical_shift_left))
                    else:
                        sch.op("vector", lambda e, pkap=pkap: e.tensor_scalar(v0l[:], pkap, 4, None, ALU.logical_shift_right))
                        sch.op("vector", lambda e: e.tensor_scalar(v0l[:], v0l[:], 7, 22, ALU.bitwise_and, ALU.logical_shift_left))
                        sch.op("vector", lambda e, pkap=pkap: e.tensor_scalar(sl_t[:], pkap, 128, 24, ALU.bitwise_and, ALU.logical_shift_left))
                    sch.op("vector", lambda e: e.tensor_scalar(v0l[:], v0l[:], C_EXP, None, ALU.add))
                    v0f = v0l.bitcast(F32)
                    sch.op("vector", lambda e, v0f=v0f: e.tensor_scalar(v8_t[:], v0f[:], 1.0, 0.0, ALU.subtract, ALU.min))
                    sch.op("vector", lambda e, v0f=v0f: e.tensor_add(v8_t[:], v0f[:], v8_t[:]))
                    sch.op("vector", lambda e: e.tensor_tensor(v8_t.bitcast(I32)[:], v8_t.bitcast(I32)[:], sl_t[:], ALU.bitwise_xor))
                    natv = nat.rearrange("p s (b g two) -> p s b g two", two=2, g=16)[:, slot, :, :, par]
                    c_dq = sch.op("vector", lambda e, natv=natv, sc_b=sc_b: e.tensor_tensor(
                        natv, v8_t.rearrange("p (b g) -> p b g", g=16), sc_b, ALU.mult),
                        inc="ve" if par == 1 else None)
                dq_done[gl] = c_dq
                sch.wait("tensor", "ve", c_dq)
                if len(dq_hist) == 1:
                    sch.wait("tensor", "pre", PRE_ALL)
                for grp in range(2):
                    bank = grp
                    if len(dq_hist) >= 2:
                        sch.wait("tensor", "ac", pt_evac[(dq_hist[-2], grp)])
                    c_tp = 0
                    for kk in range(8):
                        k = grp * 8 + kk
                        c_tp = sch.op("tensor", lambda e, slot=slot, k=k, kk=kk, bank=bank: e.transpose(
                            PT[bank][:, kk * 128:(kk + 1) * 128],
                            nat[:, slot, k * 128:(k + 1) * 128], eyeb[:]),
                            inc="pe" if kk == 7 else None)
                    sch.wait("scalar", "pe", c_tp)
                    if len(dq_hist) >= 3:
                        sch.wait("scalar", "st%d" % slot, stage_dma[dq_hist[-3]])
                    pt_evac[(gl, grp)] = sch.op("scalar", lambda e, slot=slot, grp=grp, bank=bank: e.copy(
                        stage[:, slot, grp * 8:(grp + 1) * 8, :],
                        PT[bank].rearrange("p (g q) -> p g q", q=128)),
                        inc="ac")
                    if grp == 1:
                        tp_done[gl] = c_tp
                sch.wait("scalar", "ac", pt_evac[(gl, 1)])
                stage_dma[gl] = sch.op("scalar", lambda e, m=m, oc=oc, slot=slot: e.dma_start(
                    out=w_in[m][oc], in_=stage[:, slot]), inc="st%d" % slot, n=16)

            def emit_cc(m, q):
                sch.wait("gpsimd", "st0", sch.val("st0"))
                sch.wait("gpsimd", "st1", sch.val("st1"))
                rws = 4 * 128 * 16
                c = sch.op("gpsimd", lambda e, m=m, q=q, rws=rws: e.collective_compute(
                    "AllGather", ALU.bypass,
                    replica_groups=[list(range(NCORES))],
                    ins=[w_in[m].rearrange("a p k q -> (a p k) q")[q * rws:(q + 1) * rws, :]],
                    outs=[w_out[m][q].rearrange("n a p k q -> (n a p k) q")]),
                    inc="cc")
                assert c == cc_at[(m, q)], (m, q, c, cc_at[(m, q)])

            for m in ("g", "u"):
                for oc in range(4):
                    emit_dq_chunk(m, oc, (0 if m == "g" else 16) + oc)
                emit_cc(m, 0)

            rest = []
            for q in range(1, 4):
                rest += [("g", oc) for oc in range(4 * q, 4 * q + 4)]
                rest += [("u", oc) for oc in range(4 * q, 4 * q + 4)]
            rest += [("d", oc) for oc in range(16)]

            CPH = 3

            def dq_hook(gidx):
                for j in range(CPH * gidx, min(CPH * (gidx + 1), len(rest))):
                    m, oc = rest[j]
                    base = {"g": 0, "u": 16, "d": 32}[m]
                    emit_dq_chunk(m, oc, base + oc)
                    if oc % 4 == 3:
                        emit_cc(m, oc // 4)

            f1b_ln = {}

            def f1b_ln_gen():
                sg = [("tensor", "ac", ctr.get("ac", 0))]
                yield from g_ln(
                    512, HW, 512,
                    banks=(PTf[0][0:1, :], PTf[1][0:1, :],
                           PTf[0][:, :], PTf[1][:, :]),
                    res=f1b_ln, stats_guards=sg)

            def hook_after(n, gen):
                cnt = [0]

                def hook():
                    cnt[0] += 1
                    if cnt[0] > n:
                        try:
                            next(gen)
                        except StopIteration:
                            pass
                return hook

            f1b_gen = f1b_ln_gen()
            emit_ffn_app(0, first=True, dq_hook=dq_hook,
                         att_hook=hook_after(200, f1b_gen))
            for _ in f1b_gen:
                pass
            DQ_ST0 = sch.val("st0")
            DQ_ST1 = sch.val("st1")
        for n in aw:
            sch.op("gpsimd", lambda e, n=n: e.dma_start(out=aw_bf[n][:], in_=aw[n][:]), inc="gd", n=16)
        AWBF_ALL = sch.val("gd")

        with ExitStack() as c2:
            attscr = c2.enter_context(nc.sbuf_tensor("attscr", [128, 6656], F32))
            cqb = c2.enter_context(nc.sbuf_tensor("cqb", [128, 4, TQ], BF16))
            ckvb = c2.enter_context(nc.sbuf_tensor("ckvb", [128, 4, TQ], BF16))
            qcb = c2.enter_context(nc.sbuf_tensor("qcb", [128, 16, TQ], BF16))
            kcb = c2.enter_context(nc.sbuf_tensor("kcb", [128, 16, TQ], BF16))
            qrb = c2.enter_context(nc.sbuf_tensor("qrb", [128, 8, TQ], BF16))
            krb = c2.enter_context(nc.sbuf_tensor("krb", [128, 8, TQ], BF16))
            wra = c2.enter_context(nc.sbuf_tensor("wra", [128, 1, 16, 128], BF16))

            attF = attscr
            attB = attscr.bitcast(BF16)
            prod = attB[:, 0:3072].rearrange("p (j d) -> p j d", d=192)
            prod2 = attB[:, 0:2048].rearrange("p (d j) -> p d j", j=16)
            q_t = attB[:, 3072:6144]
            k_t = attB[:, 6144:9216]
            v_t = attB[:, 9216:11264]
            o_t = attB[:, 11264:13312]
            es256 = attF[:, 5632:5888]
            rope = attB[:, 12032:12544]
            sc256 = attF[:, 6272:6528]
            at256 = attB[:, 13056:13312]
            rtm1 = lntmp[:, 0:TQ]
            rtm2 = lntmp[:, TQ:2 * TQ]
            q192 = q_t.rearrange("p (i d) -> p i d", d=192)
            k192 = k_t.rearrange("p (i d) -> p i d", d=192)

            def PB(h):
                return PTf[1][:, h * TQ:(h + 1) * TQ]

            ptst = {(b, h): [("ve", f1b_ln["ln_done"])] for b in (0, 1) for h in (0, 1)}

            def pt_guard(eng, bank, halves):
                seen = set()
                for h in halves:
                    for mk in ptst[(bank, h)]:
                        if mk not in seen:
                            seen.add(mk)
                            sch.wait(eng, mk[0], mk[1])

            def pt_setcons(bank, halves, marks):
                for h in halves:
                    ptst[(bank, h)] = list(marks)

            proj_specs = [
                ("cq", "Wc", "bc", "y", 16, 4),
                ("ckv", "Wckv", "bckv", "y", 16, 4),
                ("qc", "Wcq", "bcq", "cq", 4, 16),
                ("qr", "Wqr", "bqr", "cq", 4, 8),
                ("kc", "Wck", "bck", "ckv", 4, 16),
                ("kr", "Wkr", "bkr", "y", 16, 8),
                ("v", "Wv", "bv", "ckv", 4, 16),
            ]

            wa_hist = []
            ast = dict(
                rope_ve=app_marks[0]["down_ve"], ot_pe=app_marks[0]["down_pe"],
                buf_pe={},
                yslot={},
                av_ve=0,
                pb_free={0: [("ve", f1b_ln["ln_done"])],
                         1: [("ve", f1b_ln["ln_done"])]},
                wa_done=False,
                last=dict(ve=0, ac=0, pe=0),
                s3_ve={},
            )

            def wa_load(fn):
                slot = 0
                if not ast["wa_done"]:
                    sch.wait("gpsimd", "gd", AWBF_ALL)
                    ast["wa_done"] = True
                if len(wa_hist) >= 1:
                    sch.wait("gpsimd", "pe", wa_hist[-1][1])
                c_w = sch.op("gpsimd", lambda e, fn=fn, slot=slot: fn(e, slot), inc="wa0", n=16)
                ent = [c_w, None]
                wa_hist.append(ent)
                return slot, c_w, ent

            def att_half(half, xt_ready_ve, y_free_pe):
                abase = half * 512
                ao_ac = 0
                if half == 0:
                    sch.wait("vector", "st0", DQ_ST0)
                    sch.wait("vector", "st1", DQ_ST1)
                for qq in (0, 1):
                    q = 2 * half + qq
                    t0q = q * TQ
                    ys = qq
                    lnoff = abase + ys * TQ
                    voff = abase + (1 - ys) * TQ
                    sch.wait("gpsimd", "ve", ast["rope_ve"])
                    sch.wait("gpsimd", "pe", ast["ot_pe"])
                    c_rl = sch.op("gpsimd", lambda e, t0q=t0q: e.dma_start(
                        out=rope.rearrange("p (c t) -> p c t", c=2),
                        in_=rope_d.rearrange("p (c t) -> p c t", c=2)[:, :, t0q:t0q + TQ]), inc="pre", n=16)
                    yield
                    sg = [("scalar", "ve", xt_ready_ve), ("tensor", "ve", xt_ready_ve)]
                    for (s, c) in ptst[(0, 0)] + ptst[(0, 1)]:
                        sg.append(("tensor", s, c))
                    bg = [("tensor", s, c) for (s, c) in
                          ast["pb_free"][0] + ast["pb_free"][1]]
                    ng = [("vector",) + mk for mk in
                          ast["yslot"].get((half, ys), [("pe", y_free_pe)])]
                    ln_res = {}
                    yield from g_ln(
                        t0q, TQ, lnoff,
                        banks=(PTf[0][0:1, 0:TQ], PTf[0][0:1, TQ:2 * TQ],
                               PB(0), PB(1)),
                        res=ln_res,
                        stats_guards=sg, bc_guards=bg, norm_guards=ng)
                    ln_done = ln_res["ln_done"]
                    pt_setcons(0, (0, 1), [("ve", ln_done)])
                    ast["pb_free"] = {0: [("ve", ln_done)], 1: [("ve", ln_done)]}
                    yield
                    dst_ready = {}
                    pb_i = [0]

                    def rhs_of(src, k, lnoff=lnoff):
                        if src == "y":
                            return y[:, k, lnoff:lnoff + TQ]
                        if src == "cq":
                            return cqb[:, k, :]
                        return ckvb[:, k, :]

                    for (pname, wname, bname, ksrc, kch, mch) in proj_specs:
                        for m in range(mch):
                            slot, c_w, ent = wa_load(
                                lambda e, slot, wname=wname, kch=kch, m=m: e.dma_start(
                                    out=wra[:, slot, 0:kch, :],
                                    in_=aw_bf[wname].rearrange("(k p) n -> p k n", p=128)[:, :, m * 128:(m + 1) * 128]))
                            h = pb_i[0] % 2
                            pb_i[0] += 1
                            bank_ap = PB(h)
                            sch.wait("tensor", "wa%d" % slot, c_w)
                            if pname == "cq" and m == 0:
                                sch.wait("tensor", "ve", ln_done)
                            for mk in ast["pb_free"][h]:
                                sch.wait("tensor", mk[0], mk[1])
                            if pname in ("qc", "qr") and m == 0:
                                sch.wait("tensor", "ve", dst_ready[("cq", 3)])
                            if pname in ("kc", "v") and m == 0:
                                sch.wait("tensor", "ve", dst_ready[("ckv", 3)])
                            grp = 0
                            for k in range(kch):
                                grp = sch.op("tensor", lambda e, slot=slot, k=k, ksrc=ksrc, bank_ap=bank_ap, kch=kch: e.matmul(
                                    bank_ap, wra[:, slot, k, :], rhs_of(ksrc, k),
                                    start=(k == 0), stop=(k == kch - 1)),
                                    inc="pe" if k == kch - 1 else None)
                            ent[1] = grp
                            sch.wait("vector", "pe", grp)
                            if pname in ("qr", "kr"):
                                qr0 = es256
                                sch.wait("vector", "pe", ast["ot_pe"])
                                c1v = sch.op("vector", lambda e, bank_ap=bank_ap, bname=bname, m=m, qr0=qr0: e.tensor_scalar(
                                    qr0, bank_ap, bias_ap(bname, m), None, ALU.add), inc="ve")
                                ast["pb_free"][h] = [("ve", c1v)]
                                sch.wait("tensor", "ve", c1v)
                                pt_guard("tensor", 0, (0,))
                                c_rot = sch.op("tensor", lambda e, qr0=qr0: e.matmul(
                                    PTf[0][:, 0:TQ], rotm[:], qr0, start=True, stop=True), inc="pe")
                                sch.wait("vector", "pe", c_rot)
                                sch.wait("vector", "pre", c_rl)
                                cosap = rope[:, 0:TQ]
                                sinap = rope[:, TQ:2 * TQ]
                                sch.op("vector", lambda e, cosap=cosap, qr0=qr0: e.tensor_tensor(rtm1[:], qr0, cosap, ALU.mult))
                                sch.op("vector", lambda e, sinap=sinap: e.tensor_tensor(rtm2[:], PTf[0][:, 0:TQ], sinap, ALU.mult))
                                dstap = (qrb if pname == "qr" else krb)[:, m, :]
                                for mk in ast["buf_pe"].get(pname, []):
                                    sch.wait("vector", mk[0], mk[1])
                                c2v = sch.op("vector", lambda e, dstap=dstap: e.tensor_add(dstap, rtm1[:], rtm2[:]), inc="ve")
                                ast["rope_ve"] = c2v
                                pt_setcons(0, (0,), [("ve", c2v)])
                                dst_ready[(pname, m)] = c2v
                            else:
                                if pname == "v":
                                    dstap = y[:, m, voff:voff + TQ]
                                    for mk in ast["yslot"].get((half, 1 - ys), [("pe", y_free_pe)]):
                                        sch.wait("vector", mk[0], mk[1])
                                else:
                                    bufs = {"cq": cqb, "ckv": ckvb, "qc": qcb, "kc": kcb}
                                    dstap = bufs[pname][:, m, :]
                                    for mk in ast["buf_pe"].get(pname, []):
                                        sch.wait("vector", mk[0], mk[1])
                                c1v = sch.op("vector", lambda e, dstap=dstap, bank_ap=bank_ap, bname=bname, m=m: e.tensor_scalar(
                                    dstap, bank_ap, bias_ap(bname, m), None, ALU.add), inc="ve")
                                ast["pb_free"][h] = [("ve", c1v)]
                                dst_ready[(pname, m)] = c1v
                            yield
                    proj_all_ve = sch.val("ve")
                    proj_all_pe = sch.val("pe")
                    ast["yslot"][(half, ys)] = [("pe", proj_all_pe)]
                    ast["buf_pe"]["cq"] = [("pe", proj_all_pe)]
                    ast["buf_pe"]["ckv"] = [("pe", proj_all_pe)]
                    yield
                    for sti in range(TQ // 128):
                        s0 = sti * 128
                        gi = [0]

                        def tgroup(srcs, dst_builder, split4=False):
                            bank = gi[0] % 2
                            gi[0] += 1
                            sch.wait("tensor", "ve", proj_all_ve)
                            pt_guard("tensor", bank, (0, 1))
                            c_tp = 0
                            for idx, (srcap,) in enumerate(srcs):
                                c_tp = sch.op("tensor", lambda e, srcap=srcap, idx=idx, bank=bank: e.transpose(
                                    PT[bank][:, idx * 128:(idx + 1) * 128], srcap, eyeb[:]),
                                    inc="pe" if idx == 7 else None)
                            sch.wait("scalar", "pe", c_tp)
                            if ast["av_ve"]:
                                sch.wait("scalar", "ve", ast["av_ve"])
                            if split4:
                                src_ap = PT[bank].rearrange("p (g two d) -> p g two d", two=2, d=64)
                            else:
                                src_ap = PT[bank].rearrange("p (g q) -> p g q", q=128)
                            c = sch.op("scalar", lambda e, b=dst_builder, s=src_ap: e.copy(b(), s), inc="ac")
                            pt_setcons(bank, (0, 1), [("ac", c)])
                            return c

                        marks = []
                        for hh in range(2):
                            marks.append(tgroup([(qcb[:, hh * 8 + j, s0:s0 + 128],) for j in range(8)],
                                                lambda hh=hh: q192[:, hh * 8:(hh + 1) * 8, 0:128]))
                            yield
                        marks.append(tgroup([(qrb[:, j, s0:s0 + 128],) for j in range(8)],
                                            lambda: q192[:, :, 128:192].rearrange("p (a two) d -> p a two d", two=2),
                                            split4=True))
                        yield
                        for hh in range(2):
                            marks.append(tgroup([(kcb[:, hh * 8 + j, s0:s0 + 128],) for j in range(8)],
                                                lambda hh=hh: k192[:, hh * 8:(hh + 1) * 8, 0:128]))
                            yield
                        marks.append(tgroup([(krb[:, j, s0:s0 + 128],) for j in range(8)],
                                            lambda: k192[:, :, 128:192].rearrange("p (a two) d -> p a two d", two=2),
                                            split4=True))
                        yield
                        for hh in range(2):
                            marks.append(tgroup([(y[:, hh * 8 + j, voff + s0:voff + s0 + 128],) for j in range(8)],
                                                lambda hh=hh: v_t.rearrange("p (j d) -> p j d", d=128)[:, hh * 8:(hh + 1) * 8, :]))
                            yield
                        tq_pe = sch.val("pe")
                        for c in marks:
                            sch.wait("vector", "ac", c)
                        sch.wait("vector", "pe", ast["ot_pe"])
                        c_sc = 0
                        for i in range(H):
                            qi = q192[:, i:i + 1, :].broadcast_to([128, H, 192])
                            sch.op("vector", lambda e, qi=qi: e.tensor_tensor(prod, qi, k192, ALU.mult))
                            c_sc = sch.op("vector", lambda e, i=i: e.tensor_reduce(
                                sc256[:, i * 16:(i + 1) * 16], prod, AX.X, ALU.add),
                                inc="ve" if i == H - 1 else None)
                            if i % 4 == 3:
                                yield
                        sch.wait("vector", "ve", c_sc)
                        sc3 = sc256.rearrange("p (i j) -> p i j", j=16)
                        c_r1 = sch.op("vector", lambda e, sc3=sc3: e.tensor_reduce(nm16[:], sc3, AX.X, ALU.max, negate=True), inc="ve")
                        sch.wait("vector", "ve", c_r1)
                        c_sm = sch.op("vector", lambda e, sc3=sc3: e.tensor_tensor(
                            sc3, sc3,
                            nm16.rearrange("p (i one) -> p i one", one=1).broadcast_to([128, 16, 16]),
                            ALU.add), inc="ve")
                        sch.wait("scalar", "ve", c_sm)
                        sch.wait("scalar", "pe", ast["ot_pe"])
                        c_exp = sch.op("scalar", lambda e: e.activation(es256[:], sc256[:], AF.Exp, scale=SCALER), inc="ac")
                        sch.wait("vector", "ac", c_exp)
                        es3 = es256.rearrange("p (i j) -> p i j", j=16)
                        c_r2 = sch.op("vector", lambda e, es3=es3: e.tensor_reduce(rs16[:], es3, AX.X, ALU.add), inc="ve")
                        sch.wait("vector", "ve", c_r2)
                        c_r3 = sch.op("vector", lambda e: e.reciprocal(rs16b[:], rs16[:]), inc="ve")
                        sch.wait("vector", "ve", c_r3)
                        at3 = at256.rearrange("p (i j) -> p i j", j=16)
                        sch.op("vector", lambda e, es3=es3, at3=at3: e.tensor_tensor(
                            at3, es3,
                            rs16b.rearrange("p (i one) -> p i one", one=1).broadcast_to([128, 16, 16]),
                            ALU.mult))
                        yield
                        vv = v_t.rearrange("p (j d) -> p d j", d=128)

                        def av_reduce(e, i):
                            with nc.allow_low_precision(reason="attn@v: fp32 accum, bf16 out"):
                                return e.tensor_reduce(o_t[:, i * 128:(i + 1) * 128], prod2, AX.X, ALU.add)

                        c_av = 0
                        for i in range(H):
                            ai = at3[:, i:i + 1, :].broadcast_to([128, 128, 16])
                            sch.op("vector", lambda e, ai=ai, vv=vv: e.tensor_tensor(prod2, ai, vv, ALU.mult))
                            c_av = sch.op("vector", lambda e, i=i: av_reduce(e, i),
                                          inc="ve" if i == H - 1 else None)
                            if i % 4 == 3 and i < H - 1:
                                yield
                        ast["av_ve"] = c_av
                        ast["yslot"][(half, 1 - ys)] = [("ve", c_av), ("pe", tq_pe)]
                        yield
                        sch.wait("tensor", "ve", c_av)
                        last_tp = 0
                        for hh in range(2):
                            bank = hh % 2
                            pt_guard("tensor", bank, (0, 1))
                            c_tp = 0
                            for idx in range(8):
                                mch = hh * 8 + idx
                                c_tp = sch.op("tensor", lambda e, mch=mch, idx=idx, bank=bank: e.transpose(
                                    PT[bank][:, idx * 128:(idx + 1) * 128],
                                    o_t[:, mch * 128:(mch + 1) * 128], eyeb[:]),
                                    inc="pe" if idx == 7 else None)
                            last_tp = c_tp
                            sch.wait("scalar", "pe", c_tp)
                            sch.wait("scalar", "ve", c_av)
                            dstb = (lambda hh=hh, s0=s0, voff=voff: y[:, hh * 8:(hh + 1) * 8, voff + s0:voff + s0 + 128])
                            src_ap = PT[bank].rearrange("p (g q) -> p g q", q=128)
                            c = sch.op("scalar", lambda e, dstb=dstb, s=src_ap: e.copy(dstb(), s), inc="ac")
                            pt_setcons(bank, (0, 1), [("ac", c)])
                            yield
                        ast["ot_pe"] = last_tp
                        ao_ac = ctr.get("ac", 0)
                    cur_pe = sch.val("pe")
                    for nmq in ("qc", "kc", "qr", "kr"):
                        ast["buf_pe"][nmq] = [("pe", cur_pe)]
                    sch.wait("tensor", "ac", ao_ac)
                    s3_last_ve = 0
                    for m in range(16):
                        slot, c_w, ent = wa_load(
                            lambda e, slot, m=m: e.dma_start(
                                out=wra[:, slot, :, :],
                                in_=aw_bf["Wo"].rearrange("(k p) n -> p k n", p=128)[:, :, m * 128:(m + 1) * 128]))
                        h = m % 2
                        bank_ap = PB(h)
                        sch.wait("tensor", "wa%d" % slot, c_w)
                        for mk in ast["pb_free"][h]:
                            sch.wait("tensor", mk[0], mk[1])
                        grp = 0
                        for k in range(KCH):
                            grp = sch.op("tensor", lambda e, slot=slot, k=k, bank_ap=bank_ap, voff=voff: e.matmul(
                                bank_ap, wra[:, slot, k, :], y[:, k, voff:voff + TQ],
                                start=(k == 0), stop=(k == KCH - 1)),
                                inc="pe" if k == KCH - 1 else None)
                        ent[1] = grp
                        sch.wait("vector", "pe", grp)
                        s3_last_ve = sch.op("vector", lambda e, m=m, bank_ap=bank_ap, t0q=t0q: e.scalar_tensor_tensor(
                            xt[:, m, t0q:t0q + TQ], bank_ap, bias_ap("bo", m),
                            xt[:, m, t0q:t0q + TQ], ALU.add, ALU.add), inc="ve")
                        ast["pb_free"][h] = [("ve", s3_last_ve)]
                        yield
                    ast["yslot"][(half, 1 - ys)] = [("pe", sch.val("pe"))]
                    ast["s3_ve"][q] = s3_last_ve
                ast["last"] = dict(ve=sch.val("ve"), ac=sch.val("ac"), pe=sch.val("pe"))

            RATE = 1.2

            def mk_hook(gen):
                state = [0.0, 0]

                def hook():
                    state[0] += RATE
                    while state[1] < int(state[0]):
                        state[1] += 1
                        try:
                            next(gen)
                        except StopIteration:
                            return
                return hook

            def drain(gen):
                for _ in gen:
                    pass

            f2a_ln = {}
            f2b_ln = {}

            def next_ln_gen(t0n, res):
                q_last = 1 if t0n == 0 else 3
                s3v = ast["s3_ve"][q_last]
                sg = [("scalar", "ve", s3v), ("tensor", "ve", s3v)]
                for (b, h) in ptst:
                    for mk in ptst[(b, h)]:
                        sg.append(("tensor", mk[0], mk[1]))
                for h in (0, 1):
                    for mk in ast["pb_free"][h]:
                        sg.append(("tensor", mk[0], mk[1]))
                ng = [("vector", "pe", ast["last"]["pe"]),
                      ("vector", "ve", ast["last"]["ve"])]
                yield from g_ln(
                    t0n, HW, t0n,
                    banks=(PTf[0][0:1, :], PTf[1][0:1, :],
                           PTf[0][:, :], PTf[1][:, :]),
                    res=res, stats_guards=sg, norm_guards=ng)
                pt_setcons(0, (0, 1), [("ve", res["ln_done"])])
                pt_setcons(1, (0, 1), [("ve", res["ln_done"])])
                ast["pb_free"] = {0: [("ve", res["ln_done"])],
                                  1: [("ve", res["ln_done"])]}

            import itertools

            genA = itertools.chain(
                att_half(0, xt_ready_ve=app_marks[0]["down_ve"],
                         y_free_pe=app_marks[0]["gu_pe"]),
                next_ln_gen(0, f2a_ln))
            emit_ffn_app(512, att_hook=mk_hook(genA), prev=app_marks[0],
                         ln_done=f1b_ln["ln_done"])
            drain(genA)
            ATa_done = dict(ast["last"])

            genB = itertools.chain(
                att_half(1, xt_ready_ve=app_marks[1]["down_ve"],
                         y_free_pe=app_marks[1]["gu_pe"]),
                next_ln_gen(512, f2b_ln))
            emit_ffn_app(0, att_hook=mk_hook(genB), prev=app_marks[1],
                         ln_done=f2a_ln["ln_done"])
            drain(genB)
            ATb_done = dict(ast["last"])

        with ExitStack() as c4:
            ost = c4.enter_context(nc.sbuf_tensor("ost", [128, 2, D], F32))
            eyef = c4.enter_context(nc.sbuf_tensor("eyef_sb", [128, 128], F32))
            sch.op("sync", lambda e: e.dma_start(out=eyef[:], in_=eyef_d[:]), inc="pre", n=16)

            ost_dma = {}
            pt_free = {}
            out_guarded = [False]

            def emit_out_st8(st8, xt_ve):
                slot = st8 % 2
                last_ev = 0
                for grp4 in range(4):
                    bank = grp4 % 2
                    if not out_guarded[0]:
                        sch.wait("tensor", "pre", sch.val("pre"))
                        sch.wait("tensor", "ve", f2b_ln["ln_done"])
                        sch.wait("tensor", "ac", ATb_done["ac"])
                        sch.wait("scalar", "pe", ATb_done["pe"])
                        sch.wait("scalar", "ve", f2b_ln["ln_done"])
                        out_guarded[0] = True
                    sch.wait("tensor", "ve", xt_ve)
                    if bank in pt_free:
                        sch.wait("tensor", "ac", pt_free[bank])
                    c_tp = 0
                    for idx in range(4):
                        mch = grp4 * 4 + idx
                        c_tp = sch.op("tensor", lambda e, mch=mch, idx=idx, st8=st8, bank=bank: e.transpose(
                            PTf[bank][:, idx * 128:(idx + 1) * 128],
                            xt[:, mch, st8 * 128:(st8 + 1) * 128], eyef[:]),
                            inc="pe" if idx == 3 else None)
                    sch.wait("scalar", "pe", c_tp)
                    if st8 >= 2:
                        sch.wait("scalar", "os%d" % slot, ost_dma[st8 - 2])
                    last_ev = sch.op("scalar", lambda e, grp4=grp4, slot=slot, bank=bank: e.copy(
                        ost[:, slot, grp4 * 512:(grp4 + 1) * 512], PTf[bank][:, :]), inc="ac")
                    pt_free[bank] = last_ev
                sch.wait("sync", "ac", last_ev)
                ost_dma[st8] = sch.op("sync", lambda e, st8=st8, slot=slot: e.dma_start(
                    out=out_d[st8 * 128:(st8 + 1) * 128, :], in_=ost[:, slot]), inc="os%d" % slot, n=16)

            outA = [0]

            def outA_hook():
                outA[0] += 1
                if outA[0] % 60 == 0 and outA[0] // 60 <= 4:
                    emit_out_st8(outA[0] // 60 - 1, app_marks[2]["down_ve"])

            emit_ffn_app(512, att_hook=outA_hook, prev=app_marks[2],
                         ln_done=f2b_ln["ln_done"])
            for st8 in range(4, 8):
                emit_out_st8(st8, app_marks[3]["down_ve"])
            sch.wait("sync", "os0", sch.val("os0"))
            sch.wait("sync", "os1", sch.val("os1"))

            with nc.Block() as blk:
                sch.flush(blk)

    return nc


_CACHE = {}


def _host_constants():
    j = np.arange(32, dtype=np.float32)
    inv_freq = (1.0 / (10000.0 ** (j / np.float32(32.0)))).astype(np.float32)
    dh = np.arange(128) % 64
    fr = inv_freq[dh % 32]
    R = np.zeros((128, 128), np.float32)
    for h2 in range(2):
        b = h2 * 64
        for dd in range(32):
            R[b + dd + 32, b + dd] = -1.0
            R[b + dd, b + dd + 32] = 1.0
    return fr, R


def kernel(**inputs):
    inp = {k: np.asarray(v) for k, v in inputs.items()}
    h_t = np.asarray(inp["L0_attn_h_t"], np.float32).reshape(S, D)

    fr, R = _host_constants()
    eye_bf = np.eye(128, dtype=ml_dtypes.bfloat16)
    eye_f = np.eye(128, dtype=np.float32)

    def pack_bias(v, chunks):
        return np.ascontiguousarray(np.asarray(v, np.float32).reshape(chunks, 128).T)

    bias_host = np.ascontiguousarray(np.concatenate([
        pack_bias(inp["L0_attn_bc"], 4),
        pack_bias(inp["L0_attn_bcq"], 16),
        pack_bias(inp["L0_attn_bqr"], 8),
        pack_bias(inp["L0_attn_bckv"], 4),
        pack_bias(inp["L0_attn_bck"], 16),
        pack_bias(inp["L0_attn_bkr"], 8),
        pack_bias(inp["L0_attn_bv"], 16),
        pack_bias(inp["L0_attn_bo"], 16),
    ], axis=1))

    if "nc" not in _CACHE:
        _CACHE["nc"] = build_program()
    nc = _CACHE["nc"]

    in_maps = []
    for c in range(NCORES):
        t0 = c * T
        tpos = np.arange(t0, t0 + T, dtype=np.float32)
        ang = tpos[None, :] * fr[:, None]
        rope_cs = np.ascontiguousarray(
            np.concatenate([np.cos(ang), np.sin(ang)], axis=1).astype(ml_dtypes.bfloat16))
        m = {
            "xT": np.ascontiguousarray(h_t[t0:t0 + T, :].T),
            "biases": bias_host,
            "rope_cs": rope_cs,
            "rotm": R,
            "eye_bf": eye_bf,
            "eye_f32": eye_f,
            "Wc": np.asarray(inp["L0_attn_Wc"], np.float32),
            "Wcq": np.asarray(inp["L0_attn_Wcq"], np.float32),
            "Wqr": np.asarray(inp["L0_attn_Wqr"], np.float32),
            "Wckv": np.asarray(inp["L0_attn_Wckv"], np.float32),
            "Wck": np.asarray(inp["L0_attn_Wck"], np.float32),
            "Wkr": np.asarray(inp["L0_attn_Wkr"], np.float32),
            "Wv": np.asarray(inp["L0_attn_Wv"], np.float32),
            "Wo": np.asarray(inp["L0_attn_Wo"], np.float32),
            "g_pk": np.ascontiguousarray(inp["L0_ffn_gate_W_packed"][c * 2048:(c + 1) * 2048, :]).astype(np.int32),
            "g_sc": np.ascontiguousarray(inp["L0_ffn_gate_scales"][c * 2048:(c + 1) * 2048, :]).astype(np.float32),
            "u_pk": np.ascontiguousarray(inp["L0_ffn_up_W_packed"][c * 2048:(c + 1) * 2048, :]).astype(np.int32),
            "u_sc": np.ascontiguousarray(inp["L0_ffn_up_scales"][c * 2048:(c + 1) * 2048, :]).astype(np.float32),
            "d_pk": np.ascontiguousarray(inp["L0_ffn_down_W_packed"][:, c * 1024:(c + 1) * 1024]).astype(np.int32),
            "d_sc": np.ascontiguousarray(inp["L0_ffn_down_scales"][:, c * 64:(c + 1) * 64]).astype(np.float32),
        }
        in_maps.append(m)

    res = run_bass_kernel_spmd(nc, in_maps, core_ids=list(range(NCORES)))
    out = np.concatenate([np.asarray(res.results[c]["out"]) for c in range(NCORES)], axis=0)
    return out.reshape(1, S, D).astype(np.float32)
